# version 01
# baseline (speedup 1.0000x reference)
import numpy as np
from contextlib import ExitStack
import concourse.bass as bass
import concourse.mybir as mybir
from concourse.bass_utils import run_bass_kernel_spmd

F32 = mybir.dt.float32
BF16 = mybir.dt.bfloat16
AF = mybir.ActivationFunctionType
ALU = mybir.AluOpType
AX = mybir.AxisListType

ENG = ("tensor", "vector", "scalar", "gpsimd", "sync")
NEG = -1.0e30


class Chan:
    def __init__(self, nc, es, name):
        self.sem = es.enter_context(nc.semaphore(name))
        self.n = 0


class B:

    def __init__(self, nc, es):
        self.nc, self.es = nc, es
        self.ops = {e: [] for e in ENG}
        self.chans = {}
        self.last = None

    def ch(self, name):
        if name not in self.chans:
            self.chans[name] = Chan(self.nc, self.es, name)
        return self.chans[name]

    def step(self, eng, fn, ch=None, inc=1, waits=(), seq=True, set_last=True):
        ws = [(c, v) for (c, v) in waits if c is not None and v > 0]
        if seq and self.last is not None:
            ws.append(self.last)
        if ch is None:
            ch = self.ch("c_" + eng)
        ch.n += inc
        val = ch.n
        ws_f = tuple((c.sem, v) for (c, v) in ws)

        def closure(E, fn=fn, ws_f=ws_f, sem=ch.sem, inc=inc):
            for (s, v) in ws_f:
                E.wait_ge(s, v)
            ins = fn(E)
            ins.then_inc(sem, inc)

        self.ops[eng].append(closure)
        if set_last:
            self.last = (ch, val)
        return (ch, val)

    def dma(self, q, out, in_, waits=(), seq=True, set_last=True, ch=None, **kw):
        if ch is None:
            ch = self.ch("d_" + q)
        return self.step(q, lambda E: E.dma_start(out=out, in_=in_, **kw),
                         ch=ch, inc=16, waits=waits, seq=seq, set_last=set_last)

    def emit(self, block):
        for eng in ENG:
            fns = self.ops[eng]
            getattr(block, eng)(lambda E, fns=fns: [fn(E) for fn in fns])


def build_nc(upto=99, dbg=()):
    nc = bass.Bass("TRN2", target_bir_lowering=False)
    T = 16
    DCH = 8

    EI = lambda n, s, dt=F32: nc.dram_tensor(n, s, dt, kind="ExternalInput").ap()
    EO = lambda n, s, dt=F32: nc.dram_tensor(n, s, dt, kind="ExternalOutput").ap()
    IN = lambda n, s, dt: nc.dram_tensor(n, s, dt, kind="Internal").ap()

    x_in = EI("x", [2048, 1024])
    wqt_in, wkt_in, wvt_in, wot_in = (EI(n, [1024, 1024]) for n in ["wqt", "wkt", "wvt", "wot"])
    phiwt_in = EI("phiwt", [1024, 2048]); phib_in = EI("phi_b", [2048])
    ew_in = EI("ew_sl", [8, 1024, 1024]); eb_in = EI("eb_sl", [8, 1024])
    l1t_in = EI("l1t", [1024, 192])
    gwt_in = EI("gwt", [1024, 8]); gb_in = EI("gate_b", [8])
    w1t_in = EI("w1t", [8, 1024, 2048]); w3t_in = EI("w3t", [8, 1024, 2048])
    w2t_in = EI("w2t", [8, 1024, 1024])
    sw1t_in = EI("sw1t", [1024, 2048]); sw3t_in = EI("sw3t", [1024, 2048])
    sw2t_in = EI("sw2t", [1024, 1024])
    b1c_in = EI("b1c", [8, 128, 16]); b3c_in = EI("b3c", [8, 128, 16])
    sb1c_in = EI("sb1c", [128, 16]); sb3c_in = EI("sb3c", [128, 16])
    b2_in = EI("b2", [8, 1024]); sb2_in = EI("sb2", [1024])

    out_d = EO("out", [2048, 1024])

    wqt_bf, wkt_bf, wvt_bf, wot_bf = (IN(n, [1024, 1024], BF16)
                                      for n in ["wqt_bf", "wkt_bf", "wvt_bf", "wot_bf"])
    phiwt_bf = IN("phiwt_bf", [1024, 2048], BF16)
    l1t_bf = IN("l1t_bf", [1024, 192], BF16)
    gwt_bf = IN("gwt_bf", [1024, 8], BF16)
    w1t_bf = IN("w1t_bf", [8, 1024, 2048], BF16)
    w3t_bf = IN("w3t_bf", [8, 1024, 2048], BF16)
    w2t_bf = IN("w2t_bf", [8, 1024, 1024], BF16)
    sw1t_bf = IN("sw1t_bf", [1024, 2048], BF16)
    sw3t_bf = IN("sw3t_bf", [1024, 2048], BF16)
    sw2t_bf = IN("sw2t_bf", [1024, 1024], BF16)

    xn_tm_d = IN("xn_tm_d", [2048, 1024], BF16)
    ctm_d = IN("ctm_d", [2048, 2048], BF16)
    xt_part = IN("xt_part", [2048, 1024], F32)
    xt_red = IN("xt_red", [256, 1024], F32)
    s_d = IN("s_d", [2048, 1024], F32)
    xtt_d = IN("xtt_d", [256, 1024], BF16)
    ytl_d = IN("ytl_d", [256, 1024], BF16)
    yt_all = IN("yt_all", [2048, 1024], BF16)
    x1_d = IN("x1_d", [2048, 1024], F32)
    xn1_tm_d = IN("xn1_tm_d", [2048, 1024], BF16)
    xn1lo_d = IN("xn1lo_d", [2048, 1024], BF16)

    dbg_outs = {}
    for name, shape, dt in [
        ("dbg_xn", [2048, 1024], BF16), ("dbg_D", [2048, 2048], BF16),
        ("dbg_C", [2048, 2048], BF16), ("dbg_xtpart", [2048, 1024], F32),
        ("dbg_xtred", [256, 1024], F32), ("dbg_s", [2048, 1024], F32),
        ("dbg_ytall", [2048, 1024], BF16), ("dbg_x1", [2048, 1024], F32),
        ("dbg_xtt", [256, 1024], BF16), ("dbg_ytl", [256, 1024], BF16),
        ("dbg_xttfm", [128, 2048], BF16), ("dbg_ewb", [128, 16384], BF16),
        ("dbg_xn1", [2048, 1024], BF16), ("dbg_s1", [2048, 64], F32),
        ("dbg_pbt", [128, 2048], F32), ("dbg_v1", [2048, 64], F32),
        ("dbg_cw", [2048, 8], F32),
    ]:
        if name in dbg:
            dbg_outs[name] = EO(name, shape, dt)

    es = ExitStack()
    b = B(nc, es)
    sbL = lambda *a: es.enter_context(nc.sbuf_tensor(*a))
    R = {"side": "right"}

    with es, nc.Block() as block:
        ident_bf = sbL("ident_bf", [128, 128], BF16)
        ident_f = sbL("ident_f", [128, 128], F32)
        maskneg = sbL("maskneg", [128, 128], F32)
        ones_f = sbL("ones_f", [1, 512], F32)
        sel8 = sbL("sel8", [8, 8 * 128], F32)
        s1_sb = sbL("s1_sb", [128, T, 64], F32)
        cw_sb = sbL("cw_sb", [128, T, 8], F32)
        cwT_sb = sbL("cwT_sb", [8, 2048], F32)
        b1cols = sbL("b1cols", [128, 9, 16], F32)
        b3cols = sbL("b3cols", [128, 9, 16], F32)

        def _consts(g):
            g.memset(ident_bf[:], 1.0)
            g.affine_select(ident_bf[:], ident_bf[:], [[-1, 128]], ALU.is_equal,
                            0.0, base=0, channel_multiplier=1)
            g.memset(ident_f[:], 1.0)
            g.affine_select(ident_f[:], ident_f[:], [[-1, 128]], ALU.is_equal,
                            0.0, base=0, channel_multiplier=1)
            g.memset(maskneg[:], 0.0)
            g.affine_select(maskneg[:], maskneg[:], [[-1, 128]], ALU.is_ge,
                            NEG, base=0, channel_multiplier=1)
            g.memset(ones_f[:], 1.0)
            g.memset(sel8[:], 1.0)
            ins = None
            for e in range(8):
                ins = g.affine_select(sel8[:, e * 128:(e + 1) * 128],
                                      sel8[:, e * 128:(e + 1) * 128], [[0, 128]],
                                      ALU.is_equal, 0.0, base=-e, channel_multiplier=1)
            return ins
        b.step("gpsimd", _consts)

        wes = ExitStack()
        ps_warm = wes.enter_context(nc.psum_tensor("ps_warm", [128, 128], F32))
        with wes:
            b.step("tensor", lambda te: te.matmul(
                ps_warm[:], ident_bf[:], ident_bf[:], start=True, stop=True))

        for e in range(8):
            b.dma("scalar", b1cols[:, e, :], b1c_in[e], seq=False, set_last=False)
            b.dma("scalar", b3cols[:, e, :], b3c_in[e], seq=False, set_last=False)
        b.dma("scalar", b1cols[:, 8, :], sb1c_in, seq=False, set_last=False)
        ev_bias = b.dma("scalar", b3cols[:, 8, :], sb3c_in, seq=False, set_last=False)

        cast_ev = {}

        def cast(name, dst, src, grp):
            ch = b.ch(grp)
            b.dma("gpsimd", dst, src, ch=ch, seq=False, set_last=False)
            cast_ev[name] = (ch, None)

        cast("phiwt", phiwt_bf, phiwt_in, "dc0")
        cast("wqt", wqt_bf, wqt_in, "dc0"); cast("wkt", wkt_bf, wkt_in, "dc0")
        cast("wvt", wvt_bf, wvt_in, "dc0"); cast("wot", wot_bf, wot_in, "dc0")
        for e in range(8):
            cast(f"w1_{e}", w1t_bf[e], w1t_in[e], "dc2")
            cast(f"w3_{e}", w3t_bf[e], w3t_in[e], "dc2")
            cast(f"w2_{e}", w2t_bf[e], w2t_in[e], "dc2")
        cast("sw1", sw1t_bf, sw1t_in, "dc1"); cast("sw3", sw3t_bf, sw3t_in, "dc1")
        cast("sw2", sw2t_bf, sw2t_in, "dc1")
        cast_ev = {k: (ch, ch.n) for k, (ch, _) in cast_ev.items()}

        def load_wT(buf, src_bf, ev, cols, q="sync"):
            return b.dma(q, buf[:, :, 0:cols],
                         src_bf.rearrange("(c p) f -> p c f", p=128),
                         waits=(ev,))

        _lstacks = []

        def _close_lstacks():
            for st in reversed(_lstacks):
                st.close()

        es_l0b = ExitStack(); _lstacks.append(es_l0b)
        xn_fm = es_l0b.enter_context(nc.sbuf_tensor("xn_fm", [128, DCH, 2048], BF16))
        es_l0a = ExitStack(); _lstacks.append(es_l0a)
        xn_tm = es_l0a.enter_context(nc.sbuf_tensor("xn_tm", [128, T, 1024], BF16))

        pes = ExitStack()
        xall = pes.enter_context(nc.sbuf_tensor("xall", [128, T, 1024], F32, **R))
        p0sc = pes.enter_context(nc.sbuf_tensor("p0sc", [128, 1024], F32, **R))
        p0st = pes.enter_context(nc.sbuf_tensor("p0st", [128, T], F32, **R))
        p0st2 = pes.enter_context(nc.sbuf_tensor("p0st2", [128, T], F32, **R))
        with pes:
            b.dma("sync", xall[:], x_in.rearrange("(t p) d -> p t d", p=128))
            b.step("vector", lambda v: v.tensor_reduce(p0st[:], xall[:], AX.X, ALU.add))
            b.step("vector", lambda v: v.tensor_scalar(p0st2[:], p0st[:], 1.0 / 1024, None, ALU.mult))
            b.step("vector", lambda v: v.scalar_tensor_tensor(
                xall[:], p0st2[:].unsqueeze(2).broadcast_to([128, T, 1024]), -1.0,
                xall[:], ALU.mult, ALU.add))
            for t in range(T):
                b.step("scalar", lambda s, t=t: s.activation(
                    p0sc[:], xall[:, t, :], AF.Square, accum_out=p0st[:, t:t + 1]))
            b.step("vector", lambda v: v.tensor_scalar(
                p0st[:], p0st[:], 1.0 / 1024, 1e-5, ALU.mult, ALU.add))
            b.step("scalar", lambda s: s.activation(p0st2[:], p0st[:], AF.Sqrt))
            b.step("vector", lambda v: v.reciprocal(p0st[:], p0st2[:]))
            b.step("vector", lambda v: v.tensor_tensor(
                xn_tm[:], xall[:], p0st[:].unsqueeze(2).broadcast_to([128, T, 1024]),
                ALU.mult))
            b.dma("sync", xn_tm_d.rearrange("(t p) d -> p t d", p=128), xn_tm[:])
            for c in range(DCH):
                b.dma("sync", xn_fm[:, c, :], xn_tm_d[:, 128 * c:128 * (c + 1)],
                      transpose=True)
            if "dbg_xn" in dbg_outs:
                b.dma("sync", dbg_outs["dbg_xn"], xn_tm_d)
        if upto < 1:
            _close_lstacks(); b.emit(block); return nc

        D_all = es_l0a.enter_context(nc.sbuf_tensor("D_all", [128, T, 2048], BF16))
        pes = ExitStack()
        phiwT = pes.enter_context(nc.sbuf_tensor("phiwT", [128, DCH, 2048], BF16, **R))
        phib_r = pes.enter_context(nc.sbuf_tensor("phib_r", [1, 2048], F32, **R))
        E_sc = pes.enter_context(nc.sbuf_tensor("E_sc", [128, 2048], F32, **R))
        C_sc = pes.enter_context(nc.sbuf_tensor("C_sc", [128, 2048], BF16, **R))
        sd_sc = pes.enter_context(nc.sbuf_tensor("sd_sc", [128, 64], F32, **R))
        sc_sc = pes.enter_context(nc.sbuf_tensor("sc_sc", [128, 32], F32, **R))
        ps1 = pes.enter_context(nc.psum_tensor("ps1", [128, 512], F32))
        with pes:
            load_wT(phiwT, phiwt_bf, cast_ev["phiwt"], 2048)
            b.dma("sync", phib_r[:], phib_in.unsqueeze(0))
            for t in range(T):
                for ebk in range(4):
                    def mm_logits(te, t=t, ebk=ebk):
                        ins = te.matmul(ps1[:], ones_f[0:1, 0:128],
                                        phib_r[0:1, 512 * ebk:512 * (ebk + 1)],
                                        start=True, stop=False)
                        for c in range(DCH):
                            ins = te.matmul(ps1[:], xn_fm[:, c, 128 * t:128 * (t + 1)],
                                            phiwT[:, c, 512 * ebk:512 * (ebk + 1)],
                                            start=False, stop=(c == DCH - 1))
                        return ins
                    b.step("tensor", mm_logits)
                    b.step("scalar", lambda s, ebk=ebk: s.activation(
                        E_sc[:, 512 * ebk:512 * (ebk + 1)], ps1[:], AF.Exp))
                e3 = E_sc[:].rearrange("p (n q) -> p n q", q=32)
                b.step("vector", lambda v, e3=e3: v.tensor_reduce(sd_sc[:], e3, AX.X, ALU.add))
                b.step("vector", lambda v, e3=e3: v.tensor_reduce(
                    sc_sc[:], e3.transpose([0, 2, 1]), AX.X, ALU.add))
                b.step("vector", lambda v: v.reciprocal(sd_sc[:], sd_sc[:]))
                b.step("vector", lambda v: v.reciprocal(sc_sc[:], sc_sc[:]))
                b.step("vector", lambda v, t=t, e3=e3: v.tensor_tensor(
                    D_all[:, t, :].rearrange("p (n q) -> p n q", q=32), e3,
                    sd_sc[:].unsqueeze(2).broadcast_to([128, 64, 32]), ALU.mult))
                b.step("vector", lambda v, e3=e3: v.tensor_tensor(
                    C_sc[:].rearrange("p (n q) -> p n q", q=32), e3,
                    sc_sc[:].unsqueeze(1).broadcast_to([128, 64, 32]), ALU.mult))
                b.dma("sync", ctm_d[128 * t:128 * (t + 1), :], C_sc[:])
            if "dbg_D" in dbg_outs:
                b.dma("sync", dbg_outs["dbg_D"].rearrange("(t p) e -> p t e", p=128), D_all[:])
            if "dbg_C" in dbg_outs:
                b.dma("sync", dbg_outs["dbg_C"], ctm_d)
        if upto < 2:
            _close_lstacks(); b.emit(block); return nc

        pes = ExitStack()
        xtst = pes.enter_context(nc.sbuf_tensor("xtst", [128, 1024], F32, **R))
        ps2 = pes.enter_context(nc.psum_tensor("ps2", [128, 512], F32))
        with pes:
            for npc in range(T):
                for dh in range(2):
                    def mm_xt(te, npc=npc, dh=dh):
                        for t in range(T):
                            ins = te.matmul(ps2[:], D_all[:, t, 128 * npc:128 * (npc + 1)],
                                            xn_tm[:, t, 512 * dh:512 * (dh + 1)],
                                            start=(t == 0), stop=(t == T - 1))
                        return ins
                    b.step("tensor", mm_xt)
                    b.step("scalar", lambda s, dh=dh: s.activation(
                        xtst[:, 512 * dh:512 * (dh + 1)], ps2[:], AF.Copy))
                b.dma("sync", xt_part[128 * npc:128 * (npc + 1), :], xtst[:])
            b.step("gpsimd", lambda g: g.collective_compute(
                "ReduceScatter", ALU.add, replica_groups=[list(range(8))],
                ins=[xt_part.opt()], outs=[xt_red.opt()]), ch=b.ch("cc1"))
            if "dbg_xtpart" in dbg_outs:
                b.dma("sync", dbg_outs["dbg_xtpart"], xt_part)
            if "dbg_xtred" in dbg_outs:
                b.dma("sync", dbg_outs["dbg_xtred"], xt_red)
        es_l0a.close()
        if upto < 3:
            _close_lstacks(); b.emit(block); return nc

        pes = ExitStack()
        woT = pes.enter_context(nc.sbuf_tensor("woT", [128, DCH, 1024], BF16, **R))
        Qfm = pes.enter_context(nc.sbuf_tensor("Qfm", [128, DCH, 2048], BF16, **R))
        Kfm = pes.enter_context(nc.sbuf_tensor("Kfm", [128, DCH, 2048], BF16, **R))
        Vtm = pes.enter_context(nc.sbuf_tensor("Vtm", [128, T, 1024], BF16, **R))
        S_sc = pes.enter_context(nc.sbuf_tensor("S_sc", [128, 2048], F32, **R))
        Pb_sc = pes.enter_context(nc.sbuf_tensor("Pb_sc", [128, 2048], BF16, **R))
        PbT = pes.enter_context(nc.sbuf_tensor("PbT", [128, T, 128], BF16, **R))
        avtm = pes.enter_context(nc.sbuf_tensor("avtm", [128, 1024], BF16, **R))
        avfm = pes.enter_context(nc.sbuf_tensor("avfm", [128, DCH, 128], BF16, **R))
        stile = pes.enter_context(nc.sbuf_tensor("stile", [128, 1024], F32, **R))
        m3 = pes.enter_context(nc.sbuf_tensor("m3", [128, 1], F32, **R))
        l3 = pes.enter_context(nc.sbuf_tensor("l3", [128, 1], F32, **R))
        rl3 = pes.enter_context(nc.sbuf_tensor("rl3", [128, 1], F32, **R))
        ps3a = pes.enter_context(nc.psum_tensor("ps3a", [128, 512], F32))
        ps3t = pes.enter_context(nc.psum_tensor("ps3t", [128, 128], BF16))
        ps3av = pes.enter_context(nc.psum_tensor("ps3av", [128, 512], F32))
        ps3s = pes.enter_context(nc.psum_tensor("ps3s", [128, 512], F32))
        with pes:
            load_wT(woT, wot_bf, cast_ev["wot"], 1024)
            qes = ExitStack()
            wqk = qes.enter_context(nc.sbuf_tensor("wqk", [128, DCH, 1024], BF16, **R))
            with qes:
                for (dst, wsrc, wev) in ((Qfm, wqt_bf, "wqt"), (Kfm, wkt_bf, "wkt")):
                    b.dma("sync", wqk[:],
                          wsrc.rearrange("(c p) f -> p c f", p=128),
                          waits=(cast_ev[wev],))
                    for fc in range(DCH):
                        for tb in range(4):
                            def mm_qk(te, fc=fc, tb=tb):
                                for c in range(DCH):
                                    ins = te.matmul(
                                        ps3a[:], wqk[:, c, 128 * fc:128 * (fc + 1)],
                                        xn_fm[:, c, 512 * tb:512 * (tb + 1)],
                                        start=(c == 0), stop=(c == DCH - 1))
                                return ins
                            b.step("tensor", mm_qk)
                            b.step("scalar", lambda s, dst=dst, fc=fc, tb=tb: s.activation(
                                dst[:, fc, 512 * tb:512 * (tb + 1)], ps3a[:], AF.Copy))
            ves = ExitStack()
            wvT = ves.enter_context(nc.sbuf_tensor("wvT", [128, DCH, 1024], BF16, **R))
            with ves:
                load_wT(wvT, wvt_bf, cast_ev["wvt"], 1024)
                for t in range(T):
                    for fh in range(2):
                        def mm_v(te, t=t, fh=fh):
                            for c in range(DCH):
                                ins = te.matmul(ps3a[:], xn_fm[:, c, 128 * t:128 * (t + 1)],
                                                wvT[:, c, 512 * fh:512 * (fh + 1)],
                                                start=(c == 0), stop=(c == DCH - 1))
                            return ins
                        b.step("tensor", mm_v)
                        b.step("scalar", lambda s, t=t, fh=fh: s.activation(
                            Vtm[:, t, 512 * fh:512 * (fh + 1)], ps3a[:], AF.Copy))
            es_l0b.close()
            for qi in range(T):
                klen = 128 * (qi + 1)
                nkb = (klen + 511) // 512
                for kb in range(nkb):
                    kw = min(512, klen - 512 * kb)
                    def mm_s(te, qi=qi, kb=kb, kw=kw):
                        for c in range(DCH):
                            ins = te.matmul(ps3a[:, 0:kw], Qfm[:, c, 128 * qi:128 * (qi + 1)],
                                            Kfm[:, c, 512 * kb:512 * kb + kw],
                                            start=(c == 0), stop=(c == DCH - 1))
                        return ins
                    b.step("tensor", mm_s)
                    b.step("scalar", lambda s, kb=kb, kw=kw: s.activation(
                        S_sc[:, 512 * kb:512 * kb + kw], ps3a[:, 0:kw], AF.Copy))
                b.step("vector", lambda v, klen=klen: v.tensor_tensor(
                    S_sc[:, klen - 128:klen], S_sc[:, klen - 128:klen], maskneg[:],
                    ALU.add))
                b.step("vector", lambda v, klen=klen: v.tensor_reduce(
                    m3[:], S_sc[:, 0:klen], AX.X, ALU.max, negate=True))
                b.step("vector", lambda v: v.tensor_scalar(
                    m3[:], m3[:], 1.0 / 32.0, None, ALU.mult))
                b.step("scalar", lambda s, klen=klen: s.activation(
                    Pb_sc[:, 0:klen], S_sc[:, 0:klen], AF.Exp, bias=m3[:],
                    scale=1.0 / 32.0, accum_out=l3[:]))
                b.step("vector", lambda v: v.reciprocal(rl3[:], l3[:]))
                for kt in range(qi + 1):
                    b.step("tensor", lambda te, kt=kt: te.transpose(
                        ps3t[:], Pb_sc[:, 128 * kt:128 * (kt + 1)], ident_bf[:]))
                    b.step("vector", lambda v, kt=kt: v.tensor_copy(PbT[:, kt, :], ps3t[:]))
                for dh in range(2):
                    def mm_av(te, qi=qi, dh=dh):
                        for kt in range(qi + 1):
                            ins = te.matmul(ps3av[:], PbT[:, kt, :],
                                            Vtm[:, kt, 512 * dh:512 * (dh + 1)],
                                            start=(kt == 0), stop=(kt == qi))
                        return ins
                    b.step("tensor", mm_av)
                    b.step("scalar", lambda s, dh=dh: s.activation(
                        avtm[:, 512 * dh:512 * (dh + 1)], ps3av[:], AF.Copy,
                        scale=rl3[:]))
                for c in range(DCH):
                    b.step("tensor", lambda te, c=c: te.transpose(
                        ps3t[:], avtm[:, 128 * c:128 * (c + 1)], ident_bf[:]))
                    b.step("vector", lambda v, c=c: v.tensor_copy(avfm[:, c, :], ps3t[:]))
                for fh in range(2):
                    def mm_so(te, fh=fh):
                        for c in range(DCH):
                            ins = te.matmul(ps3s[:], avfm[:, c, :],
                                            woT[:, c, 512 * fh:512 * (fh + 1)],
                                            start=(c == 0), stop=(c == DCH - 1))
                        return ins
                    b.step("tensor", mm_so)
                    b.step("scalar", lambda s, fh=fh: s.activation(
                        stile[:, 512 * fh:512 * (fh + 1)], ps3s[:], AF.Copy))
                b.dma("sync", s_d[128 * qi:128 * (qi + 1), :], stile[:])
            if "dbg_s" in dbg_outs:
                b.dma("sync", dbg_outs["dbg_s"], s_d)
        if upto < 4:
            _close_lstacks(); b.emit(block); return nc

        pes = ExitStack()
        xts = pes.enter_context(nc.sbuf_tensor("xts", [128, 2, 1024], F32, **R))
        p4sc = pes.enter_context(nc.sbuf_tensor("p4sc", [128, 1024], F32, **R))
        p4a = pes.enter_context(nc.sbuf_tensor("p4a", [128, 2], F32, **R))
        p4b = pes.enter_context(nc.sbuf_tensor("p4b", [128, 2], F32, **R))
        xnt = pes.enter_context(nc.sbuf_tensor("xnt", [128, 2, 1024], BF16, **R))
        xttfm = pes.enter_context(nc.sbuf_tensor("xttfm", [128, DCH, 256], BF16, **R))
        ewb = pes.enter_context(nc.sbuf_tensor("ewb", [128, DCH, 2048], BF16, **R))
        ebr = pes.enter_context(nc.sbuf_tensor("ebr", [1, 1024], F32, **R))
        ytst = pes.enter_context(nc.sbuf_tensor("ytst", [32, 512], BF16, **R))
        ps4 = pes.enter_context(nc.psum_tensor("ps4", [32, 512], F32))
        with pes:
            ev_ew = [None] * 8
            ev_mm4 = [None] * 8
            p4_start = b.last

            def load_ew(n):
                w = (ev_mm4[n - 2],) if n >= 2 else (p4_start,)
                ev_ew[n] = b.dma(
                    "gpsimd", ewb[:, :, 1024 * (n % 2):1024 * (n % 2) + 1024],
                    ew_in[n].rearrange("(c p) h -> p c h", p=128),
                    ch=b.ch(f"dew{n % 2}"), waits=w, seq=False, set_last=False)
            load_ew(0); load_ew(1)
            b.dma("sync", xts[:], xt_red.rearrange("(t p) d -> p t d", p=128),
                  waits=((b.ch("cc1"), 1),))
            b.step("vector", lambda v: v.tensor_scalar(
                xts[:], xts[:], 65000.0, -33000.0, ALU.min, ALU.max))
            b.step("vector", lambda v: v.tensor_reduce(p4a[:], xts[:], AX.X, ALU.add))
            b.step("vector", lambda v: v.tensor_scalar(p4b[:], p4a[:], 1.0 / 1024, None, ALU.mult))
            b.step("vector", lambda v: v.scalar_tensor_tensor(
                xts[:], p4b[:].unsqueeze(2).broadcast_to([128, 2, 1024]), -1.0,
                xts[:], ALU.mult, ALU.add))
            for t in range(2):
                b.step("scalar", lambda s, t=t: s.activation(
                    p4sc[:], xts[:, t, :], AF.Square, accum_out=p4a[:, t:t + 1]))
            b.step("vector", lambda v: v.tensor_scalar(
                p4a[:], p4a[:], 1.0 / 1024, 1e-5, ALU.mult, ALU.add))
            b.step("scalar", lambda s: s.activation(p4b[:], p4a[:], AF.Sqrt))
            b.step("vector", lambda v: v.reciprocal(p4a[:], p4b[:]))
            b.step("vector", lambda v: v.tensor_tensor(
                xts[:], xts[:], p4a[:].unsqueeze(2).broadcast_to([128, 2, 1024]),
                ALU.mult))
            b.step("scalar", lambda s: s.activation(xnt[:], xts[:], AF.Tanh))
            b.dma("sync", xtt_d.rearrange("(t p) d -> p t d", p=128), xnt[:])
            for c in range(DCH):
                b.dma("sync", xttfm[:, c, :], xtt_d[:, 128 * c:128 * (c + 1)],
                      transpose=True)
            for n in range(8):
                b.dma("sync", ebr[:], eb_in[n].unsqueeze(0))
                for hh in range(2):
                    def mm_yt(te, n=n, hh=hh):
                        ins = te.matmul(ps4[:], ones_f[0:1, 0:32],
                                        ebr[0:1, 512 * hh:512 * (hh + 1)],
                                        start=True, stop=False)
                        for c in range(DCH):
                            ins = te.matmul(
                                ps4[:], xttfm[:, c, 32 * n:32 * (n + 1)],
                                ewb[:, c, 1024 * (n % 2) + 512 * hh:1024 * (n % 2) + 512 * (hh + 1)],
                                start=False, stop=(c == DCH - 1))
                        return ins
                    ev_mm4[n] = b.step("tensor", mm_yt, waits=(ev_ew[n],))
                    if n == 1 and hh == 0 and "dbg_ewb" in dbg_outs:
                        b.dma("sync", dbg_outs["dbg_ewb"], ewb[:].rearrange("p c n -> p (c n)"))
                    b.step("scalar", lambda s: s.activation(ytst[:], ps4[:], AF.Copy))
                    b.dma("sync", ytl_d[32 * n:32 * (n + 1), 512 * hh:512 * (hh + 1)], ytst[:])
                if n + 2 < 8:
                    load_ew(n + 2)
            b.step("gpsimd", lambda g: g.collective_compute(
                "AllGather", ALU.bypass, replica_groups=[list(range(8))],
                ins=[ytl_d.opt()], outs=[yt_all.opt()]), ch=b.ch("cc2"))
            if "dbg_xttfm" in dbg_outs:
                b.dma("sync", dbg_outs["dbg_xttfm"], xttfm[:].rearrange("p c n -> p (c n)"))
            if "dbg_xtt" in dbg_outs:
                b.dma("sync", dbg_outs["dbg_xtt"], xtt_d)
            if "dbg_ytl" in dbg_outs:
                b.dma("sync", dbg_outs["dbg_ytl"], ytl_d)
            if "dbg_ytall" in dbg_outs:
                b.dma("sync", dbg_outs["dbg_ytall"], yt_all)
        if upto < 5:
            _close_lstacks(); b.emit(block); return nc

        pes = ExitStack()
        Cfm = pes.enter_context(nc.sbuf_tensor("Cfm", [128, T, 2048], BF16, **R))
        ytsb = pes.enter_context(nc.sbuf_tensor("ytsb", [128, T, 1024], BF16, **R))
        gt5 = pes.enter_context(nc.sbuf_tensor("gt5", [128, 1024], F32, **R))
        xt5 = pes.enter_context(nc.sbuf_tensor("xt5", [128, 1024], F32, **R))
        st5 = pes.enter_context(nc.sbuf_tensor("st5", [128, 1024], F32, **R))
        ps5 = pes.enter_context(nc.psum_tensor("ps5", [128, 512], F32))
        with pes:
            b.dma("sync", ytsb[:], yt_all.rearrange("(c p) h -> p c h", p=128),
                  waits=((b.ch("cc2"), 1),))
            for npc in range(T):
                b.dma("sync", Cfm[:, npc, :], ctm_d[:, 128 * npc:128 * (npc + 1)],
                      transpose=True)
            for t in range(T):
                for hh in range(2):
                    def mm_y(te, t=t, hh=hh):
                        for npc in range(T):
                            ins = te.matmul(ps5[:], Cfm[:, npc, 128 * t:128 * (t + 1)],
                                            ytsb[:, npc, 512 * hh:512 * (hh + 1)],
                                            start=(npc == 0), stop=(npc == T - 1))
                        return ins
                    b.step("tensor", mm_y)
                    b.step("scalar", lambda s, hh=hh: s.activation(
                        gt5[:, 512 * hh:512 * (hh + 1)], ps5[:], AF.Silu))
                b.dma("sync", xt5[:], x_in[128 * t:128 * (t + 1), :])
                b.dma("sync", st5[:], s_d[128 * t:128 * (t + 1), :])
                b.step("vector", lambda v: v.tensor_tensor(gt5[:], gt5[:], st5[:], ALU.mult))
                b.step("vector", lambda v: v.tensor_tensor(xt5[:], xt5[:], gt5[:], ALU.add))
                b.dma("sync", x1_d[128 * t:128 * (t + 1), :], xt5[:])
            if "dbg_x1" in dbg_outs:
                b.dma("sync", dbg_outs["dbg_x1"], x1_d)
        if upto < 6:
            _close_lstacks(); b.emit(block); return nc

        es_l1 = ExitStack(); _lstacks.append(es_l1)
        xn1_fm = es_l1.enter_context(nc.sbuf_tensor("xn1_fm", [128, DCH, 2048], BF16))
        pes = ExitStack()
        x1all = pes.enter_context(nc.sbuf_tensor("x1all", [128, T, 1024], F32, **R))
        p6sc = pes.enter_context(nc.sbuf_tensor("p6sc", [128, 1024], F32, **R))
        p6a = pes.enter_context(nc.sbuf_tensor("p6a", [128, T], F32, **R))
        p6b = pes.enter_context(nc.sbuf_tensor("p6b", [128, T], F32, **R))
        xn1tm = pes.enter_context(nc.sbuf_tensor("xn1tm", [128, T, 1024], BF16, **R))
        xn1lo = pes.enter_context(nc.sbuf_tensor("xn1lo", [128, T, 1024], BF16, **R))
        with pes:
            b.dma("sync", x1all[:], x1_d.rearrange("(t p) d -> p t d", p=128))
            b.step("vector", lambda v: v.tensor_reduce(p6a[:], x1all[:], AX.X, ALU.add))
            b.step("vector", lambda v: v.tensor_scalar(p6b[:], p6a[:], 1.0 / 1024, None, ALU.mult))
            b.step("vector", lambda v: v.scalar_tensor_tensor(
                x1all[:], p6b[:].unsqueeze(2).broadcast_to([128, T, 1024]), -1.0,
                x1all[:], ALU.mult, ALU.add))
            for t in range(T):
                b.step("scalar", lambda s, t=t: s.activation(
                    p6sc[:], x1all[:, t, :], AF.Square, accum_out=p6a[:, t:t + 1]))
            b.step("vector", lambda v: v.tensor_scalar(
                p6a[:], p6a[:], 1.0 / 1024, 1e-5, ALU.mult, ALU.add))
            b.step("scalar", lambda s: s.activation(p6b[:], p6a[:], AF.Sqrt))
            b.step("vector", lambda v: v.reciprocal(p6a[:], p6b[:]))
            b.step("vector", lambda v: v.tensor_tensor(
                x1all[:], x1all[:], p6a[:].unsqueeze(2).broadcast_to([128, T, 1024]),
                ALU.mult))
            b.step("vector", lambda v: v.tensor_copy(xn1tm[:], x1all[:]))
            b.step("vector", lambda v: v.tensor_tensor(
                xn1lo[:], x1all[:], xn1tm[:], ALU.subtract))
            b.dma("sync", xn1_tm_d.rearrange("(t p) d -> p t d", p=128), xn1tm[:])
            b.dma("sync", xn1lo_d.rearrange("(t p) d -> p t d", p=128), xn1lo[:])
            for c in range(DCH):
                b.dma("sync", xn1_fm[:, c, :], xn1_tm_d[:, 128 * c:128 * (c + 1)],
                      transpose=True)
            if "dbg_xn1" in dbg_outs:
                b.dma("sync", dbg_outs["dbg_xn1"], xn1_tm_d)
        if upto < 7:
            _close_lstacks(); b.emit(block); return nc

        es_ff = ExitStack(); _lstacks.append(es_ff)
        xn1_ff = es_ff.enter_context(nc.sbuf_tensor("xn1_ff", [128, DCH, 2048], F32, **R))
        pes = ExitStack()
        wt1 = pes.enter_context(nc.sbuf_tensor("wt1", [128, DCH, 192], F32, **R))
        qfm1 = pes.enter_context(nc.sbuf_tensor("qfm1", [64, 2048], F32, **R))
        kfm1 = pes.enter_context(nc.sbuf_tensor("kfm1", [64, 2048], F32, **R))
        vtm1 = pes.enter_context(nc.sbuf_tensor("vtm1", [128, T, 64], F32, **R))
        st1 = pes.enter_context(nc.sbuf_tensor("st1", [128, 2048], F32, **R))
        st1e = pes.enter_context(nc.sbuf_tensor("st1e", [128, 2048], F32, **R))
        pbt1 = pes.enter_context(nc.sbuf_tensor("pbt1", [128, 2048], F32, **R))
        m7 = pes.enter_context(nc.sbuf_tensor("m7", [128, 1], F32, **R))
        l7 = pes.enter_context(nc.sbuf_tensor("l7", [128, 1], F32, **R))
        ps7 = pes.enter_context(nc.psum_tensor("ps7", [128, 512], F32))
        ps7q = pes.enter_context(nc.psum_tensor("ps7q", [64, 512], F32))
        ps7av = pes.enter_context(nc.psum_tensor("ps7av", [128, 1024], F32))
        with pes:
            les = ExitStack()
            lo_fm = les.enter_context(nc.sbuf_tensor("lo_fm", [128, DCH, 2048], BF16, **R))
            with les:
                for c in range(DCH):
                    b.dma("sync", lo_fm[:, c, :], xn1lo_d[:, 128 * c:128 * (c + 1)],
                          transpose=True)
                b.step("vector", lambda v: v.tensor_tensor(
                    xn1_ff[:], xn1_fm[:], lo_fm[:], ALU.add))
            b.dma("sync", wt1[:], l1t_in.rearrange("(c p) f -> p c f", p=128))
            for tb in range(4):
                for (dst, woff) in ((qfm1, 0), (kfm1, 64)):
                    def mm_qk1(te, tb=tb, woff=woff):
                        for c in range(DCH):
                            ins = te.matmul(ps7q[:], wt1[:, c, woff:woff + 64],
                                            xn1_ff[:, c, 512 * tb:512 * (tb + 1)],
                                            start=(c == 0), stop=(c == DCH - 1))
                        return ins
                    b.step("tensor", mm_qk1)
                    b.step("scalar", lambda s, dst=dst, tb=tb: s.activation(
                        dst[0:64, 512 * tb:512 * (tb + 1)], ps7q[:], AF.Copy))
            for t in range(T):
                def mm_v1(te, t=t):
                    for c in range(DCH):
                        ins = te.matmul(ps7[:, 0:64],
                                        xn1_ff[:, c, 128 * t:128 * (t + 1)],
                                        wt1[:, c, 128:192],
                                        start=(c == 0), stop=(c == DCH - 1))
                    return ins
                b.step("tensor", mm_v1)
                b.step("scalar", lambda s, t=t: s.activation(vtm1[:, t, :], ps7[:, 0:64], AF.Copy))
            for kt in range(T):
                for ib in range(4):
                    b.step("tensor", lambda te, kt=kt, ib=ib: te.matmul(
                        ps7[:], kfm1[0:64, 128 * kt:128 * (kt + 1)],
                        qfm1[0:64, 512 * ib:512 * (ib + 1)], start=True, stop=True))
                    b.step("scalar", lambda s, ib=ib: s.activation(
                        st1[:, 512 * ib:512 * (ib + 1)], ps7[:], AF.Copy))
                b.step("vector", lambda v: v.tensor_reduce(
                    m7[:], st1[:], AX.X, ALU.max, negate=True))
                b.step("scalar", lambda s: s.activation(
                    st1e[:], st1[:], AF.Exp, bias=m7[:], accum_out=l7[:]))
                b.step("vector", lambda v: v.reciprocal(l7[:], l7[:]))
                b.step("vector", lambda v: v.tensor_scalar(
                    pbt1[:], st1e[:], l7[:], None, ALU.mult))
                if kt == 0 and "dbg_pbt" in dbg_outs:
                    b.dma("sync", dbg_outs["dbg_pbt"], pbt1[:])
                def mm_av1(te, kt=kt):
                    for ib in range(T):
                        ins = te.matmul(ps7av[:, 64 * ib:64 * (ib + 1)],
                                        pbt1[:, 128 * ib:128 * (ib + 1)], vtm1[:, kt, :],
                                        start=(kt == 0 and ib % 8 == 0),
                                        stop=(kt == T - 1))
                    return ins
                b.step("tensor", mm_av1)
            for ti2 in range(2):
                b.step("scalar", lambda s, ti2=ti2: s.activation(
                    s1_sb[:].rearrange("p t v -> p (t v)")[:, 512 * ti2:512 * (ti2 + 1)],
                    ps7av[:, 512 * ti2:512 * (ti2 + 1)], AF.Copy))
            if "dbg_v1" in dbg_outs:
                b.dma("sync", dbg_outs["dbg_v1"].rearrange("(t p) v -> p t v", p=128), vtm1[:])
            if "dbg_s1" in dbg_outs:
                b.dma("sync", dbg_outs["dbg_s1"].rearrange("(t p) v -> p t v", p=128), s1_sb[:])
        if upto < 8:
            _close_lstacks(); b.emit(block); return nc

        pes = ExitStack()
        gwT = pes.enter_context(nc.sbuf_tensor("gwT", [128, DCH, 8], F32, **R))
        gbrep = pes.enter_context(nc.sbuf_tensor("gbrep", [128, 8], F32, **R))
        gbrow = pes.enter_context(nc.sbuf_tensor("gbrow", [1, 8], F32, **R))
        r8a = pes.enter_context(nc.sbuf_tensor("r8a", [128, 8], F32, **R))
        r8b = pes.enter_context(nc.sbuf_tensor("r8b", [128, 8], F32, **R))
        r8c = pes.enter_context(nc.sbuf_tensor("r8c", [128, 8], F32, **R))
        m8 = pes.enter_context(nc.sbuf_tensor("m8", [128, 1], F32, **R))
        m8b = pes.enter_context(nc.sbuf_tensor("m8b", [128, 1], F32, **R))
        ps8 = pes.enter_context(nc.psum_tensor("ps8", [128, 8], F32))
        ps8t = pes.enter_context(nc.psum_tensor("ps8t", [8, 128], F32))
        with pes:
            b.dma("sync", gwT[:], gwt_in.rearrange("(c p) f -> p c f", p=128))
            b.dma("sync", gbrow[:], gb_in.unsqueeze(0))
            b.step("tensor", lambda te: te.matmul(
                ps8[:], ones_f[0:1, 0:128], gbrow[:], start=True, stop=True))
            b.step("scalar", lambda s: s.activation(gbrep[:], ps8[:], AF.Copy))
            for t in range(T):
                def mm_g(te, t=t):
                    for c in range(DCH):
                        ins = te.matmul(ps8[:], xn1_ff[:, c, 128 * t:128 * (t + 1)],
                                        gwT[:, c, :], start=(c == 0), stop=(c == DCH - 1))
                    return ins
                b.step("tensor", mm_g)
                b.step("scalar", lambda s: s.activation(r8a[:], ps8[:], AF.Copy))
                b.step("vector", lambda v: v.tensor_reduce(
                    m8[:], r8a[:], AX.X, ALU.max, negate=True))
                b.step("scalar", lambda s: s.activation(
                    r8b[:], r8a[:], AF.Exp, bias=m8[:], accum_out=m8b[:]))
                b.step("vector", lambda v: v.reciprocal(m8[:], m8b[:]))
                b.step("vector", lambda v, t=t: v.tensor_scalar(
                    cw_sb[:, t, :], r8b[:], m8[:], None, ALU.mult))
                b.step("vector", lambda v, t=t: v.tensor_tensor(
                    r8a[:], cw_sb[:, t, :], gbrep[:], ALU.add))
                b.step("vector", lambda v: v.tensor_reduce(
                    m8[:], r8a[:], AX.X, ALU.max, negate=True))
                b.step("vector", lambda v: v.tensor_scalar(
                    r8b[:], r8a[:], m8[:], 0.0, ALU.add, ALU.is_ge))
                b.step("vector", lambda v: v.scalar_tensor_tensor(
                    r8c[:], r8b[:], NEG, r8a[:], ALU.mult, ALU.add))
                b.step("vector", lambda v: v.tensor_reduce(
                    m8[:], r8c[:], AX.X, ALU.max, negate=True))
                b.step("vector", lambda v: v.tensor_scalar(
                    r8b[:], r8a[:], m8[:], 0.0, ALU.add, ALU.is_ge))
                b.step("vector", lambda v, t=t: v.tensor_tensor(
                    cw_sb[:, t, :], cw_sb[:, t, :], r8b[:], ALU.mult))
                b.step("tensor", lambda te, t=t: te.transpose(
                    ps8t[:], cw_sb[:, t, :], ident_f[:]))
                b.step("scalar", lambda s, t=t: s.activation(
                    cwT_sb[:, 128 * t:128 * (t + 1)], ps8t[:], AF.Copy))
            if "dbg_cw" in dbg_outs:
                b.dma("sync", dbg_outs["dbg_cw"].rearrange("(t p) e -> p t e", p=128), cw_sb[:])
        es_ff.close()
        es_l1.close()
        if upto < 9:
            _close_lstacks(); b.emit(block); return nc

        pes = ExitStack()
        Aw = pes.enter_context(nc.sbuf_tensor("Aw", [128, DCH, 2048], BF16, **R))
        Bw = pes.enter_context(nc.sbuf_tensor("Bw", [128, DCH, 2048], BF16, **R))
        w2b = pes.enter_context(nc.sbuf_tensor("w2b", [128, DCH, 1024], BF16, **R))
        acc = pes.enter_context(nc.sbuf_tensor("acc", [128, 8, 1024], F32, **R))
        xn1h = pes.enter_context(nc.sbuf_tensor("xn1h", [128, DCH, 1024], BF16, **R))
        a_tb = pes.enter_context(nc.sbuf_tensor("a_tb", [128, DCH, 512], BF16, **R))
        cwrep = pes.enter_context(nc.sbuf_tensor("cwrep", [128, 1024], F32, **R))
        t_h1 = pes.enter_context(nc.sbuf_tensor("t_h1", [128, 512], F32, **R))
        t_h3 = pes.enter_context(nc.sbuf_tensor("t_h3", [128, 512], F32, **R))
        t_G = pes.enter_context(nc.sbuf_tensor("t_G", [128, 512], F32, **R))
        t_gm = pes.enter_context(nc.sbuf_tensor("t_gm", [128, 512], F32, **R))
        t_gs = pes.enter_context(nc.sbuf_tensor("t_gs", [128, 512], F32, **R))
        t_lc = pes.enter_context(nc.sbuf_tensor("t_lc", [128, 512], F32, **R))
        t_a = pes.enter_context(nc.sbuf_tensor("t_a", [128, 512], F32, **R))
        b2sb = pes.enter_context(nc.sbuf_tensor("b2sb", [8, 1024], F32, **R))
        sb2r = pes.enter_context(nc.sbuf_tensor("sb2r", [1, 1024], F32, **R))
        g9 = pes.enter_context(nc.sbuf_tensor("g9", [128, 1024], F32, **R))
        x9 = pes.enter_context(nc.sbuf_tensor("x9", [128, 1024], F32, **R))
        ph1e = pes.enter_context(nc.psum_tensor("ph1e", [128, 512], F32))
        ph3e = pes.enter_context(nc.psum_tensor("ph3e", [128, 512], F32))
        ph1o = pes.enter_context(nc.psum_tensor("ph1o", [128, 512], F32))
        ph3o = pes.enter_context(nc.psum_tensor("ph3o", [128, 512], F32))
        ps9y = pes.enter_context(nc.psum_tensor("ps9y", [128, 512], F32))
        ps9c = pes.enter_context(nc.psum_tensor("ps9c", [128, 512], F32))
        with pes:
            b.dma("sync", b2sb[:], b2_in)
            b.dma("sync", sb2r[:], sb2_in.unsqueeze(0))
            for th in range(2):
                for c in range(DCH):
                    b.dma("sync", xn1h[:, c, :],
                          xn1_tm_d[1024 * th:1024 * (th + 1), 128 * c:128 * (c + 1)],
                          transpose=True)
                for e in range(9):
                    shared = (e == 8)
                    if shared:
                        srcs = (sw1t_bf, sw3t_bf, sw2t_bf)
                        evs = (cast_ev["sw1"], cast_ev["sw3"], cast_ev["sw2"])
                    else:
                        srcs = (w1t_bf[e], w3t_bf[e], w2t_bf[e])
                        evs = (cast_ev[f"w1_{e}"], cast_ev[f"w3_{e}"], cast_ev[f"w2_{e}"])
                    b.dma("gpsimd", Aw[:], srcs[0].rearrange("(c p) f -> p c f", p=128),
                          waits=(evs[0],))
                    b.dma("gpsimd", Bw[:], srcs[1].rearrange("(c p) f -> p c f", p=128),
                          waits=(evs[1],))
                    b.dma("gpsimd", w2b[:], srcs[2].rearrange("(c p) f -> p c f", p=128),
                          waits=(evs[2],))
                    if not shared:
                        for tb in range(2):
                            b.step("tensor", lambda te, e=e, th=th, tb=tb: te.matmul(
                                ps9c[:], sel8[:, 128 * e:128 * (e + 1)],
                                cwT_sb[:, 1024 * th + 512 * tb:1024 * th + 512 * (tb + 1)],
                                start=True, stop=True))
                            b.step("scalar", lambda s, tb=tb: s.activation(
                                cwrep[:, 512 * tb:512 * (tb + 1)], ps9c[:], AF.Copy))
                    for tb in range(2):
                        tsl = slice(512 * tb, 512 * (tb + 1))
                        for fc in range(DCH):
                            for (pt, woff, wbuf) in (
                                    (ph1e, 0, Aw), (ph3e, 0, Bw),
                                    (ph1o, 1024, Aw), (ph3o, 1024, Bw)):
                                def mm_h(te, pt=pt, wbuf=wbuf, woff=woff, fc=fc, tsl=tsl):
                                    for c in range(DCH):
                                        ins = te.matmul(
                                            pt[:], wbuf[:, c, woff + 128 * fc:woff + 128 * (fc + 1)],
                                            xn1h[:, c, tsl], start=(c == 0),
                                            stop=(c == DCH - 1))
                                    return ins
                                b.step("tensor", mm_h)
                            ei = 8 if shared else e
                            b.step("scalar", lambda s, fc=fc, ei=ei: s.activation(
                                t_h1[:], ph1e[:], AF.Identity, bias=b1cols[:, ei, fc:fc + 1]),
                                waits=(ev_bias,))
                            b.step("scalar", lambda s, fc=fc, ei=ei: s.activation(
                                t_h3[:], ph3e[:], AF.Identity, bias=b3cols[:, ei, fc:fc + 1]))
                            b.step("vector", lambda v: v.tensor_tensor(t_G[:], t_h1[:], t_h3[:], ALU.mult))
                            b.step("gpsimd", lambda g: g.tensor_scalar(
                                t_gm[:], t_G[:], 7.0, None, ALU.min))
                            b.step("scalar", lambda s: s.activation(t_gs[:], t_gm[:], AF.Gelu_apprx_sigmoid))
                            b.step("scalar", lambda s, fc=fc, ei=ei: s.activation(
                                t_h1[:], ph1o[:], AF.Identity, bias=b1cols[:, ei, 8 + fc:9 + fc]))
                            b.step("scalar", lambda s, fc=fc, ei=ei: s.activation(
                                t_h3[:], ph3o[:], AF.Identity, bias=b3cols[:, ei, 8 + fc:9 + fc]))
                            b.step("vector", lambda v: v.tensor_tensor(t_G[:], t_h1[:], t_h3[:], ALU.mult))
                            b.step("gpsimd", lambda g: g.tensor_scalar(
                                t_lc[:], t_G[:], 7.0, -7.0, ALU.min, ALU.max))
                            if shared:
                                b.step("vector", lambda v, fc=fc: v.scalar_tensor_tensor(
                                    a_tb[:, fc, :], t_lc[:], 1.0, t_gs[:], ALU.add, ALU.mult))
                            else:
                                b.step("vector", lambda v: v.scalar_tensor_tensor(
                                    t_a[:], t_lc[:], 1.0, t_gs[:], ALU.add, ALU.mult))
                                b.step("vector", lambda v, fc=fc, tsl=tsl: v.tensor_tensor(
                                    a_tb[:, fc, :], t_a[:], cwrep[:, tsl], ALU.mult))
                        for tsub in range(4):
                            tt = 8 * th + 4 * tb + tsub
                            ta = 4 * tb + tsub
                            for dh in range(2):
                                def mm_y9(te, tsub=tsub, dh=dh, e=e, tt=tt):
                                    ins = None
                                    if e == 0:
                                        ins = te.matmul(ps9y[:], cwT_sb[:, 128 * tt:128 * (tt + 1)],
                                                        b2sb[:, 512 * dh:512 * (dh + 1)],
                                                        start=True, stop=False)
                                    if e == 8:
                                        ins = te.matmul(ps9y[:], ones_f[0:1, 0:128],
                                                        sb2r[0:1, 512 * dh:512 * (dh + 1)],
                                                        start=True, stop=False)
                                    for fc in range(DCH):
                                        ins = te.matmul(
                                            ps9y[:], a_tb[:, fc, 128 * tsub:128 * (tsub + 1)],
                                            w2b[:, fc, 512 * dh:512 * (dh + 1)],
                                            start=(fc == 0 and e not in (0, 8)),
                                            stop=(fc == DCH - 1))
                                    return ins
                                b.step("tensor", mm_y9)
                                if e == 0:
                                    b.step("scalar", lambda s, ta=ta, dh=dh: s.activation(
                                        acc[:, ta, 512 * dh:512 * (dh + 1)], ps9y[:], AF.Copy))
                                else:
                                    b.step("vector", lambda v, ta=ta, dh=dh: v.tensor_tensor(
                                        acc[:, ta, 512 * dh:512 * (dh + 1)], ps9y[:],
                                        acc[:, ta, 512 * dh:512 * (dh + 1)], ALU.add))
                for ta in range(8):
                    t = 8 * th + ta
                    b.step("scalar", lambda s, ta=ta: s.activation(g9[:], acc[:, ta, :], AF.Silu))
                    b.dma("sync", x9[:], x1_d[128 * t:128 * (t + 1), :])
                    b.step("vector", lambda v, t=t: v.tensor_tensor(
                        g9[:].rearrange("p (r v) -> p r v", v=64),
                        g9[:].rearrange("p (r v) -> p r v", v=64),
                        s1_sb[:, t, :].unsqueeze(1).broadcast_to([128, 16, 64]), ALU.mult))
                    b.step("vector", lambda v: v.tensor_tensor(x9[:], x9[:], g9[:], ALU.add))
                    b.dma("sync", out_d[128 * t:128 * (t + 1), :], x9[:])
        fin_s, fin_v = b.last[0].sem, b.last[1]
        b.ops["gpsimd"].append(lambda E, s=fin_s, v=fin_v: E.wait_ge(s, v))
        _close_lstacks()
        b.emit(block)
    return nc


def prep_inputs(inputs):
    f = lambda v: np.ascontiguousarray(np.asarray(v, dtype=np.float32))
    i = {k: np.asarray(v) for k, v in inputs.items()}
    w1 = i["L1_ffn_w1"]; w3 = i["L1_ffn_w3"]
    w1sep = np.concatenate([w1[:, 0::2, :], w1[:, 1::2, :]], axis=1)
    w3sep = np.concatenate([w3[:, 0::2, :], w3[:, 1::2, :]], axis=1)
    sw1 = i["L1_ffn_sw1"]; sw3 = i["L1_ffn_sw3"]
    sw1sep = np.concatenate([sw1[0::2, :], sw1[1::2, :]], axis=0)
    sw3sep = np.concatenate([sw3[0::2, :], sw3[1::2, :]], axis=0)

    def cols(bvec):
        return f(bvec.reshape(8, 128, 2).transpose(1, 2, 0).reshape(128, 16))

    shared = {
        "wqt": f(i["L0_attn_Wq"].T), "wkt": f(i["L0_attn_Wk"].T),
        "wvt": f(i["L0_attn_Wv"].T), "wot": f(i["L0_attn_Wo"].T),
        "phiwt": f(i["L0_ffn_phi_w"].T), "phi_b": f(i["L0_ffn_phi_b"]),
        "l1t": f(np.concatenate([i["L1_attn_Wq"].T, i["L1_attn_Wk"].T,
                                 i["L1_attn_Wv"].T], axis=1)),
        "gwt": f(i["L1_ffn_gate_w"].T), "gate_b": f(i["L1_ffn_gate_b"]),
        "w1t": f(w1sep.transpose(0, 2, 1)), "w3t": f(w3sep.transpose(0, 2, 1)),
        "w2t": f(i["L1_ffn_w2"].transpose(0, 2, 1)),
        "sw1t": f(sw1sep.T), "sw3t": f(sw3sep.T), "sw2t": f(i["L1_ffn_sw2"].T),
        "b1c": np.stack([cols(i["L1_ffn_b1"][e]) for e in range(8)]),
        "b3c": np.stack([cols(i["L1_ffn_b3"][e]) for e in range(8)]),
        "sb1c": cols(i["L1_ffn_sb1"]), "sb3c": cols(i["L1_ffn_sb3"]),
        "b2": f(i["L1_ffn_b2"]), "sb2": f(i["L1_ffn_sb2"]),
    }
    per_core = []
    for c in range(8):
        m = dict(shared)
        m["x"] = f(i["L0_attn_in_features"][c])
        m["ew_sl"] = f(i["L0_ffn_expert_w"][8 * c:8 * (c + 1)])
        m["eb_sl"] = f(i["L0_ffn_expert_b"][8 * c:8 * (c + 1)])
        per_core.append(m)
    return per_core


def kernel(**inputs):
    nc = build_nc()
    in_maps = prep_inputs(inputs)
    res = run_bass_kernel_spmd(nc, in_maps, core_ids=list(range(8)))
    return np.stack([np.asarray(r["out"], dtype=np.float32) for r in res.results])


# version 02
# speedup vs baseline: 1.3778x; 1.3778x over previous
import numpy as np
from contextlib import ExitStack
import concourse.bass as bass
import concourse.mybir as mybir
from concourse.bass_utils import run_bass_kernel_spmd

F32 = mybir.dt.float32
BF16 = mybir.dt.bfloat16
AF = mybir.ActivationFunctionType
ALU = mybir.AluOpType
AX = mybir.AxisListType

ENG = ("tensor", "vector", "scalar", "gpsimd", "sync")
NEG = -1.0e30


class Chan:
    def __init__(self, nc, es, name):
        self.sem = es.enter_context(nc.semaphore(name))
        self.n = 0


class B:

    def __init__(self, nc, es):
        self.nc, self.es = nc, es
        self.ops = {e: [] for e in ENG}
        self.chans = {}
        self.last = None

    def ch(self, name):
        if name not in self.chans:
            self.chans[name] = Chan(self.nc, self.es, name)
        return self.chans[name]

    def step(self, eng, fn, ch=None, inc=1, waits=(), seq=True, set_last=True):
        ws = [(c, v) for (c, v) in waits if c is not None and v > 0]
        if seq and self.last is not None:
            ws.append(self.last)
        if ch is None:
            ch = self.ch("c_" + eng)
        ch.n += inc
        val = ch.n
        ws_f = tuple((c.sem, v) for (c, v) in ws)

        def closure(E, fn=fn, ws_f=ws_f, sem=ch.sem, inc=inc):
            for (s, v) in ws_f:
                E.wait_ge(s, v)
            ins = fn(E)
            ins.then_inc(sem, inc)

        self.ops[eng].append(closure)
        if set_last:
            self.last = (ch, val)
        return (ch, val)

    def dma(self, q, out, in_, waits=(), seq=True, set_last=True, ch=None, **kw):
        if ch is None:
            ch = self.ch("d_" + q)
        return self.step(q, lambda E: E.dma_start(out=out, in_=in_, **kw),
                         ch=ch, inc=16, waits=waits, seq=seq, set_last=set_last)

    def emit(self, block):
        for eng in ENG:
            fns = self.ops[eng]
            getattr(block, eng)(lambda E, fns=fns: [fn(E) for fn in fns])


def build_nc(upto=99, dbg=()):
    nc = bass.Bass("TRN2", target_bir_lowering=False)
    T = 16
    DCH = 8

    EI = lambda n, s, dt=F32: nc.dram_tensor(n, s, dt, kind="ExternalInput").ap()
    EO = lambda n, s, dt=F32: nc.dram_tensor(n, s, dt, kind="ExternalOutput").ap()
    IN = lambda n, s, dt: nc.dram_tensor(n, s, dt, kind="Internal").ap()

    x_in = EI("x", [2048, 1024])
    wqt_in, wkt_in, wvt_in, wot_in = (EI(n, [1024, 1024]) for n in ["wqt", "wkt", "wvt", "wot"])
    phiwt_in = EI("phiwt", [1024, 2048]); phib_in = EI("phi_b", [2048])
    ew_in = EI("ew_sl", [8, 1024, 1024]); eb_in = EI("eb_sl", [8, 1024])
    l1t_in = EI("l1t", [1024, 192])
    gwt_in = EI("gwt", [1024, 8]); gb_in = EI("gate_b", [8])
    w1t_in = EI("w1t", [8, 1024, 2048]); w3t_in = EI("w3t", [8, 1024, 2048])
    w2t_in = EI("w2t", [8, 1024, 1024])
    sw1t_in = EI("sw1t", [1024, 2048]); sw3t_in = EI("sw3t", [1024, 2048])
    sw2t_in = EI("sw2t", [1024, 1024])
    b1c_in = EI("b1c", [8, 128, 16]); b3c_in = EI("b3c", [8, 128, 16])
    sb1c_in = EI("sb1c", [128, 16]); sb3c_in = EI("sb3c", [128, 16])
    b2_in = EI("b2", [8, 1024]); sb2_in = EI("sb2", [1024])

    out_d = EO("out", [2048, 1024])

    wqt_bf, wkt_bf, wvt_bf, wot_bf = (IN(n, [1024, 1024], BF16)
                                      for n in ["wqt_bf", "wkt_bf", "wvt_bf", "wot_bf"])
    phiwt_bf = IN("phiwt_bf", [1024, 2048], BF16)
    l1t_bf = IN("l1t_bf", [1024, 192], BF16)
    gwt_bf = IN("gwt_bf", [1024, 8], BF16)
    w1t_bf = IN("w1t_bf", [8, 1024, 2048], BF16)
    w3t_bf = IN("w3t_bf", [8, 1024, 2048], BF16)
    w2t_bf = IN("w2t_bf", [8, 1024, 1024], BF16)
    sw1t_bf = IN("sw1t_bf", [1024, 2048], BF16)
    sw3t_bf = IN("sw3t_bf", [1024, 2048], BF16)
    sw2t_bf = IN("sw2t_bf", [1024, 1024], BF16)

    xn_tm_d = IN("xn_tm_d", [2048, 1024], BF16)
    ctm_d = IN("ctm_d", [2048, 2048], BF16)
    xt_part = IN("xt_part", [2048, 1024], F32)
    xt_red = IN("xt_red", [256, 1024], F32)
    s_d = IN("s_d", [2048, 1024], F32)
    xtt_d = IN("xtt_d", [256, 1024], BF16)
    ytl_d = IN("ytl_d", [256, 1024], BF16)
    yt_all = IN("yt_all", [2048, 1024], BF16)
    x1_d = IN("x1_d", [2048, 1024], F32)
    xn1_tm_d = IN("xn1_tm_d", [2048, 1024], BF16)
    xn1lo_d = IN("xn1lo_d", [2048, 1024], BF16)

    dbg_outs = {}
    for name, shape, dt in [
        ("dbg_xn", [2048, 1024], BF16), ("dbg_D", [2048, 2048], BF16),
        ("dbg_C", [2048, 2048], BF16), ("dbg_xtpart", [2048, 1024], F32),
        ("dbg_xtred", [256, 1024], F32), ("dbg_s", [2048, 1024], F32),
        ("dbg_ytall", [2048, 1024], BF16), ("dbg_x1", [2048, 1024], F32),
        ("dbg_xtt", [256, 1024], BF16), ("dbg_ytl", [256, 1024], BF16),
        ("dbg_xttfm", [128, 2048], BF16), ("dbg_ewb", [128, 16384], BF16),
        ("dbg_xn1", [2048, 1024], BF16), ("dbg_s1", [2048, 64], F32),
        ("dbg_pbt", [128, 2048], F32), ("dbg_v1", [2048, 64], F32),
        ("dbg_cw", [2048, 8], F32),
    ]:
        if name in dbg:
            dbg_outs[name] = EO(name, shape, dt)

    es = ExitStack()
    b = B(nc, es)
    sbL = lambda *a: es.enter_context(nc.sbuf_tensor(*a))
    R = {"side": "right"}

    with es, nc.Block() as block:
        ident_bf = sbL("ident_bf", [128, 128], BF16)
        ident_f = sbL("ident_f", [128, 128], F32)
        maskneg = sbL("maskneg", [128, 128], F32)
        ones_f = sbL("ones_f", [1, 512], F32)
        sel8 = sbL("sel8", [8, 8 * 128], F32)
        s1_sb = sbL("s1_sb", [128, T, 64], F32)
        cw_sb = sbL("cw_sb", [128, T, 8], F32)
        cwT_sb = sbL("cwT_sb", [8, 2048], F32)
        b1cols = sbL("b1cols", [128, 9, 16], F32)
        b3cols = sbL("b3cols", [128, 9, 16], F32)

        def _consts(g):
            g.memset(ident_bf[:], 1.0)
            g.affine_select(ident_bf[:], ident_bf[:], [[-1, 128]], ALU.is_equal,
                            0.0, base=0, channel_multiplier=1)
            g.memset(ident_f[:], 1.0)
            g.affine_select(ident_f[:], ident_f[:], [[-1, 128]], ALU.is_equal,
                            0.0, base=0, channel_multiplier=1)
            g.memset(maskneg[:], 0.0)
            g.affine_select(maskneg[:], maskneg[:], [[-1, 128]], ALU.is_ge,
                            NEG, base=0, channel_multiplier=1)
            g.memset(ones_f[:], 1.0)
            g.memset(sel8[:], 1.0)
            ins = None
            for e in range(8):
                ins = g.affine_select(sel8[:, e * 128:(e + 1) * 128],
                                      sel8[:, e * 128:(e + 1) * 128], [[0, 128]],
                                      ALU.is_equal, 0.0, base=-e, channel_multiplier=1)
            return ins
        b.step("gpsimd", _consts)

        wes = ExitStack()
        ps_warm = wes.enter_context(nc.psum_tensor("ps_warm", [128, 128], F32))
        with wes:
            b.step("tensor", lambda te: te.matmul(
                ps_warm[:], ident_bf[:], ident_bf[:], start=True, stop=True))

        for e in range(8):
            b.dma("scalar", b1cols[:, e, :], b1c_in[e], seq=False, set_last=False)
            b.dma("scalar", b3cols[:, e, :], b3c_in[e], seq=False, set_last=False)
        b.dma("scalar", b1cols[:, 8, :], sb1c_in, seq=False, set_last=False)
        ev_bias = b.dma("scalar", b3cols[:, 8, :], sb3c_in, seq=False, set_last=False)

        cast_ev = {}

        def cast(name, dst, src, grp):
            ch = b.ch(grp)
            b.dma("gpsimd", dst, src, ch=ch, seq=False, set_last=False)
            cast_ev[name] = (ch, None)

        cast("phiwt", phiwt_bf, phiwt_in, "dc0")
        cast("wqt", wqt_bf, wqt_in, "dc0"); cast("wkt", wkt_bf, wkt_in, "dc0")
        cast("wvt", wvt_bf, wvt_in, "dc0"); cast("wot", wot_bf, wot_in, "dc0")
        for e in range(8):
            cast(f"w1_{e}", w1t_bf[e], w1t_in[e], "dc2")
            cast(f"w3_{e}", w3t_bf[e], w3t_in[e], "dc2")
            cast(f"w2_{e}", w2t_bf[e], w2t_in[e], "dc2")
        cast("sw1", sw1t_bf, sw1t_in, "dc1"); cast("sw3", sw3t_bf, sw3t_in, "dc1")
        cast("sw2", sw2t_bf, sw2t_in, "dc1")
        cast_ev = {k: (ch, ch.n) for k, (ch, _) in cast_ev.items()}

        def load_wT(buf, src_bf, ev, cols, q="sync"):
            return b.dma(q, buf[:, :, 0:cols],
                         src_bf.rearrange("(c p) f -> p c f", p=128),
                         waits=(ev,))

        _lstacks = []

        def _close_lstacks():
            for st in reversed(_lstacks):
                st.close()

        es_l0b = ExitStack(); _lstacks.append(es_l0b)
        xn_fm = es_l0b.enter_context(nc.sbuf_tensor("xn_fm", [128, DCH, 2048], BF16))
        es_l0a = ExitStack(); _lstacks.append(es_l0a)
        xn_tm = es_l0a.enter_context(nc.sbuf_tensor("xn_tm", [128, T, 1024], BF16))

        pes = ExitStack()
        xall = pes.enter_context(nc.sbuf_tensor("xall", [128, T, 1024], F32, **R))
        p0sc = pes.enter_context(nc.sbuf_tensor("p0sc", [128, 1024], F32, **R))
        p0st = pes.enter_context(nc.sbuf_tensor("p0st", [128, T], F32, **R))
        p0st2 = pes.enter_context(nc.sbuf_tensor("p0st2", [128, T], F32, **R))
        with pes:
            b.dma("sync", xall[:], x_in.rearrange("(t p) d -> p t d", p=128))
            b.step("vector", lambda v: v.tensor_reduce(p0st[:], xall[:], AX.X, ALU.add))
            b.step("vector", lambda v: v.tensor_scalar(p0st2[:], p0st[:], 1.0 / 1024, None, ALU.mult))
            b.step("vector", lambda v: v.scalar_tensor_tensor(
                xall[:], p0st2[:].unsqueeze(2).broadcast_to([128, T, 1024]), -1.0,
                xall[:], ALU.mult, ALU.add))
            for t in range(T):
                b.step("scalar", lambda s, t=t: s.activation(
                    p0sc[:], xall[:, t, :], AF.Square, accum_out=p0st[:, t:t + 1]))
            b.step("vector", lambda v: v.tensor_scalar(
                p0st[:], p0st[:], 1.0 / 1024, 1e-5, ALU.mult, ALU.add))
            b.step("scalar", lambda s: s.activation(p0st2[:], p0st[:], AF.Sqrt))
            b.step("vector", lambda v: v.reciprocal(p0st[:], p0st2[:]))
            b.step("vector", lambda v: v.tensor_tensor(
                xn_tm[:], xall[:], p0st[:].unsqueeze(2).broadcast_to([128, T, 1024]),
                ALU.mult))
            b.dma("sync", xn_tm_d.rearrange("(t p) d -> p t d", p=128), xn_tm[:])
            for c in range(DCH):
                b.dma("sync", xn_fm[:, c, :], xn_tm_d[:, 128 * c:128 * (c + 1)],
                      transpose=True)
            if "dbg_xn" in dbg_outs:
                b.dma("sync", dbg_outs["dbg_xn"], xn_tm_d)
        if upto < 1:
            _close_lstacks(); b.emit(block); return nc

        D_all = es_l0a.enter_context(nc.sbuf_tensor("D_all", [128, T, 2048], BF16))
        pes = ExitStack()
        phiwT = pes.enter_context(nc.sbuf_tensor("phiwT", [128, DCH, 2048], BF16, **R))
        phib_r = pes.enter_context(nc.sbuf_tensor("phib_r", [1, 2048], F32, **R))
        E_sc = pes.enter_context(nc.sbuf_tensor("E_sc", [128, 2048], F32, **R))
        C_sc = pes.enter_context(nc.sbuf_tensor("C_sc", [128, 2048], BF16, **R))
        sd_sc = pes.enter_context(nc.sbuf_tensor("sd_sc", [128, 64], F32, **R))
        sc_sc = pes.enter_context(nc.sbuf_tensor("sc_sc", [128, 32], F32, **R))
        ps1 = pes.enter_context(nc.psum_tensor("ps1", [128, 512], F32))
        with pes:
            load_wT(phiwT, phiwt_bf, cast_ev["phiwt"], 2048)
            b.dma("sync", phib_r[:], phib_in.unsqueeze(0))
            for t in range(T):
                for ebk in range(4):
                    def mm_logits(te, t=t, ebk=ebk):
                        ins = te.matmul(ps1[:], ones_f[0:1, 0:128],
                                        phib_r[0:1, 512 * ebk:512 * (ebk + 1)],
                                        start=True, stop=False)
                        for c in range(DCH):
                            ins = te.matmul(ps1[:], xn_fm[:, c, 128 * t:128 * (t + 1)],
                                            phiwT[:, c, 512 * ebk:512 * (ebk + 1)],
                                            start=False, stop=(c == DCH - 1))
                        return ins
                    b.step("tensor", mm_logits)
                    b.step("scalar", lambda s, ebk=ebk: s.activation(
                        E_sc[:, 512 * ebk:512 * (ebk + 1)], ps1[:], AF.Exp))
                e3 = E_sc[:].rearrange("p (n q) -> p n q", q=32)
                b.step("vector", lambda v, e3=e3: v.tensor_reduce(sd_sc[:], e3, AX.X, ALU.add))
                b.step("vector", lambda v, e3=e3: v.tensor_reduce(
                    sc_sc[:], e3.transpose([0, 2, 1]), AX.X, ALU.add))
                b.step("vector", lambda v: v.reciprocal(sd_sc[:], sd_sc[:]))
                b.step("vector", lambda v: v.reciprocal(sc_sc[:], sc_sc[:]))
                b.step("vector", lambda v, t=t, e3=e3: v.tensor_tensor(
                    D_all[:, t, :].rearrange("p (n q) -> p n q", q=32), e3,
                    sd_sc[:].unsqueeze(2).broadcast_to([128, 64, 32]), ALU.mult))
                b.step("vector", lambda v, e3=e3: v.tensor_tensor(
                    C_sc[:].rearrange("p (n q) -> p n q", q=32), e3,
                    sc_sc[:].unsqueeze(1).broadcast_to([128, 64, 32]), ALU.mult))
                b.dma("sync", ctm_d[128 * t:128 * (t + 1), :], C_sc[:])
            if "dbg_D" in dbg_outs:
                b.dma("sync", dbg_outs["dbg_D"].rearrange("(t p) e -> p t e", p=128), D_all[:])
            if "dbg_C" in dbg_outs:
                b.dma("sync", dbg_outs["dbg_C"], ctm_d)
        if upto < 2:
            _close_lstacks(); b.emit(block); return nc

        pes = ExitStack()
        xtst = pes.enter_context(nc.sbuf_tensor("xtst", [128, 1024], F32, **R))
        ps2 = pes.enter_context(nc.psum_tensor("ps2", [128, 512], F32))
        with pes:
            for npc in range(T):
                for dh in range(2):
                    def mm_xt(te, npc=npc, dh=dh):
                        for t in range(T):
                            ins = te.matmul(ps2[:], D_all[:, t, 128 * npc:128 * (npc + 1)],
                                            xn_tm[:, t, 512 * dh:512 * (dh + 1)],
                                            start=(t == 0), stop=(t == T - 1))
                        return ins
                    b.step("tensor", mm_xt)
                    b.step("scalar", lambda s, dh=dh: s.activation(
                        xtst[:, 512 * dh:512 * (dh + 1)], ps2[:], AF.Copy))
                b.dma("sync", xt_part[128 * npc:128 * (npc + 1), :], xtst[:])
            b.step("gpsimd", lambda g: g.collective_compute(
                "ReduceScatter", ALU.add, replica_groups=[list(range(8))],
                ins=[xt_part.opt()], outs=[xt_red.opt()]), ch=b.ch("cc1"))
            if "dbg_xtpart" in dbg_outs:
                b.dma("sync", dbg_outs["dbg_xtpart"], xt_part)
            if "dbg_xtred" in dbg_outs:
                b.dma("sync", dbg_outs["dbg_xtred"], xt_red)
        es_l0a.close()
        if upto < 3:
            _close_lstacks(); b.emit(block); return nc

        pes = ExitStack()
        woT = pes.enter_context(nc.sbuf_tensor("woT", [128, DCH, 1024], BF16, **R))
        Qfm = pes.enter_context(nc.sbuf_tensor("Qfm", [128, DCH, 2048], BF16, **R))
        Kfm = pes.enter_context(nc.sbuf_tensor("Kfm", [128, DCH, 2048], BF16, **R))
        Vtm = pes.enter_context(nc.sbuf_tensor("Vtm", [128, T, 1024], BF16, **R))
        S_sc = pes.enter_context(nc.sbuf_tensor("S_sc", [128, 2048], F32, **R))
        Pb_sc = pes.enter_context(nc.sbuf_tensor("Pb_sc", [128, 2048], BF16, **R))
        PbT = pes.enter_context(nc.sbuf_tensor("PbT", [128, T, 128], BF16, **R))
        avtm = pes.enter_context(nc.sbuf_tensor("avtm", [128, 1024], BF16, **R))
        avfm = pes.enter_context(nc.sbuf_tensor("avfm", [128, DCH, 128], BF16, **R))
        stile = pes.enter_context(nc.sbuf_tensor("stile", [128, 1024], F32, **R))
        m3 = pes.enter_context(nc.sbuf_tensor("m3", [128, 1], F32, **R))
        l3 = pes.enter_context(nc.sbuf_tensor("l3", [128, 1], F32, **R))
        rl3 = pes.enter_context(nc.sbuf_tensor("rl3", [128, 1], F32, **R))
        ps3a = pes.enter_context(nc.psum_tensor("ps3a", [128, 512], F32))
        ps3t = pes.enter_context(nc.psum_tensor("ps3t", [128, 128], BF16))
        ps3av = pes.enter_context(nc.psum_tensor("ps3av", [128, 512], F32))
        ps3s = pes.enter_context(nc.psum_tensor("ps3s", [128, 512], F32))
        with pes:
            load_wT(woT, wot_bf, cast_ev["wot"], 1024)
            qes = ExitStack()
            wqk = qes.enter_context(nc.sbuf_tensor("wqk", [128, DCH, 1024], BF16, **R))
            with qes:
                for (dst, wsrc, wev) in ((Qfm, wqt_bf, "wqt"), (Kfm, wkt_bf, "wkt")):
                    b.dma("sync", wqk[:],
                          wsrc.rearrange("(c p) f -> p c f", p=128),
                          waits=(cast_ev[wev],))
                    for fc in range(DCH):
                        for tb in range(4):
                            def mm_qk(te, fc=fc, tb=tb):
                                for c in range(DCH):
                                    ins = te.matmul(
                                        ps3a[:], wqk[:, c, 128 * fc:128 * (fc + 1)],
                                        xn_fm[:, c, 512 * tb:512 * (tb + 1)],
                                        start=(c == 0), stop=(c == DCH - 1))
                                return ins
                            b.step("tensor", mm_qk)
                            b.step("scalar", lambda s, dst=dst, fc=fc, tb=tb: s.activation(
                                dst[:, fc, 512 * tb:512 * (tb + 1)], ps3a[:], AF.Copy))
            ves = ExitStack()
            wvT = ves.enter_context(nc.sbuf_tensor("wvT", [128, DCH, 1024], BF16, **R))
            with ves:
                load_wT(wvT, wvt_bf, cast_ev["wvt"], 1024)
                for t in range(T):
                    for fh in range(2):
                        def mm_v(te, t=t, fh=fh):
                            for c in range(DCH):
                                ins = te.matmul(ps3a[:], xn_fm[:, c, 128 * t:128 * (t + 1)],
                                                wvT[:, c, 512 * fh:512 * (fh + 1)],
                                                start=(c == 0), stop=(c == DCH - 1))
                            return ins
                        b.step("tensor", mm_v)
                        b.step("scalar", lambda s, t=t, fh=fh: s.activation(
                            Vtm[:, t, 512 * fh:512 * (fh + 1)], ps3a[:], AF.Copy))
            es_l0b.close()
            for qi in range(T):
                klen = 128 * (qi + 1)
                nkb = (klen + 511) // 512
                for kb in range(nkb):
                    kw = min(512, klen - 512 * kb)
                    def mm_s(te, qi=qi, kb=kb, kw=kw):
                        for c in range(DCH):
                            ins = te.matmul(ps3a[:, 0:kw], Qfm[:, c, 128 * qi:128 * (qi + 1)],
                                            Kfm[:, c, 512 * kb:512 * kb + kw],
                                            start=(c == 0), stop=(c == DCH - 1))
                        return ins
                    b.step("tensor", mm_s)
                    b.step("scalar", lambda s, kb=kb, kw=kw: s.activation(
                        S_sc[:, 512 * kb:512 * kb + kw], ps3a[:, 0:kw], AF.Copy))
                b.step("vector", lambda v, klen=klen: v.tensor_tensor(
                    S_sc[:, klen - 128:klen], S_sc[:, klen - 128:klen], maskneg[:],
                    ALU.add))
                b.step("vector", lambda v, klen=klen: v.tensor_reduce(
                    m3[:], S_sc[:, 0:klen], AX.X, ALU.max, negate=True))
                b.step("vector", lambda v: v.tensor_scalar(
                    m3[:], m3[:], 1.0 / 32.0, None, ALU.mult))
                b.step("scalar", lambda s, klen=klen: s.activation(
                    Pb_sc[:, 0:klen], S_sc[:, 0:klen], AF.Exp, bias=m3[:],
                    scale=1.0 / 32.0, accum_out=l3[:]))
                b.step("vector", lambda v: v.reciprocal(rl3[:], l3[:]))
                for kt in range(qi + 1):
                    b.step("tensor", lambda te, kt=kt: te.transpose(
                        ps3t[:], Pb_sc[:, 128 * kt:128 * (kt + 1)], ident_bf[:]))
                    b.step("vector", lambda v, kt=kt: v.tensor_copy(PbT[:, kt, :], ps3t[:]))
                for dh in range(2):
                    def mm_av(te, qi=qi, dh=dh):
                        for kt in range(qi + 1):
                            ins = te.matmul(ps3av[:], PbT[:, kt, :],
                                            Vtm[:, kt, 512 * dh:512 * (dh + 1)],
                                            start=(kt == 0), stop=(kt == qi))
                        return ins
                    b.step("tensor", mm_av)
                    b.step("scalar", lambda s, dh=dh: s.activation(
                        avtm[:, 512 * dh:512 * (dh + 1)], ps3av[:], AF.Copy,
                        scale=rl3[:]))
                for c in range(DCH):
                    b.step("tensor", lambda te, c=c: te.transpose(
                        ps3t[:], avtm[:, 128 * c:128 * (c + 1)], ident_bf[:]))
                    b.step("vector", lambda v, c=c: v.tensor_copy(avfm[:, c, :], ps3t[:]))
                for fh in range(2):
                    def mm_so(te, fh=fh):
                        for c in range(DCH):
                            ins = te.matmul(ps3s[:], avfm[:, c, :],
                                            woT[:, c, 512 * fh:512 * (fh + 1)],
                                            start=(c == 0), stop=(c == DCH - 1))
                        return ins
                    b.step("tensor", mm_so)
                    b.step("scalar", lambda s, fh=fh: s.activation(
                        stile[:, 512 * fh:512 * (fh + 1)], ps3s[:], AF.Copy))
                b.dma("sync", s_d[128 * qi:128 * (qi + 1), :], stile[:])
            if "dbg_s" in dbg_outs:
                b.dma("sync", dbg_outs["dbg_s"], s_d)
        if upto < 4:
            _close_lstacks(); b.emit(block); return nc

        pes = ExitStack()
        xts = pes.enter_context(nc.sbuf_tensor("xts", [128, 2, 1024], F32, **R))
        p4sc = pes.enter_context(nc.sbuf_tensor("p4sc", [128, 1024], F32, **R))
        p4a = pes.enter_context(nc.sbuf_tensor("p4a", [128, 2], F32, **R))
        p4b = pes.enter_context(nc.sbuf_tensor("p4b", [128, 2], F32, **R))
        xnt = pes.enter_context(nc.sbuf_tensor("xnt", [128, 2, 1024], BF16, **R))
        xttfm = pes.enter_context(nc.sbuf_tensor("xttfm", [128, DCH, 256], BF16, **R))
        ewb = pes.enter_context(nc.sbuf_tensor("ewb", [128, DCH, 2048], BF16, **R))
        ebr = pes.enter_context(nc.sbuf_tensor("ebr", [1, 1024], F32, **R))
        ytst = pes.enter_context(nc.sbuf_tensor("ytst", [32, 512], BF16, **R))
        ps4 = pes.enter_context(nc.psum_tensor("ps4", [32, 512], F32))
        with pes:
            ev_ew = [None] * 8
            ev_mm4 = [None] * 8
            p4_start = b.last

            def load_ew(n):
                w = (ev_mm4[n - 2],) if n >= 2 else (p4_start,)
                ev_ew[n] = b.dma(
                    "gpsimd", ewb[:, :, 1024 * (n % 2):1024 * (n % 2) + 1024],
                    ew_in[n].rearrange("(c p) h -> p c h", p=128),
                    ch=b.ch(f"dew{n % 2}"), waits=w, seq=False, set_last=False)
            load_ew(0); load_ew(1)
            b.dma("sync", xts[:], xt_red.rearrange("(t p) d -> p t d", p=128),
                  waits=((b.ch("cc1"), 1),))
            b.step("vector", lambda v: v.tensor_scalar(
                xts[:], xts[:], 65000.0, -33000.0, ALU.min, ALU.max))
            b.step("vector", lambda v: v.tensor_reduce(p4a[:], xts[:], AX.X, ALU.add))
            b.step("vector", lambda v: v.tensor_scalar(p4b[:], p4a[:], 1.0 / 1024, None, ALU.mult))
            b.step("vector", lambda v: v.scalar_tensor_tensor(
                xts[:], p4b[:].unsqueeze(2).broadcast_to([128, 2, 1024]), -1.0,
                xts[:], ALU.mult, ALU.add))
            for t in range(2):
                b.step("scalar", lambda s, t=t: s.activation(
                    p4sc[:], xts[:, t, :], AF.Square, accum_out=p4a[:, t:t + 1]))
            b.step("vector", lambda v: v.tensor_scalar(
                p4a[:], p4a[:], 1.0 / 1024, 1e-5, ALU.mult, ALU.add))
            b.step("scalar", lambda s: s.activation(p4b[:], p4a[:], AF.Sqrt))
            b.step("vector", lambda v: v.reciprocal(p4a[:], p4b[:]))
            b.step("vector", lambda v: v.tensor_tensor(
                xts[:], xts[:], p4a[:].unsqueeze(2).broadcast_to([128, 2, 1024]),
                ALU.mult))
            b.step("scalar", lambda s: s.activation(xnt[:], xts[:], AF.Tanh))
            b.dma("sync", xtt_d.rearrange("(t p) d -> p t d", p=128), xnt[:])
            for c in range(DCH):
                b.dma("sync", xttfm[:, c, :], xtt_d[:, 128 * c:128 * (c + 1)],
                      transpose=True)
            for n in range(8):
                b.dma("sync", ebr[:], eb_in[n].unsqueeze(0))
                for hh in range(2):
                    def mm_yt(te, n=n, hh=hh):
                        ins = te.matmul(ps4[:], ones_f[0:1, 0:32],
                                        ebr[0:1, 512 * hh:512 * (hh + 1)],
                                        start=True, stop=False)
                        for c in range(DCH):
                            ins = te.matmul(
                                ps4[:], xttfm[:, c, 32 * n:32 * (n + 1)],
                                ewb[:, c, 1024 * (n % 2) + 512 * hh:1024 * (n % 2) + 512 * (hh + 1)],
                                start=False, stop=(c == DCH - 1))
                        return ins
                    ev_mm4[n] = b.step("tensor", mm_yt, waits=(ev_ew[n],))
                    if n == 1 and hh == 0 and "dbg_ewb" in dbg_outs:
                        b.dma("sync", dbg_outs["dbg_ewb"], ewb[:].rearrange("p c n -> p (c n)"))
                    b.step("scalar", lambda s: s.activation(ytst[:], ps4[:], AF.Copy))
                    b.dma("sync", ytl_d[32 * n:32 * (n + 1), 512 * hh:512 * (hh + 1)], ytst[:])
                if n + 2 < 8:
                    load_ew(n + 2)
            b.step("gpsimd", lambda g: g.collective_compute(
                "AllGather", ALU.bypass, replica_groups=[list(range(8))],
                ins=[ytl_d.opt()], outs=[yt_all.opt()]), ch=b.ch("cc2"))
            if "dbg_xttfm" in dbg_outs:
                b.dma("sync", dbg_outs["dbg_xttfm"], xttfm[:].rearrange("p c n -> p (c n)"))
            if "dbg_xtt" in dbg_outs:
                b.dma("sync", dbg_outs["dbg_xtt"], xtt_d)
            if "dbg_ytl" in dbg_outs:
                b.dma("sync", dbg_outs["dbg_ytl"], ytl_d)
            if "dbg_ytall" in dbg_outs:
                b.dma("sync", dbg_outs["dbg_ytall"], yt_all)
        if upto < 5:
            _close_lstacks(); b.emit(block); return nc

        pes = ExitStack()
        Cfm = pes.enter_context(nc.sbuf_tensor("Cfm", [128, T, 2048], BF16, **R))
        ytsb = pes.enter_context(nc.sbuf_tensor("ytsb", [128, T, 1024], BF16, **R))
        gt5 = pes.enter_context(nc.sbuf_tensor("gt5", [128, 1024], F32, **R))
        xt5 = pes.enter_context(nc.sbuf_tensor("xt5", [128, 1024], F32, **R))
        st5 = pes.enter_context(nc.sbuf_tensor("st5", [128, 1024], F32, **R))
        ps5 = pes.enter_context(nc.psum_tensor("ps5", [128, 512], F32))
        with pes:
            b.dma("sync", ytsb[:], yt_all.rearrange("(c p) h -> p c h", p=128),
                  waits=((b.ch("cc2"), 1),))
            for npc in range(T):
                b.dma("sync", Cfm[:, npc, :], ctm_d[:, 128 * npc:128 * (npc + 1)],
                      transpose=True)
            for t in range(T):
                for hh in range(2):
                    def mm_y(te, t=t, hh=hh):
                        for npc in range(T):
                            ins = te.matmul(ps5[:], Cfm[:, npc, 128 * t:128 * (t + 1)],
                                            ytsb[:, npc, 512 * hh:512 * (hh + 1)],
                                            start=(npc == 0), stop=(npc == T - 1))
                        return ins
                    b.step("tensor", mm_y)
                    b.step("scalar", lambda s, hh=hh: s.activation(
                        gt5[:, 512 * hh:512 * (hh + 1)], ps5[:], AF.Silu))
                b.dma("sync", xt5[:], x_in[128 * t:128 * (t + 1), :])
                b.dma("sync", st5[:], s_d[128 * t:128 * (t + 1), :])
                b.step("vector", lambda v: v.tensor_tensor(gt5[:], gt5[:], st5[:], ALU.mult))
                b.step("vector", lambda v: v.tensor_tensor(xt5[:], xt5[:], gt5[:], ALU.add))
                b.dma("sync", x1_d[128 * t:128 * (t + 1), :], xt5[:])
            if "dbg_x1" in dbg_outs:
                b.dma("sync", dbg_outs["dbg_x1"], x1_d)
        if upto < 6:
            _close_lstacks(); b.emit(block); return nc

        es_l1 = ExitStack(); _lstacks.append(es_l1)
        xn1_fm = es_l1.enter_context(nc.sbuf_tensor("xn1_fm", [128, DCH, 2048], BF16))
        pes = ExitStack()
        x1all = pes.enter_context(nc.sbuf_tensor("x1all", [128, T, 1024], F32, **R))
        p6sc = pes.enter_context(nc.sbuf_tensor("p6sc", [128, 1024], F32, **R))
        p6a = pes.enter_context(nc.sbuf_tensor("p6a", [128, T], F32, **R))
        p6b = pes.enter_context(nc.sbuf_tensor("p6b", [128, T], F32, **R))
        xn1tm = pes.enter_context(nc.sbuf_tensor("xn1tm", [128, T, 1024], BF16, **R))
        xn1lo = pes.enter_context(nc.sbuf_tensor("xn1lo", [128, T, 1024], BF16, **R))
        with pes:
            b.dma("sync", x1all[:], x1_d.rearrange("(t p) d -> p t d", p=128))
            b.step("vector", lambda v: v.tensor_reduce(p6a[:], x1all[:], AX.X, ALU.add))
            b.step("vector", lambda v: v.tensor_scalar(p6b[:], p6a[:], 1.0 / 1024, None, ALU.mult))
            b.step("vector", lambda v: v.scalar_tensor_tensor(
                x1all[:], p6b[:].unsqueeze(2).broadcast_to([128, T, 1024]), -1.0,
                x1all[:], ALU.mult, ALU.add))
            for t in range(T):
                b.step("scalar", lambda s, t=t: s.activation(
                    p6sc[:], x1all[:, t, :], AF.Square, accum_out=p6a[:, t:t + 1]))
            b.step("vector", lambda v: v.tensor_scalar(
                p6a[:], p6a[:], 1.0 / 1024, 1e-5, ALU.mult, ALU.add))
            b.step("scalar", lambda s: s.activation(p6b[:], p6a[:], AF.Sqrt))
            b.step("vector", lambda v: v.reciprocal(p6a[:], p6b[:]))
            b.step("vector", lambda v: v.tensor_tensor(
                x1all[:], x1all[:], p6a[:].unsqueeze(2).broadcast_to([128, T, 1024]),
                ALU.mult))
            b.step("vector", lambda v: v.tensor_copy(xn1tm[:], x1all[:]))
            b.step("vector", lambda v: v.tensor_tensor(
                xn1lo[:], x1all[:], xn1tm[:], ALU.subtract))
            b.dma("sync", xn1_tm_d.rearrange("(t p) d -> p t d", p=128), xn1tm[:])
            b.dma("sync", xn1lo_d.rearrange("(t p) d -> p t d", p=128), xn1lo[:])
            for c in range(DCH):
                b.dma("sync", xn1_fm[:, c, :], xn1_tm_d[:, 128 * c:128 * (c + 1)],
                      transpose=True)
            if "dbg_xn1" in dbg_outs:
                b.dma("sync", dbg_outs["dbg_xn1"], xn1_tm_d)
        if upto < 7:
            _close_lstacks(); b.emit(block); return nc

        es_ff = ExitStack(); _lstacks.append(es_ff)
        xn1_ff = es_ff.enter_context(nc.sbuf_tensor("xn1_ff", [128, DCH, 2048], F32, **R))
        pes = ExitStack()
        wt1 = pes.enter_context(nc.sbuf_tensor("wt1", [128, DCH, 192], F32, **R))
        qfm1 = pes.enter_context(nc.sbuf_tensor("qfm1", [64, 2048], F32, **R))
        kfm1 = pes.enter_context(nc.sbuf_tensor("kfm1", [64, 2048], F32, **R))
        vtm1 = pes.enter_context(nc.sbuf_tensor("vtm1", [128, T, 64], F32, **R))
        st1 = pes.enter_context(nc.sbuf_tensor("st1", [128, 2048], F32, **R))
        st1e = pes.enter_context(nc.sbuf_tensor("st1e", [128, 2048], F32, **R))
        pbt1 = pes.enter_context(nc.sbuf_tensor("pbt1", [128, 2048], F32, **R))
        m7 = pes.enter_context(nc.sbuf_tensor("m7", [128, 1], F32, **R))
        l7 = pes.enter_context(nc.sbuf_tensor("l7", [128, 1], F32, **R))
        ps7 = pes.enter_context(nc.psum_tensor("ps7", [128, 512], F32))
        ps7q = pes.enter_context(nc.psum_tensor("ps7q", [64, 512], F32))
        ps7av = pes.enter_context(nc.psum_tensor("ps7av", [128, 1024], F32))
        with pes:
            les = ExitStack()
            lo_fm = les.enter_context(nc.sbuf_tensor("lo_fm", [128, DCH, 2048], BF16, **R))
            with les:
                for c in range(DCH):
                    b.dma("sync", lo_fm[:, c, :], xn1lo_d[:, 128 * c:128 * (c + 1)],
                          transpose=True)
                b.step("vector", lambda v: v.tensor_tensor(
                    xn1_ff[:], xn1_fm[:], lo_fm[:], ALU.add))
            b.dma("sync", wt1[:], l1t_in.rearrange("(c p) f -> p c f", p=128))
            for tb in range(4):
                for (dst, woff) in ((qfm1, 0), (kfm1, 64)):
                    def mm_qk1(te, tb=tb, woff=woff):
                        for c in range(DCH):
                            ins = te.matmul(ps7q[:], wt1[:, c, woff:woff + 64],
                                            xn1_ff[:, c, 512 * tb:512 * (tb + 1)],
                                            start=(c == 0), stop=(c == DCH - 1))
                        return ins
                    b.step("tensor", mm_qk1)
                    b.step("scalar", lambda s, dst=dst, tb=tb: s.activation(
                        dst[0:64, 512 * tb:512 * (tb + 1)], ps7q[:], AF.Copy))
            for t in range(T):
                def mm_v1(te, t=t):
                    for c in range(DCH):
                        ins = te.matmul(ps7[:, 0:64],
                                        xn1_ff[:, c, 128 * t:128 * (t + 1)],
                                        wt1[:, c, 128:192],
                                        start=(c == 0), stop=(c == DCH - 1))
                    return ins
                b.step("tensor", mm_v1)
                b.step("scalar", lambda s, t=t: s.activation(vtm1[:, t, :], ps7[:, 0:64], AF.Copy))
            for kt in range(T):
                for ib in range(4):
                    b.step("tensor", lambda te, kt=kt, ib=ib: te.matmul(
                        ps7[:], kfm1[0:64, 128 * kt:128 * (kt + 1)],
                        qfm1[0:64, 512 * ib:512 * (ib + 1)], start=True, stop=True))
                    b.step("scalar", lambda s, ib=ib: s.activation(
                        st1[:, 512 * ib:512 * (ib + 1)], ps7[:], AF.Copy))
                b.step("vector", lambda v: v.tensor_reduce(
                    m7[:], st1[:], AX.X, ALU.max, negate=True))
                b.step("scalar", lambda s: s.activation(
                    st1e[:], st1[:], AF.Exp, bias=m7[:], accum_out=l7[:]))
                b.step("vector", lambda v: v.reciprocal(l7[:], l7[:]))
                b.step("vector", lambda v: v.tensor_scalar(
                    pbt1[:], st1e[:], l7[:], None, ALU.mult))
                if kt == 0 and "dbg_pbt" in dbg_outs:
                    b.dma("sync", dbg_outs["dbg_pbt"], pbt1[:])
                def mm_av1(te, kt=kt):
                    for ib in range(T):
                        ins = te.matmul(ps7av[:, 64 * ib:64 * (ib + 1)],
                                        pbt1[:, 128 * ib:128 * (ib + 1)], vtm1[:, kt, :],
                                        start=(kt == 0 and ib % 8 == 0),
                                        stop=(kt == T - 1))
                    return ins
                b.step("tensor", mm_av1)
            for ti2 in range(2):
                b.step("scalar", lambda s, ti2=ti2: s.activation(
                    s1_sb[:].rearrange("p t v -> p (t v)")[:, 512 * ti2:512 * (ti2 + 1)],
                    ps7av[:, 512 * ti2:512 * (ti2 + 1)], AF.Copy))
            if "dbg_v1" in dbg_outs:
                b.dma("sync", dbg_outs["dbg_v1"].rearrange("(t p) v -> p t v", p=128), vtm1[:])
            if "dbg_s1" in dbg_outs:
                b.dma("sync", dbg_outs["dbg_s1"].rearrange("(t p) v -> p t v", p=128), s1_sb[:])
        if upto < 8:
            _close_lstacks(); b.emit(block); return nc

        pes = ExitStack()
        gwT = pes.enter_context(nc.sbuf_tensor("gwT", [128, DCH, 8], F32, **R))
        gbrep = pes.enter_context(nc.sbuf_tensor("gbrep", [128, 8], F32, **R))
        gbrow = pes.enter_context(nc.sbuf_tensor("gbrow", [1, 8], F32, **R))
        r8a = pes.enter_context(nc.sbuf_tensor("r8a", [128, 8], F32, **R))
        r8b = pes.enter_context(nc.sbuf_tensor("r8b", [128, 8], F32, **R))
        r8c = pes.enter_context(nc.sbuf_tensor("r8c", [128, 8], F32, **R))
        m8 = pes.enter_context(nc.sbuf_tensor("m8", [128, 1], F32, **R))
        m8b = pes.enter_context(nc.sbuf_tensor("m8b", [128, 1], F32, **R))
        ps8 = pes.enter_context(nc.psum_tensor("ps8", [128, 8], F32))
        ps8t = pes.enter_context(nc.psum_tensor("ps8t", [8, 128], F32))
        with pes:
            b.dma("sync", gwT[:], gwt_in.rearrange("(c p) f -> p c f", p=128))
            b.dma("sync", gbrow[:], gb_in.unsqueeze(0))
            b.step("tensor", lambda te: te.matmul(
                ps8[:], ones_f[0:1, 0:128], gbrow[:], start=True, stop=True))
            b.step("scalar", lambda s: s.activation(gbrep[:], ps8[:], AF.Copy))
            for t in range(T):
                def mm_g(te, t=t):
                    for c in range(DCH):
                        ins = te.matmul(ps8[:], xn1_ff[:, c, 128 * t:128 * (t + 1)],
                                        gwT[:, c, :], start=(c == 0), stop=(c == DCH - 1))
                    return ins
                b.step("tensor", mm_g)
                b.step("scalar", lambda s: s.activation(r8a[:], ps8[:], AF.Copy))
                b.step("vector", lambda v: v.tensor_reduce(
                    m8[:], r8a[:], AX.X, ALU.max, negate=True))
                b.step("scalar", lambda s: s.activation(
                    r8b[:], r8a[:], AF.Exp, bias=m8[:], accum_out=m8b[:]))
                b.step("vector", lambda v: v.reciprocal(m8[:], m8b[:]))
                b.step("vector", lambda v, t=t: v.tensor_scalar(
                    cw_sb[:, t, :], r8b[:], m8[:], None, ALU.mult))
                b.step("vector", lambda v, t=t: v.tensor_tensor(
                    r8a[:], cw_sb[:, t, :], gbrep[:], ALU.add))
                b.step("vector", lambda v: v.tensor_reduce(
                    m8[:], r8a[:], AX.X, ALU.max, negate=True))
                b.step("vector", lambda v: v.tensor_scalar(
                    r8b[:], r8a[:], m8[:], 0.0, ALU.add, ALU.is_ge))
                b.step("vector", lambda v: v.scalar_tensor_tensor(
                    r8c[:], r8b[:], NEG, r8a[:], ALU.mult, ALU.add))
                b.step("vector", lambda v: v.tensor_reduce(
                    m8[:], r8c[:], AX.X, ALU.max, negate=True))
                b.step("vector", lambda v: v.tensor_scalar(
                    r8b[:], r8a[:], m8[:], 0.0, ALU.add, ALU.is_ge))
                b.step("vector", lambda v, t=t: v.tensor_tensor(
                    cw_sb[:, t, :], cw_sb[:, t, :], r8b[:], ALU.mult))
                b.step("tensor", lambda te, t=t: te.transpose(
                    ps8t[:], cw_sb[:, t, :], ident_f[:]))
                b.step("scalar", lambda s, t=t: s.activation(
                    cwT_sb[:, 128 * t:128 * (t + 1)], ps8t[:], AF.Copy))
            if "dbg_cw" in dbg_outs:
                b.dma("sync", dbg_outs["dbg_cw"].rearrange("(t p) e -> p t e", p=128), cw_sb[:])
        es_ff.close()
        es_l1.close()
        if upto < 9:
            _close_lstacks(); b.emit(block); return nc

        pes = ExitStack()
        Aw = pes.enter_context(nc.sbuf_tensor("Aw", [128, DCH, 2048], BF16, **R))
        Bw = pes.enter_context(nc.sbuf_tensor("Bw", [128, DCH, 2048], BF16, **R))
        w2b = pes.enter_context(nc.sbuf_tensor("w2b", [128, DCH, 1024], BF16, **R))
        acc = pes.enter_context(nc.sbuf_tensor("acc", [128, 8, 1024], F32, **R))
        xn1h = pes.enter_context(nc.sbuf_tensor("xn1h", [128, DCH, 1024], BF16, **R))
        a_tb = pes.enter_context(nc.sbuf_tensor("a_tb", [128, DCH, 512], BF16, **R))
        cwrep = pes.enter_context(nc.sbuf_tensor("cwrep", [128, 1024], F32, **R))
        u_sc = [[pes.enter_context(nc.sbuf_tensor(f"u{p}_{j}", [128, 512], F32, **R))
                 for j in range(4)] for p in range(2)]
        gG = [pes.enter_context(nc.sbuf_tensor(f"gG{p}", [128, 512], F32, **R)) for p in range(2)]
        gL = [pes.enter_context(nc.sbuf_tensor(f"gL{p}", [128, 512], F32, **R)) for p in range(2)]
        b2sb = pes.enter_context(nc.sbuf_tensor("b2sb", [8, 1024], F32, **R))
        sb2r = pes.enter_context(nc.sbuf_tensor("sb2r", [1, 1024], F32, **R))
        g9 = pes.enter_context(nc.sbuf_tensor("g9", [128, 1024], F32, **R))
        x9 = pes.enter_context(nc.sbuf_tensor("x9", [128, 1024], F32, **R))
        hp = [[pes.enter_context(nc.psum_tensor(f"hp{p}_{j}", [128, 512], F32))
               for j in range(4)] for p in range(2)]
        ps9y = hp[0][0]
        ps9c = hp[1][0]
        with pes:
            p9_entry = b.last
            b.dma("sync", b2sb[:], b2_in)
            b.dma("sync", sb2r[:], sb2_in.unsqueeze(0))
            ch_dwh = b.ch("dwh")
            ch_dw2 = b.ch("dw2")
            ev_h_free = b.last
            ev_w2_free = b.last
            ev_tb_done = b.last
            for th in range(2):
                for c in range(DCH):
                    b.dma("sync", xn1h[:, c, :],
                          xn1_tm_d[1024 * th:1024 * (th + 1), 128 * c:128 * (c + 1)],
                          transpose=True)
                ev_xn1h = b.last
                for e in range(9):
                    shared = (e == 8)
                    if shared:
                        srcs = (sw1t_bf, sw3t_bf, sw2t_bf)
                        evs = (cast_ev["sw1"], cast_ev["sw3"], cast_ev["sw2"])
                    else:
                        srcs = (w1t_bf[e], w3t_bf[e], w2t_bf[e])
                        evs = (cast_ev[f"w1_{e}"], cast_ev[f"w3_{e}"], cast_ev[f"w2_{e}"])
                    b.dma("gpsimd", Aw[:], srcs[0].rearrange("(c p) f -> p c f", p=128),
                          ch=ch_dwh, waits=(evs[0], ev_h_free, p9_entry),
                          seq=False, set_last=False)
                    ev_ld_h = b.dma("gpsimd", Bw[:],
                                    srcs[1].rearrange("(c p) f -> p c f", p=128),
                                    ch=ch_dwh, waits=(evs[1],), seq=False, set_last=False)
                    ev_ld_w2 = b.dma("gpsimd", w2b[:],
                                     srcs[2].rearrange("(c p) f -> p c f", p=128),
                                     ch=ch_dw2, waits=(evs[2], ev_w2_free, p9_entry),
                                     seq=False, set_last=False)
                    for tb in range(2):
                        tsl = slice(512 * tb, 512 * (tb + 1))
                        ev_start = ev_tb_done
                        if not shared:
                            ev_c = b.step("tensor", lambda te, e=e, th=th, tb=tb: te.matmul(
                                ps9c[:], sel8[:, 128 * e:128 * (e + 1)],
                                cwT_sb[:, 1024 * th + 512 * tb:1024 * th + 512 * (tb + 1)],
                                start=True, stop=True),
                                waits=(ev_start,), seq=False, set_last=False)
                            ev_cw = b.step("scalar", lambda s, tb=tb: s.activation(
                                cwrep[:, 0:512], ps9c[:], AF.Copy),
                                waits=(ev_c,), seq=False, set_last=False)
                        ev_pe = {}
                        ev_evac = {}
                        ev_v1 = {}
                        ev_v2 = {}
                        for fc in range(DCH):
                            p = fc % 2
                            pw = [ev_xn1h, (ch_dwh, ch_dwh.n), ev_start]
                            if fc - 2 in ev_evac:
                                pw.append(ev_evac[fc - 2])
                            if fc == 1 and not shared:
                                pw.append(ev_cw)
                            def mm_h4(te, fc=fc, p=p, tsl=tsl):
                                ins = None
                                for (j, woff, wbuf) in ((0, 0, Aw), (1, 0, Bw),
                                                        (2, 1024, Aw), (3, 1024, Bw)):
                                    for c in range(DCH):
                                        ins = te.matmul(
                                            hp[p][j][:],
                                            wbuf[:, c, woff + 128 * fc:woff + 128 * (fc + 1)],
                                            xn1h[:, c, tsl], start=(c == 0),
                                            stop=(c == DCH - 1))
                                return ins
                            ev_pe[fc] = b.step("tensor", mm_h4, waits=tuple(pw),
                                               seq=False, set_last=False)
                            ei = 8 if shared else e
                            ew_ = [ev_pe[fc], ev_bias]
                            if fc - 2 in ev_v2:
                                ew_.append(ev_v2[fc - 2])
                            def ev4(s, fc=fc, p=p, ei=ei):
                                s.activation(u_sc[p][0][:], hp[p][0][:], AF.Identity,
                                             bias=b1cols[:, ei, fc:fc + 1])
                                s.activation(u_sc[p][1][:], hp[p][1][:], AF.Identity,
                                             bias=b3cols[:, ei, fc:fc + 1])
                                s.activation(u_sc[p][2][:], hp[p][2][:], AF.Identity,
                                             bias=b1cols[:, ei, 8 + fc:9 + fc])
                                return s.activation(u_sc[p][3][:], hp[p][3][:], AF.Identity,
                                                    bias=b3cols[:, ei, 8 + fc:9 + fc])
                            ev_evac[fc] = b.step("scalar", ev4, waits=tuple(ew_),
                                                 seq=False, set_last=False)
                            def v1(v, p=p):
                                v.tensor_tensor(gG[p][:], u_sc[p][0][:], u_sc[p][1][:], ALU.mult)
                                return v.tensor_tensor(gL[p][:], u_sc[p][2][:], u_sc[p][3][:], ALU.mult)
                            ev_v1[fc] = b.step("vector", v1, waits=(ev_evac[fc],),
                                               seq=False, set_last=False)
                            def p1(g, p=p):
                                g.tensor_scalar(gG[p][:], gG[p][:], 7.0, None, ALU.min)
                                return g.tensor_scalar(gL[p][:], gL[p][:], 7.0, -7.0,
                                                       ALU.min, ALU.max)
                            ev_p1 = b.step("gpsimd", p1, waits=(ev_v1[fc],),
                                           seq=False, set_last=False)
                            ev_a1 = b.step("scalar", lambda s, p=p: s.activation(
                                u_sc[p][1][:], gG[p][:], AF.Gelu_apprx_sigmoid),
                                waits=(ev_p1,), seq=False, set_last=False)
                            vw = [ev_a1]
                            if not shared:
                                vw.append(ev_cw)
                            def v2(v, p=p, fc=fc, tsl=tsl, shared=shared):
                                if shared:
                                    return v.scalar_tensor_tensor(
                                        a_tb[:, fc, :], gL[p][:], 1.0, u_sc[p][1][:],
                                        ALU.add, ALU.mult)
                                v.scalar_tensor_tensor(
                                    u_sc[p][0][:], gL[p][:], 1.0, u_sc[p][1][:],
                                    ALU.add, ALU.mult)
                                return v.tensor_tensor(a_tb[:, fc, :], u_sc[p][0][:],
                                                       cwrep[:, 0:512], ALU.mult)
                            ev_v2[fc] = b.step("vector", v2, waits=tuple(vw),
                                               seq=False, set_last=False)
                        ev_h_free = ev_pe[DCH - 1]
                        ev_prev_y = (ev_evac[DCH - 2], ev_v2[DCH - 1],
                                     (ch_dw2, ch_dw2.n))
                        for tsub in range(4):
                            tt = 8 * th + 4 * tb + tsub
                            ta = 4 * tb + tsub
                            for dh in range(2):
                                def mm_y9(te, tsub=tsub, dh=dh, e=e, tt=tt):
                                    ins = None
                                    if e == 0:
                                        ins = te.matmul(ps9y[:], cwT_sb[:, 128 * tt:128 * (tt + 1)],
                                                        b2sb[:, 512 * dh:512 * (dh + 1)],
                                                        start=True, stop=False)
                                    if e == 8:
                                        ins = te.matmul(ps9y[:], ones_f[0:1, 0:128],
                                                        sb2r[0:1, 512 * dh:512 * (dh + 1)],
                                                        start=True, stop=False)
                                    for fc in range(DCH):
                                        ins = te.matmul(
                                            ps9y[:], a_tb[:, fc, 128 * tsub:128 * (tsub + 1)],
                                            w2b[:, fc, 512 * dh:512 * (dh + 1)],
                                            start=(fc == 0 and e not in (0, 8)),
                                            stop=(fc == DCH - 1))
                                    return ins
                                ev_y = b.step("tensor", mm_y9, waits=ev_prev_y,
                                              seq=False, set_last=False)
                                if e == 0:
                                    ev_ac = b.step("scalar", lambda s, ta=ta, dh=dh: s.activation(
                                        acc[:, ta, 512 * dh:512 * (dh + 1)], ps9y[:], AF.Copy),
                                        waits=(ev_y,), seq=False, set_last=False)
                                else:
                                    ev_ac = b.step("vector", lambda v, ta=ta, dh=dh: v.tensor_tensor(
                                        acc[:, ta, 512 * dh:512 * (dh + 1)], ps9y[:],
                                        acc[:, ta, 512 * dh:512 * (dh + 1)], ALU.add),
                                        waits=(ev_y,), seq=False, set_last=False)
                                ev_prev_y = (ev_ac,)
                        ev_w2_free = ev_y
                        ev_tb_done = ev_ac
                b.last = ev_tb_done
                for ta in range(8):
                    t = 8 * th + ta
                    b.step("scalar", lambda s, ta=ta: s.activation(g9[:], acc[:, ta, :], AF.Silu))
                    b.dma("sync", x9[:], x1_d[128 * t:128 * (t + 1), :])
                    b.step("vector", lambda v, t=t: v.tensor_tensor(
                        g9[:].rearrange("p (r v) -> p r v", v=64),
                        g9[:].rearrange("p (r v) -> p r v", v=64),
                        s1_sb[:, t, :].unsqueeze(1).broadcast_to([128, 16, 64]), ALU.mult))
                    b.step("vector", lambda v: v.tensor_tensor(x9[:], x9[:], g9[:], ALU.add))
                    b.dma("sync", out_d[128 * t:128 * (t + 1), :], x9[:])
                ev_tb_done = b.last
        fin_s, fin_v = b.last[0].sem, b.last[1]
        b.ops["gpsimd"].append(lambda E, s=fin_s, v=fin_v: E.wait_ge(s, v))
        _close_lstacks()
        b.emit(block)
    return nc


def prep_inputs(inputs):
    f = lambda v: np.ascontiguousarray(np.asarray(v, dtype=np.float32))
    i = {k: np.asarray(v) for k, v in inputs.items()}
    w1 = i["L1_ffn_w1"]; w3 = i["L1_ffn_w3"]
    w1sep = np.concatenate([w1[:, 0::2, :], w1[:, 1::2, :]], axis=1)
    w3sep = np.concatenate([w3[:, 0::2, :], w3[:, 1::2, :]], axis=1)
    sw1 = i["L1_ffn_sw1"]; sw3 = i["L1_ffn_sw3"]
    sw1sep = np.concatenate([sw1[0::2, :], sw1[1::2, :]], axis=0)
    sw3sep = np.concatenate([sw3[0::2, :], sw3[1::2, :]], axis=0)

    def cols(bvec):
        return f(bvec.reshape(8, 128, 2).transpose(1, 2, 0).reshape(128, 16))

    shared = {
        "wqt": f(i["L0_attn_Wq"].T), "wkt": f(i["L0_attn_Wk"].T),
        "wvt": f(i["L0_attn_Wv"].T), "wot": f(i["L0_attn_Wo"].T),
        "phiwt": f(i["L0_ffn_phi_w"].T), "phi_b": f(i["L0_ffn_phi_b"]),
        "l1t": f(np.concatenate([i["L1_attn_Wq"].T, i["L1_attn_Wk"].T,
                                 i["L1_attn_Wv"].T], axis=1)),
        "gwt": f(i["L1_ffn_gate_w"].T), "gate_b": f(i["L1_ffn_gate_b"]),
        "w1t": f(w1sep.transpose(0, 2, 1)), "w3t": f(w3sep.transpose(0, 2, 1)),
        "w2t": f(i["L1_ffn_w2"].transpose(0, 2, 1)),
        "sw1t": f(sw1sep.T), "sw3t": f(sw3sep.T), "sw2t": f(i["L1_ffn_sw2"].T),
        "b1c": np.stack([cols(i["L1_ffn_b1"][e]) for e in range(8)]),
        "b3c": np.stack([cols(i["L1_ffn_b3"][e]) for e in range(8)]),
        "sb1c": cols(i["L1_ffn_sb1"]), "sb3c": cols(i["L1_ffn_sb3"]),
        "b2": f(i["L1_ffn_b2"]), "sb2": f(i["L1_ffn_sb2"]),
    }
    per_core = []
    for c in range(8):
        m = dict(shared)
        m["x"] = f(i["L0_attn_in_features"][c])
        m["ew_sl"] = f(i["L0_ffn_expert_w"][8 * c:8 * (c + 1)])
        m["eb_sl"] = f(i["L0_ffn_expert_b"][8 * c:8 * (c + 1)])
        per_core.append(m)
    return per_core


def kernel(**inputs):
    nc = build_nc()
    in_maps = prep_inputs(inputs)
    res = run_bass_kernel_spmd(nc, in_maps, core_ids=list(range(8)))
    return np.stack([np.asarray(r["out"], dtype=np.float32) for r in res.results])


# version 03
# speedup vs baseline: 1.4900x; 1.0814x over previous
import numpy as np
from contextlib import ExitStack
import concourse.bass as bass
import concourse.mybir as mybir
from concourse.bass_utils import run_bass_kernel_spmd

F32 = mybir.dt.float32
BF16 = mybir.dt.bfloat16
AF = mybir.ActivationFunctionType
ALU = mybir.AluOpType
AX = mybir.AxisListType

ENG = ("tensor", "vector", "scalar", "gpsimd", "sync")
NEG = -1.0e30


class Chan:
    def __init__(self, nc, es, name):
        self.sem = es.enter_context(nc.semaphore(name))
        self.n = 0


class B:

    def __init__(self, nc, es):
        self.nc, self.es = nc, es
        self.ops = {e: [] for e in ENG}
        self.chans = {}
        self.last = None

    def ch(self, name):
        if name not in self.chans:
            self.chans[name] = Chan(self.nc, self.es, name)
        return self.chans[name]

    def step(self, eng, fn, ch=None, inc=1, waits=(), seq=True, set_last=True):
        ws = [(c, v) for (c, v) in waits if c is not None and v > 0]
        if seq and self.last is not None:
            ws.append(self.last)
        if ch is None:
            ch = self.ch("c_" + eng)
        ch.n += inc
        val = ch.n
        ws_f = tuple((c.sem, v) for (c, v) in ws)

        def closure(E, fn=fn, ws_f=ws_f, sem=ch.sem, inc=inc):
            for (s, v) in ws_f:
                E.wait_ge(s, v)
            ins = fn(E)
            ins.then_inc(sem, inc)

        self.ops[eng].append(closure)
        if set_last:
            self.last = (ch, val)
        return (ch, val)

    def dma(self, q, out, in_, waits=(), seq=True, set_last=True, ch=None, **kw):
        if ch is None:
            ch = self.ch("d_" + q)
        return self.step(q, lambda E: E.dma_start(out=out, in_=in_, **kw),
                         ch=ch, inc=16, waits=waits, seq=seq, set_last=set_last)

    def emit(self, block):
        for eng in ENG:
            fns = self.ops[eng]
            getattr(block, eng)(lambda E, fns=fns: [fn(E) for fn in fns])


def build_nc(upto=99, dbg=()):
    nc = bass.Bass("TRN2", target_bir_lowering=False)
    T = 16
    DCH = 8

    EI = lambda n, s, dt=F32: nc.dram_tensor(n, s, dt, kind="ExternalInput").ap()
    EO = lambda n, s, dt=F32: nc.dram_tensor(n, s, dt, kind="ExternalOutput").ap()
    IN = lambda n, s, dt: nc.dram_tensor(n, s, dt, kind="Internal").ap()

    x_in = EI("x", [2048, 1024])
    wqt_in, wkt_in, wvt_in, wot_in = (EI(n, [1024, 1024]) for n in ["wqt", "wkt", "wvt", "wot"])
    phiwt_in = EI("phiwt", [1024, 2048]); phib_in = EI("phi_b", [2048])
    ew_in = EI("ew_sl", [8, 1024, 1024]); eb_in = EI("eb_sl", [8, 1024])
    l1t_in = EI("l1t", [1024, 192])
    gwt_in = EI("gwt", [1024, 8]); gb_in = EI("gate_b", [8])
    w1t_in = EI("w1t", [8, 1024, 2048]); w3t_in = EI("w3t", [8, 1024, 2048])
    w2t_in = EI("w2t", [8, 1024, 1024])
    sw1t_in = EI("sw1t", [1024, 2048]); sw3t_in = EI("sw3t", [1024, 2048])
    sw2t_in = EI("sw2t", [1024, 1024])
    b1c_in = EI("b1c", [8, 128, 16]); b3c_in = EI("b3c", [8, 128, 16])
    sb1c_in = EI("sb1c", [128, 16]); sb3c_in = EI("sb3c", [128, 16])
    b2_in = EI("b2", [8, 1024]); sb2_in = EI("sb2", [1024])

    out_d = EO("out", [2048, 1024])

    wqt_bf, wkt_bf, wvt_bf, wot_bf = (IN(n, [1024, 1024], BF16)
                                      for n in ["wqt_bf", "wkt_bf", "wvt_bf", "wot_bf"])
    phiwt_bf = IN("phiwt_bf", [1024, 2048], BF16)
    l1t_bf = IN("l1t_bf", [1024, 192], BF16)
    gwt_bf = IN("gwt_bf", [1024, 8], BF16)
    w1t_bf = IN("w1t_bf", [8, 1024, 2048], BF16)
    w3t_bf = IN("w3t_bf", [8, 1024, 2048], BF16)
    w2t_bf = IN("w2t_bf", [8, 1024, 1024], BF16)
    sw1t_bf = IN("sw1t_bf", [1024, 2048], BF16)
    sw3t_bf = IN("sw3t_bf", [1024, 2048], BF16)
    sw2t_bf = IN("sw2t_bf", [1024, 1024], BF16)

    xn_tm_d = IN("xn_tm_d", [2048, 1024], BF16)
    ctm_d = IN("ctm_d", [2048, 2048], BF16)
    xt_part = IN("xt_part", [2048, 1024], F32)
    xt_red = IN("xt_red", [256, 1024], F32)
    s_d = IN("s_d", [2048, 1024], F32)
    xtt_d = IN("xtt_d", [256, 1024], BF16)
    ytl_d = IN("ytl_d", [256, 1024], BF16)
    yt_all = IN("yt_all", [2048, 1024], BF16)
    x1_d = IN("x1_d", [2048, 1024], F32)
    xn1_tm_d = IN("xn1_tm_d", [2048, 1024], BF16)
    xn1lo_d = IN("xn1lo_d", [2048, 1024], BF16)

    dbg_outs = {}
    for name, shape, dt in [
        ("dbg_xn", [2048, 1024], BF16), ("dbg_D", [2048, 2048], BF16),
        ("dbg_C", [2048, 2048], BF16), ("dbg_xtpart", [2048, 1024], F32),
        ("dbg_xtred", [256, 1024], F32), ("dbg_s", [2048, 1024], F32),
        ("dbg_ytall", [2048, 1024], BF16), ("dbg_x1", [2048, 1024], F32),
        ("dbg_xtt", [256, 1024], BF16), ("dbg_ytl", [256, 1024], BF16),
        ("dbg_xttfm", [128, 2048], BF16), ("dbg_ewb", [128, 16384], BF16),
        ("dbg_xn1", [2048, 1024], BF16), ("dbg_s1", [2048, 64], F32),
        ("dbg_pbt", [128, 2048], F32), ("dbg_v1", [2048, 64], F32),
        ("dbg_cw", [2048, 8], F32),
    ]:
        if name in dbg:
            dbg_outs[name] = EO(name, shape, dt)

    es = ExitStack()
    b = B(nc, es)
    sbL = lambda *a: es.enter_context(nc.sbuf_tensor(*a))
    R = {"side": "right"}

    with es, nc.Block() as block:
        ident_bf = sbL("ident_bf", [128, 128], BF16)
        ident_f = sbL("ident_f", [128, 128], F32)
        maskneg = sbL("maskneg", [128, 128], F32)
        ones_f = sbL("ones_f", [1, 512], F32)
        sel8 = sbL("sel8", [8, 8 * 128], F32)
        s1_sb = sbL("s1_sb", [128, T, 64], F32)
        cw_sb = sbL("cw_sb", [128, T, 8], F32)
        cwT_sb = sbL("cwT_sb", [8, 2048], F32)
        b1cols = sbL("b1cols", [128, 9, 16], F32)
        b3cols = sbL("b3cols", [128, 9, 16], F32)

        def _consts(g):
            g.memset(ident_bf[:], 1.0)
            g.affine_select(ident_bf[:], ident_bf[:], [[-1, 128]], ALU.is_equal,
                            0.0, base=0, channel_multiplier=1)
            g.memset(ident_f[:], 1.0)
            g.affine_select(ident_f[:], ident_f[:], [[-1, 128]], ALU.is_equal,
                            0.0, base=0, channel_multiplier=1)
            g.memset(maskneg[:], 0.0)
            g.affine_select(maskneg[:], maskneg[:], [[-1, 128]], ALU.is_ge,
                            NEG, base=0, channel_multiplier=1)
            g.memset(ones_f[:], 1.0)
            g.memset(sel8[:], 1.0)
            ins = None
            for e in range(8):
                ins = g.affine_select(sel8[:, e * 128:(e + 1) * 128],
                                      sel8[:, e * 128:(e + 1) * 128], [[0, 128]],
                                      ALU.is_equal, 0.0, base=-e, channel_multiplier=1)
            return ins
        b.step("gpsimd", _consts)

        wes = ExitStack()
        ps_warm = wes.enter_context(nc.psum_tensor("ps_warm", [128, 128], F32))
        with wes:
            b.step("tensor", lambda te: te.matmul(
                ps_warm[:], ident_bf[:], ident_bf[:], start=True, stop=True))

        for e in range(8):
            b.dma("scalar", b1cols[:, e, :], b1c_in[e], seq=False, set_last=False)
            b.dma("scalar", b3cols[:, e, :], b3c_in[e], seq=False, set_last=False)
        b.dma("scalar", b1cols[:, 8, :], sb1c_in, seq=False, set_last=False)
        ev_bias = b.dma("scalar", b3cols[:, 8, :], sb3c_in, seq=False, set_last=False)

        cast_ev = {}

        def cast(name, dst, src, grp):
            ch = b.ch(grp)
            b.dma("gpsimd", dst, src, ch=ch, seq=False, set_last=False)
            cast_ev[name] = (ch, None)

        cast("phiwt", phiwt_bf, phiwt_in, "dc0")
        cast("wqt", wqt_bf, wqt_in, "dc0"); cast("wkt", wkt_bf, wkt_in, "dc0")
        cast("wvt", wvt_bf, wvt_in, "dc0"); cast("wot", wot_bf, wot_in, "dc0")
        ch_g1 = b.ch("g1")
        b.chans["dc2"] = Chan(nc, es, "dc2")
        b.ops["gpsimd"].append(lambda E, s=ch_g1.sem: E.wait_ge(s, 1))
        for e in range(8):
            cast(f"w1_{e}", w1t_bf[e], w1t_in[e], "dc2")
            cast(f"w3_{e}", w3t_bf[e], w3t_in[e], "dc2")
            cast(f"w2_{e}", w2t_bf[e], w2t_in[e], "dc2")
        cast("sw1", sw1t_bf, sw1t_in, "dc1"); cast("sw3", sw3t_bf, sw3t_in, "dc1")
        cast("sw2", sw2t_bf, sw2t_in, "dc1")
        cast_ev = {k: (ch, ch.n) for k, (ch, _) in cast_ev.items()}

        def load_wT(buf, src_bf, ev, cols, q="sync"):
            return b.dma(q, buf[:, :, 0:cols],
                         src_bf.rearrange("(c p) f -> p c f", p=128),
                         waits=(ev,))

        _lstacks = []

        def _close_lstacks():
            for st in reversed(_lstacks):
                st.close()

        es_l0b = ExitStack(); _lstacks.append(es_l0b)
        xn_fm = es_l0b.enter_context(nc.sbuf_tensor("xn_fm", [128, DCH, 2048], BF16))
        es_l0a = ExitStack(); _lstacks.append(es_l0a)
        xn_tm = es_l0a.enter_context(nc.sbuf_tensor("xn_tm", [128, T, 1024], BF16))

        pes = ExitStack()
        xall = pes.enter_context(nc.sbuf_tensor("xall", [128, T, 1024], F32, **R))
        p0sc = pes.enter_context(nc.sbuf_tensor("p0sc", [128, 1024], F32, **R))
        p0st = pes.enter_context(nc.sbuf_tensor("p0st", [128, T], F32, **R))
        p0st2 = pes.enter_context(nc.sbuf_tensor("p0st2", [128, T], F32, **R))
        with pes:
            b.dma("sync", xall[:], x_in.rearrange("(t p) d -> p t d", p=128))
            b.step("vector", lambda v: v.tensor_reduce(p0st[:], xall[:], AX.X, ALU.add))
            b.step("vector", lambda v: v.tensor_scalar(p0st2[:], p0st[:], 1.0 / 1024, None, ALU.mult))
            b.step("vector", lambda v: v.scalar_tensor_tensor(
                xall[:], p0st2[:].unsqueeze(2).broadcast_to([128, T, 1024]), -1.0,
                xall[:], ALU.mult, ALU.add))
            for t in range(T):
                b.step("scalar", lambda s, t=t: s.activation(
                    p0sc[:], xall[:, t, :], AF.Square, accum_out=p0st[:, t:t + 1]))
            b.step("vector", lambda v: v.tensor_scalar(
                p0st[:], p0st[:], 1.0 / 1024, 1e-5, ALU.mult, ALU.add))
            b.step("scalar", lambda s: s.activation(p0st2[:], p0st[:], AF.Sqrt))
            b.step("vector", lambda v: v.reciprocal(p0st[:], p0st2[:]))
            b.step("vector", lambda v: v.tensor_tensor(
                xn_tm[:], xall[:], p0st[:].unsqueeze(2).broadcast_to([128, T, 1024]),
                ALU.mult))
            b.dma("sync", xn_tm_d.rearrange("(t p) d -> p t d", p=128), xn_tm[:])
            for c in range(DCH):
                b.dma("sync", xn_fm[:, c, :], xn_tm_d[:, 128 * c:128 * (c + 1)],
                      transpose=True)
            if "dbg_xn" in dbg_outs:
                b.dma("sync", dbg_outs["dbg_xn"], xn_tm_d)
        if upto < 1:
            _close_lstacks(); b.emit(block); return nc

        D_all = es_l0a.enter_context(nc.sbuf_tensor("D_all", [128, T, 2048], BF16))
        pes = ExitStack()
        phiwT = pes.enter_context(nc.sbuf_tensor("phiwT", [128, DCH, 2048], BF16, **R))
        phib_r = pes.enter_context(nc.sbuf_tensor("phib_r", [1, 2048], F32, **R))
        E_sc = pes.enter_context(nc.sbuf_tensor("E_sc", [128, 2048], F32, **R))
        C_sc = pes.enter_context(nc.sbuf_tensor("C_sc", [128, 2048], BF16, **R))
        sd_sc = pes.enter_context(nc.sbuf_tensor("sd_sc", [128, 64], F32, **R))
        sc_sc = pes.enter_context(nc.sbuf_tensor("sc_sc", [128, 32], F32, **R))
        ps1 = pes.enter_context(nc.psum_tensor("ps1", [128, 512], F32))
        with pes:
            load_wT(phiwT, phiwt_bf, cast_ev["phiwt"], 2048)
            b.dma("sync", phib_r[:], phib_in.unsqueeze(0))
            for t in range(T):
                for ebk in range(4):
                    def mm_logits(te, t=t, ebk=ebk):
                        ins = te.matmul(ps1[:], ones_f[0:1, 0:128],
                                        phib_r[0:1, 512 * ebk:512 * (ebk + 1)],
                                        start=True, stop=False)
                        for c in range(DCH):
                            ins = te.matmul(ps1[:], xn_fm[:, c, 128 * t:128 * (t + 1)],
                                            phiwT[:, c, 512 * ebk:512 * (ebk + 1)],
                                            start=False, stop=(c == DCH - 1))
                        return ins
                    b.step("tensor", mm_logits)
                    b.step("scalar", lambda s, ebk=ebk: s.activation(
                        E_sc[:, 512 * ebk:512 * (ebk + 1)], ps1[:], AF.Exp))
                e3 = E_sc[:].rearrange("p (n q) -> p n q", q=32)
                b.step("vector", lambda v, e3=e3: v.tensor_reduce(sd_sc[:], e3, AX.X, ALU.add))
                b.step("vector", lambda v, e3=e3: v.tensor_reduce(
                    sc_sc[:], e3.transpose([0, 2, 1]), AX.X, ALU.add))
                b.step("vector", lambda v: v.reciprocal(sd_sc[:], sd_sc[:]))
                b.step("vector", lambda v: v.reciprocal(sc_sc[:], sc_sc[:]))
                b.step("vector", lambda v, t=t, e3=e3: v.tensor_tensor(
                    D_all[:, t, :].rearrange("p (n q) -> p n q", q=32), e3,
                    sd_sc[:].unsqueeze(2).broadcast_to([128, 64, 32]), ALU.mult))
                b.step("vector", lambda v, e3=e3: v.tensor_tensor(
                    C_sc[:].rearrange("p (n q) -> p n q", q=32), e3,
                    sc_sc[:].unsqueeze(1).broadcast_to([128, 64, 32]), ALU.mult))
                b.dma("sync", ctm_d[128 * t:128 * (t + 1), :], C_sc[:])
            if "dbg_D" in dbg_outs:
                b.dma("sync", dbg_outs["dbg_D"].rearrange("(t p) e -> p t e", p=128), D_all[:])
            if "dbg_C" in dbg_outs:
                b.dma("sync", dbg_outs["dbg_C"], ctm_d)
        ch_g1.n += 1
        b.step("sync", lambda E: E.nop(), ch=ch_g1, inc=0, set_last=False) if False else None
        b.ops["sync"].append(lambda E, s=ch_g1.sem: E.nop().then_inc(s, 1))
        if upto < 2:
            _close_lstacks(); b.emit(block); return nc

        pes = ExitStack()
        xtst = pes.enter_context(nc.sbuf_tensor("xtst", [128, 1024], F32, **R))
        ps2 = pes.enter_context(nc.psum_tensor("ps2", [128, 512], F32))
        with pes:
            for npc in range(T):
                for dh in range(2):
                    def mm_xt(te, npc=npc, dh=dh):
                        for t in range(T):
                            ins = te.matmul(ps2[:], D_all[:, t, 128 * npc:128 * (npc + 1)],
                                            xn_tm[:, t, 512 * dh:512 * (dh + 1)],
                                            start=(t == 0), stop=(t == T - 1))
                        return ins
                    b.step("tensor", mm_xt)
                    b.step("scalar", lambda s, dh=dh: s.activation(
                        xtst[:, 512 * dh:512 * (dh + 1)], ps2[:], AF.Copy))
                b.dma("sync", xt_part[128 * npc:128 * (npc + 1), :], xtst[:])
            b.step("gpsimd", lambda g: g.collective_compute(
                "ReduceScatter", ALU.add, replica_groups=[list(range(8))],
                ins=[xt_part.opt()], outs=[xt_red.opt()]), ch=b.ch("cc1"))
            if "dbg_xtpart" in dbg_outs:
                b.dma("sync", dbg_outs["dbg_xtpart"], xt_part)
            if "dbg_xtred" in dbg_outs:
                b.dma("sync", dbg_outs["dbg_xtred"], xt_red)
        es_l0a.close()
        if upto < 3:
            _close_lstacks(); b.emit(block); return nc

        pes = ExitStack()
        woT = pes.enter_context(nc.sbuf_tensor("woT", [128, DCH, 1024], BF16, **R))
        Qfm = pes.enter_context(nc.sbuf_tensor("Qfm", [128, DCH, 2048], BF16, **R))
        Kfm = pes.enter_context(nc.sbuf_tensor("Kfm", [128, DCH, 2048], BF16, **R))
        Vtm = pes.enter_context(nc.sbuf_tensor("Vtm", [128, T, 1024], BF16, **R))
        S_sc = pes.enter_context(nc.sbuf_tensor("S_sc", [128, 2048], F32, **R))
        Pb_sc = pes.enter_context(nc.sbuf_tensor("Pb_sc", [128, 2048], BF16, **R))
        PbT = pes.enter_context(nc.sbuf_tensor("PbT", [128, T, 128], BF16, **R))
        avtm = pes.enter_context(nc.sbuf_tensor("avtm", [128, 1024], BF16, **R))
        avfm = pes.enter_context(nc.sbuf_tensor("avfm", [128, DCH, 128], BF16, **R))
        stile = pes.enter_context(nc.sbuf_tensor("stile", [128, 1024], F32, **R))
        m3 = pes.enter_context(nc.sbuf_tensor("m3", [128, 1], F32, **R))
        l3 = pes.enter_context(nc.sbuf_tensor("l3", [128, 1], F32, **R))
        rl3 = pes.enter_context(nc.sbuf_tensor("rl3", [128, 1], F32, **R))
        ps3a = pes.enter_context(nc.psum_tensor("ps3a", [128, 512], F32))
        ps3t = pes.enter_context(nc.psum_tensor("ps3t", [128, 128], BF16))
        ps3av = pes.enter_context(nc.psum_tensor("ps3av", [128, 512], F32))
        ps3s = pes.enter_context(nc.psum_tensor("ps3s", [128, 512], F32))
        with pes:
            load_wT(woT, wot_bf, cast_ev["wot"], 1024)
            qes = ExitStack()
            wqk = qes.enter_context(nc.sbuf_tensor("wqk", [128, DCH, 1024], BF16, **R))
            with qes:
                for (dst, wsrc, wev) in ((Qfm, wqt_bf, "wqt"), (Kfm, wkt_bf, "wkt")):
                    b.dma("sync", wqk[:],
                          wsrc.rearrange("(c p) f -> p c f", p=128),
                          waits=(cast_ev[wev],))
                    for fc in range(DCH):
                        for tb in range(4):
                            def mm_qk(te, fc=fc, tb=tb):
                                for c in range(DCH):
                                    ins = te.matmul(
                                        ps3a[:], wqk[:, c, 128 * fc:128 * (fc + 1)],
                                        xn_fm[:, c, 512 * tb:512 * (tb + 1)],
                                        start=(c == 0), stop=(c == DCH - 1))
                                return ins
                            b.step("tensor", mm_qk)
                            b.step("scalar", lambda s, dst=dst, fc=fc, tb=tb: s.activation(
                                dst[:, fc, 512 * tb:512 * (tb + 1)], ps3a[:], AF.Copy))
            ves = ExitStack()
            wvT = ves.enter_context(nc.sbuf_tensor("wvT", [128, DCH, 1024], BF16, **R))
            with ves:
                load_wT(wvT, wvt_bf, cast_ev["wvt"], 1024)
                for t in range(T):
                    for fh in range(2):
                        def mm_v(te, t=t, fh=fh):
                            for c in range(DCH):
                                ins = te.matmul(ps3a[:], xn_fm[:, c, 128 * t:128 * (t + 1)],
                                                wvT[:, c, 512 * fh:512 * (fh + 1)],
                                                start=(c == 0), stop=(c == DCH - 1))
                            return ins
                        b.step("tensor", mm_v)
                        b.step("scalar", lambda s, t=t, fh=fh: s.activation(
                            Vtm[:, t, 512 * fh:512 * (fh + 1)], ps3a[:], AF.Copy))
            es_l0b.close()
            for qi in range(T):
                klen = 128 * (qi + 1)
                nkb = (klen + 511) // 512
                for kb in range(nkb):
                    kw = min(512, klen - 512 * kb)
                    def mm_s(te, qi=qi, kb=kb, kw=kw):
                        for c in range(DCH):
                            ins = te.matmul(ps3a[:, 0:kw], Qfm[:, c, 128 * qi:128 * (qi + 1)],
                                            Kfm[:, c, 512 * kb:512 * kb + kw],
                                            start=(c == 0), stop=(c == DCH - 1))
                        return ins
                    b.step("tensor", mm_s)
                    b.step("scalar", lambda s, kb=kb, kw=kw: s.activation(
                        S_sc[:, 512 * kb:512 * kb + kw], ps3a[:, 0:kw], AF.Copy))
                b.step("vector", lambda v, klen=klen: v.tensor_tensor(
                    S_sc[:, klen - 128:klen], S_sc[:, klen - 128:klen], maskneg[:],
                    ALU.add))
                b.step("vector", lambda v, klen=klen: v.tensor_reduce(
                    m3[:], S_sc[:, 0:klen], AX.X, ALU.max, negate=True))
                b.step("vector", lambda v: v.tensor_scalar(
                    m3[:], m3[:], 1.0 / 32.0, None, ALU.mult))
                b.step("scalar", lambda s, klen=klen: s.activation(
                    Pb_sc[:, 0:klen], S_sc[:, 0:klen], AF.Exp, bias=m3[:],
                    scale=1.0 / 32.0, accum_out=l3[:]))
                b.step("vector", lambda v: v.reciprocal(rl3[:], l3[:]))
                for kt in range(qi + 1):
                    b.step("tensor", lambda te, kt=kt: te.transpose(
                        ps3t[:], Pb_sc[:, 128 * kt:128 * (kt + 1)], ident_bf[:]))
                    b.step("vector", lambda v, kt=kt: v.tensor_copy(PbT[:, kt, :], ps3t[:]))
                for dh in range(2):
                    def mm_av(te, qi=qi, dh=dh):
                        for kt in range(qi + 1):
                            ins = te.matmul(ps3av[:], PbT[:, kt, :],
                                            Vtm[:, kt, 512 * dh:512 * (dh + 1)],
                                            start=(kt == 0), stop=(kt == qi))
                        return ins
                    b.step("tensor", mm_av)
                    b.step("scalar", lambda s, dh=dh: s.activation(
                        avtm[:, 512 * dh:512 * (dh + 1)], ps3av[:], AF.Copy,
                        scale=rl3[:]))
                for c in range(DCH):
                    b.step("tensor", lambda te, c=c: te.transpose(
                        ps3t[:], avtm[:, 128 * c:128 * (c + 1)], ident_bf[:]))
                    b.step("vector", lambda v, c=c: v.tensor_copy(avfm[:, c, :], ps3t[:]))
                for fh in range(2):
                    def mm_so(te, fh=fh):
                        for c in range(DCH):
                            ins = te.matmul(ps3s[:], avfm[:, c, :],
                                            woT[:, c, 512 * fh:512 * (fh + 1)],
                                            start=(c == 0), stop=(c == DCH - 1))
                        return ins
                    b.step("tensor", mm_so)
                    b.step("scalar", lambda s, fh=fh: s.activation(
                        stile[:, 512 * fh:512 * (fh + 1)], ps3s[:], AF.Copy))
                b.dma("sync", s_d[128 * qi:128 * (qi + 1), :], stile[:])
            if "dbg_s" in dbg_outs:
                b.dma("sync", dbg_outs["dbg_s"], s_d)
        if upto < 4:
            _close_lstacks(); b.emit(block); return nc

        pes = ExitStack()
        xts = pes.enter_context(nc.sbuf_tensor("xts", [128, 2, 1024], F32, **R))
        p4sc = pes.enter_context(nc.sbuf_tensor("p4sc", [128, 1024], F32, **R))
        p4a = pes.enter_context(nc.sbuf_tensor("p4a", [128, 2], F32, **R))
        p4b = pes.enter_context(nc.sbuf_tensor("p4b", [128, 2], F32, **R))
        xnt = pes.enter_context(nc.sbuf_tensor("xnt", [128, 2, 1024], BF16, **R))
        xttfm = pes.enter_context(nc.sbuf_tensor("xttfm", [128, DCH, 256], BF16, **R))
        ewb = pes.enter_context(nc.sbuf_tensor("ewb", [128, DCH, 2048], BF16, **R))
        ebr = pes.enter_context(nc.sbuf_tensor("ebr", [1, 1024], F32, **R))
        ytst = pes.enter_context(nc.sbuf_tensor("ytst", [32, 512], BF16, **R))
        ps4 = pes.enter_context(nc.psum_tensor("ps4", [32, 512], F32))
        with pes:
            ev_ew = [None] * 8
            ev_mm4 = [None] * 8
            p4_start = b.last

            def load_ew(n):
                w = (ev_mm4[n - 2],) if n >= 2 else (p4_start,)
                ev_ew[n] = b.dma(
                    "gpsimd", ewb[:, :, 1024 * (n % 2):1024 * (n % 2) + 1024],
                    ew_in[n].rearrange("(c p) h -> p c h", p=128),
                    ch=b.ch(f"dew{n % 2}"), waits=w, seq=False, set_last=False)
            load_ew(0); load_ew(1)
            b.dma("sync", xts[:], xt_red.rearrange("(t p) d -> p t d", p=128),
                  waits=((b.ch("cc1"), 1),))
            b.step("vector", lambda v: v.tensor_scalar(
                xts[:], xts[:], 65000.0, -33000.0, ALU.min, ALU.max))
            b.step("vector", lambda v: v.tensor_reduce(p4a[:], xts[:], AX.X, ALU.add))
            b.step("vector", lambda v: v.tensor_scalar(p4b[:], p4a[:], 1.0 / 1024, None, ALU.mult))
            b.step("vector", lambda v: v.scalar_tensor_tensor(
                xts[:], p4b[:].unsqueeze(2).broadcast_to([128, 2, 1024]), -1.0,
                xts[:], ALU.mult, ALU.add))
            for t in range(2):
                b.step("scalar", lambda s, t=t: s.activation(
                    p4sc[:], xts[:, t, :], AF.Square, accum_out=p4a[:, t:t + 1]))
            b.step("vector", lambda v: v.tensor_scalar(
                p4a[:], p4a[:], 1.0 / 1024, 1e-5, ALU.mult, ALU.add))
            b.step("scalar", lambda s: s.activation(p4b[:], p4a[:], AF.Sqrt))
            b.step("vector", lambda v: v.reciprocal(p4a[:], p4b[:]))
            b.step("vector", lambda v: v.tensor_tensor(
                xts[:], xts[:], p4a[:].unsqueeze(2).broadcast_to([128, 2, 1024]),
                ALU.mult))
            b.step("scalar", lambda s: s.activation(xnt[:], xts[:], AF.Tanh))
            b.dma("sync", xtt_d.rearrange("(t p) d -> p t d", p=128), xnt[:])
            for c in range(DCH):
                b.dma("sync", xttfm[:, c, :], xtt_d[:, 128 * c:128 * (c + 1)],
                      transpose=True)
            for n in range(8):
                b.dma("sync", ebr[:], eb_in[n].unsqueeze(0))
                for hh in range(2):
                    def mm_yt(te, n=n, hh=hh):
                        ins = te.matmul(ps4[:], ones_f[0:1, 0:32],
                                        ebr[0:1, 512 * hh:512 * (hh + 1)],
                                        start=True, stop=False)
                        for c in range(DCH):
                            ins = te.matmul(
                                ps4[:], xttfm[:, c, 32 * n:32 * (n + 1)],
                                ewb[:, c, 1024 * (n % 2) + 512 * hh:1024 * (n % 2) + 512 * (hh + 1)],
                                start=False, stop=(c == DCH - 1))
                        return ins
                    ev_mm4[n] = b.step("tensor", mm_yt, waits=(ev_ew[n],))
                    if n == 1 and hh == 0 and "dbg_ewb" in dbg_outs:
                        b.dma("sync", dbg_outs["dbg_ewb"], ewb[:].rearrange("p c n -> p (c n)"))
                    b.step("scalar", lambda s: s.activation(ytst[:], ps4[:], AF.Copy))
                    b.dma("sync", ytl_d[32 * n:32 * (n + 1), 512 * hh:512 * (hh + 1)], ytst[:])
                if n + 2 < 8:
                    load_ew(n + 2)
            b.step("gpsimd", lambda g: g.collective_compute(
                "AllGather", ALU.bypass, replica_groups=[list(range(8))],
                ins=[ytl_d.opt()], outs=[yt_all.opt()]), ch=b.ch("cc2"))
            if "dbg_xttfm" in dbg_outs:
                b.dma("sync", dbg_outs["dbg_xttfm"], xttfm[:].rearrange("p c n -> p (c n)"))
            if "dbg_xtt" in dbg_outs:
                b.dma("sync", dbg_outs["dbg_xtt"], xtt_d)
            if "dbg_ytl" in dbg_outs:
                b.dma("sync", dbg_outs["dbg_ytl"], ytl_d)
            if "dbg_ytall" in dbg_outs:
                b.dma("sync", dbg_outs["dbg_ytall"], yt_all)
        if upto < 5:
            _close_lstacks(); b.emit(block); return nc

        pes = ExitStack()
        Cfm = pes.enter_context(nc.sbuf_tensor("Cfm", [128, T, 2048], BF16, **R))
        ytsb = pes.enter_context(nc.sbuf_tensor("ytsb", [128, T, 1024], BF16, **R))
        gt5 = [pes.enter_context(nc.sbuf_tensor(f"gt5{p}", [128, 1024], F32, **R)) for p in range(2)]
        xt5 = [pes.enter_context(nc.sbuf_tensor(f"xt5{p}", [128, 1024], F32, **R)) for p in range(2)]
        st5 = [pes.enter_context(nc.sbuf_tensor(f"st5{p}", [128, 1024], F32, **R)) for p in range(2)]
        ps5 = [pes.enter_context(nc.psum_tensor(f"ps5{p}", [128, 512], F32)) for p in range(2)]
        with pes:
            b.dma("sync", ytsb[:], yt_all.rearrange("(c p) h -> p c h", p=128),
                  waits=((b.ch("cc2"), 1),))
            for npc in range(T):
                b.dma("sync", Cfm[:, npc, :], ctm_d[:, 128 * npc:128 * (npc + 1)],
                      transpose=True)
            ev_cfm = b.last
            ev_silu = {}; ev_tt2 = {}; ev_out = {}; ev_ldx = {}; ev_lds = {}
            for t in range(T):
                par = t % 2
                wl = (ev_out[t - 2],) if t - 2 in ev_out else (ev_cfm,)
                ev_ldx[t] = b.dma("sync", xt5[par][:], x_in[128 * t:128 * (t + 1), :],
                                  waits=wl, seq=False, set_last=False)
                ev_lds[t] = b.dma("scalar", st5[par][:], s_d[128 * t:128 * (t + 1), :],
                                  waits=wl, seq=False, set_last=False)
                for hh in range(2):
                    mw = [ev_cfm]
                    if (t - 1, hh) in ev_silu:
                        mw.append(ev_silu[(t - 1, hh)])
                    def mm_y(te, t=t, hh=hh):
                        for npc in range(T):
                            ins = te.matmul(ps5[hh][:], Cfm[:, npc, 128 * t:128 * (t + 1)],
                                            ytsb[:, npc, 512 * hh:512 * (hh + 1)],
                                            start=(npc == 0), stop=(npc == T - 1))
                        return ins
                    ev_mm = b.step("tensor", mm_y, waits=tuple(mw), seq=False, set_last=False)
                    sw = [ev_mm]
                    if t - 2 in ev_tt2:
                        sw.append(ev_tt2[t - 2])
                    ev_silu[(t, hh)] = b.step("scalar", lambda s, hh=hh, par=par: s.activation(
                        gt5[par][:, 512 * hh:512 * (hh + 1)], ps5[hh][:], AF.Silu),
                        waits=tuple(sw), seq=False, set_last=False)
                b.step("vector", lambda v, par=par: v.tensor_tensor(
                    gt5[par][:], gt5[par][:], st5[par][:], ALU.mult),
                    waits=(ev_silu[(t, 1)], ev_lds[t]), seq=False, set_last=False)
                ev_tt2[t] = b.step("vector", lambda v, par=par: v.tensor_tensor(
                    xt5[par][:], xt5[par][:], gt5[par][:], ALU.add),
                    waits=(ev_ldx[t],), seq=False, set_last=False)
                ev_out[t] = b.dma("sync", x1_d[128 * t:128 * (t + 1), :], xt5[par][:],
                                  waits=(ev_tt2[t],), seq=False, set_last=False)
            b.last = ev_out[T - 1]
            if "dbg_x1" in dbg_outs:
                b.dma("sync", dbg_outs["dbg_x1"], x1_d)
        if upto < 6:
            _close_lstacks(); b.emit(block); return nc

        es_l1 = ExitStack(); _lstacks.append(es_l1)
        xn1_fm = es_l1.enter_context(nc.sbuf_tensor("xn1_fm", [128, DCH, 2048], BF16))
        pes = ExitStack()
        x1all = pes.enter_context(nc.sbuf_tensor("x1all", [128, T, 1024], F32, **R))
        p6sc = pes.enter_context(nc.sbuf_tensor("p6sc", [128, 1024], F32, **R))
        p6a = pes.enter_context(nc.sbuf_tensor("p6a", [128, T], F32, **R))
        p6b = pes.enter_context(nc.sbuf_tensor("p6b", [128, T], F32, **R))
        xn1tm = pes.enter_context(nc.sbuf_tensor("xn1tm", [128, T, 1024], BF16, **R))
        xn1lo = pes.enter_context(nc.sbuf_tensor("xn1lo", [128, T, 1024], BF16, **R))
        with pes:
            b.dma("sync", x1all[:], x1_d.rearrange("(t p) d -> p t d", p=128))
            b.step("vector", lambda v: v.tensor_reduce(p6a[:], x1all[:], AX.X, ALU.add))
            b.step("vector", lambda v: v.tensor_scalar(p6b[:], p6a[:], 1.0 / 1024, None, ALU.mult))
            b.step("vector", lambda v: v.scalar_tensor_tensor(
                x1all[:], p6b[:].unsqueeze(2).broadcast_to([128, T, 1024]), -1.0,
                x1all[:], ALU.mult, ALU.add))
            for t in range(T):
                b.step("scalar", lambda s, t=t: s.activation(
                    p6sc[:], x1all[:, t, :], AF.Square, accum_out=p6a[:, t:t + 1]))
            b.step("vector", lambda v: v.tensor_scalar(
                p6a[:], p6a[:], 1.0 / 1024, 1e-5, ALU.mult, ALU.add))
            b.step("scalar", lambda s: s.activation(p6b[:], p6a[:], AF.Sqrt))
            b.step("vector", lambda v: v.reciprocal(p6a[:], p6b[:]))
            b.step("vector", lambda v: v.tensor_tensor(
                x1all[:], x1all[:], p6a[:].unsqueeze(2).broadcast_to([128, T, 1024]),
                ALU.mult))
            b.step("vector", lambda v: v.tensor_copy(xn1tm[:], x1all[:]))
            b.step("vector", lambda v: v.tensor_tensor(
                xn1lo[:], x1all[:], xn1tm[:], ALU.subtract))
            b.dma("sync", xn1_tm_d.rearrange("(t p) d -> p t d", p=128), xn1tm[:])
            b.dma("sync", xn1lo_d.rearrange("(t p) d -> p t d", p=128), xn1lo[:])
            for c in range(DCH):
                b.dma("sync", xn1_fm[:, c, :], xn1_tm_d[:, 128 * c:128 * (c + 1)],
                      transpose=True)
            if "dbg_xn1" in dbg_outs:
                b.dma("sync", dbg_outs["dbg_xn1"], xn1_tm_d)
        if upto < 7:
            _close_lstacks(); b.emit(block); return nc

        es_ff = ExitStack(); _lstacks.append(es_ff)
        xn1_ff = es_ff.enter_context(nc.sbuf_tensor("xn1_ff", [128, DCH, 2048], F32, **R))
        pes = ExitStack()
        wt1 = pes.enter_context(nc.sbuf_tensor("wt1", [128, DCH, 192], F32, **R))
        qfm1 = pes.enter_context(nc.sbuf_tensor("qfm1", [64, 2048], F32, **R))
        kfm1 = pes.enter_context(nc.sbuf_tensor("kfm1", [64, 2048], F32, **R))
        vtm1 = pes.enter_context(nc.sbuf_tensor("vtm1", [128, T, 64], F32, **R))
        st1 = pes.enter_context(nc.sbuf_tensor("st1", [128, 2048], F32, **R))
        st1e = pes.enter_context(nc.sbuf_tensor("st1e", [128, 2048], F32, **R))
        pbt1 = pes.enter_context(nc.sbuf_tensor("pbt1", [128, 2048], F32, **R))
        m7 = pes.enter_context(nc.sbuf_tensor("m7", [128, 1], F32, **R))
        l7 = pes.enter_context(nc.sbuf_tensor("l7", [128, 1], F32, **R))
        ps7 = pes.enter_context(nc.psum_tensor("ps7", [128, 512], F32))
        ps7q = pes.enter_context(nc.psum_tensor("ps7q", [64, 512], F32))
        ps7av = pes.enter_context(nc.psum_tensor("ps7av", [128, 1024], F32))
        with pes:
            les = ExitStack()
            lo_fm = les.enter_context(nc.sbuf_tensor("lo_fm", [128, DCH, 2048], BF16, **R))
            with les:
                for c in range(DCH):
                    b.dma("sync", lo_fm[:, c, :], xn1lo_d[:, 128 * c:128 * (c + 1)],
                          transpose=True)
                b.step("vector", lambda v: v.tensor_tensor(
                    xn1_ff[:], xn1_fm[:], lo_fm[:], ALU.add))
            b.dma("sync", wt1[:], l1t_in.rearrange("(c p) f -> p c f", p=128))
            for tb in range(4):
                for (dst, woff) in ((qfm1, 0), (kfm1, 64)):
                    def mm_qk1(te, tb=tb, woff=woff):
                        for c in range(DCH):
                            ins = te.matmul(ps7q[:], wt1[:, c, woff:woff + 64],
                                            xn1_ff[:, c, 512 * tb:512 * (tb + 1)],
                                            start=(c == 0), stop=(c == DCH - 1))
                        return ins
                    b.step("tensor", mm_qk1)
                    b.step("scalar", lambda s, dst=dst, tb=tb: s.activation(
                        dst[0:64, 512 * tb:512 * (tb + 1)], ps7q[:], AF.Copy))
            for t in range(T):
                def mm_v1(te, t=t):
                    for c in range(DCH):
                        ins = te.matmul(ps7[:, 0:64],
                                        xn1_ff[:, c, 128 * t:128 * (t + 1)],
                                        wt1[:, c, 128:192],
                                        start=(c == 0), stop=(c == DCH - 1))
                    return ins
                b.step("tensor", mm_v1)
                b.step("scalar", lambda s, t=t: s.activation(vtm1[:, t, :], ps7[:, 0:64], AF.Copy))
            for kt in range(T):
                for ib in range(4):
                    b.step("tensor", lambda te, kt=kt, ib=ib: te.matmul(
                        ps7[:], kfm1[0:64, 128 * kt:128 * (kt + 1)],
                        qfm1[0:64, 512 * ib:512 * (ib + 1)], start=True, stop=True))
                    b.step("scalar", lambda s, ib=ib: s.activation(
                        st1[:, 512 * ib:512 * (ib + 1)], ps7[:], AF.Copy))
                b.step("vector", lambda v: v.tensor_reduce(
                    m7[:], st1[:], AX.X, ALU.max, negate=True))
                b.step("scalar", lambda s: s.activation(
                    st1e[:], st1[:], AF.Exp, bias=m7[:], accum_out=l7[:]))
                b.step("vector", lambda v: v.reciprocal(l7[:], l7[:]))
                b.step("vector", lambda v: v.tensor_scalar(
                    pbt1[:], st1e[:], l7[:], None, ALU.mult))
                if kt == 0 and "dbg_pbt" in dbg_outs:
                    b.dma("sync", dbg_outs["dbg_pbt"], pbt1[:])
                def mm_av1(te, kt=kt):
                    for ib in range(T):
                        ins = te.matmul(ps7av[:, 64 * ib:64 * (ib + 1)],
                                        pbt1[:, 128 * ib:128 * (ib + 1)], vtm1[:, kt, :],
                                        start=(kt == 0 and ib % 8 == 0),
                                        stop=(kt == T - 1))
                    return ins
                b.step("tensor", mm_av1)
            for ti2 in range(2):
                b.step("scalar", lambda s, ti2=ti2: s.activation(
                    s1_sb[:].rearrange("p t v -> p (t v)")[:, 512 * ti2:512 * (ti2 + 1)],
                    ps7av[:, 512 * ti2:512 * (ti2 + 1)], AF.Copy))
            if "dbg_v1" in dbg_outs:
                b.dma("sync", dbg_outs["dbg_v1"].rearrange("(t p) v -> p t v", p=128), vtm1[:])
            if "dbg_s1" in dbg_outs:
                b.dma("sync", dbg_outs["dbg_s1"].rearrange("(t p) v -> p t v", p=128), s1_sb[:])
        if upto < 8:
            _close_lstacks(); b.emit(block); return nc

        pes = ExitStack()
        gwT = pes.enter_context(nc.sbuf_tensor("gwT", [128, DCH, 8], F32, **R))
        gbrep = pes.enter_context(nc.sbuf_tensor("gbrep", [128, 8], F32, **R))
        gbrow = pes.enter_context(nc.sbuf_tensor("gbrow", [1, 8], F32, **R))
        r8a = pes.enter_context(nc.sbuf_tensor("r8a", [128, 16, 8], F32, **R))
        r8b = pes.enter_context(nc.sbuf_tensor("r8b", [128, 16, 8], F32, **R))
        r8c = pes.enter_context(nc.sbuf_tensor("r8c", [128, 16, 8], F32, **R))
        m8 = pes.enter_context(nc.sbuf_tensor("m8", [128, 16], F32, **R))
        m8b = pes.enter_context(nc.sbuf_tensor("m8b", [128, 16], F32, **R))
        ps8 = pes.enter_context(nc.psum_tensor("ps8", [128, 128], F32))
        ps8t = pes.enter_context(nc.psum_tensor("ps8t", [8, 128], F32))
        ps8u = pes.enter_context(nc.psum_tensor("ps8u", [8, 128], F32))
        with pes:
            b.dma("sync", gwT[:], gwt_in.rearrange("(c p) f -> p c f", p=128))
            b.dma("sync", gbrow[:], gb_in.unsqueeze(0))
            b.step("tensor", lambda te: te.matmul(
                ps8t[0:1, 0:8] if False else ps8[:, 0:8], ones_f[0:1, 0:128], gbrow[:],
                start=True, stop=True))
            b.step("scalar", lambda s: s.activation(gbrep[:], ps8[:, 0:8], AF.Copy))
            def mm_gall(te):
                ins = None
                for t in range(T):
                    for c in range(DCH):
                        ins = te.matmul(ps8[:, 8 * t:8 * (t + 1)],
                                        xn1_ff[:, c, 128 * t:128 * (t + 1)],
                                        gwT[:, c, :],
                                        start=(t == 0 and c == 0), stop=(c == DCH - 1))
                return ins
            b.step("tensor", mm_gall)
            b.step("scalar", lambda s: s.activation(
                r8a[:].rearrange("p t e -> p (t e)"), ps8[:], AF.Copy))
            bc = lambda x: x.unsqueeze(2).broadcast_to([128, T, 8])
            b.step("vector", lambda v: v.tensor_reduce(m8[:], r8a[:], AX.X, ALU.max, negate=True))
            b.step("vector", lambda v: v.tensor_tensor(r8c[:], r8a[:], bc(m8[:]), ALU.add))
            b.step("scalar", lambda s: s.activation(r8b[:], r8c[:], AF.Exp))
            b.step("vector", lambda v: v.tensor_reduce(m8[:], r8b[:], AX.X, ALU.add))
            b.step("vector", lambda v: v.reciprocal(m8[:], m8[:]))
            b.step("vector", lambda v: v.tensor_tensor(
                cw_sb[:], r8b[:], bc(m8[:]), ALU.mult))
            b.step("vector", lambda v: v.tensor_tensor(
                r8a[:], cw_sb[:], gbrep[:].unsqueeze(1).broadcast_to([128, T, 8]),
                ALU.add))
            b.step("vector", lambda v: v.tensor_reduce(m8[:], r8a[:], AX.X, ALU.max, negate=True))
            b.step("vector", lambda v: v.tensor_tensor(r8b[:], r8a[:], bc(m8[:]), ALU.add))
            b.step("vector", lambda v: v.tensor_scalar(
                r8b[:], r8b[:], 0.0, None, ALU.is_ge))
            b.step("vector", lambda v: v.scalar_tensor_tensor(
                r8c[:], r8b[:], NEG, r8a[:], ALU.mult, ALU.add))
            b.step("vector", lambda v: v.tensor_reduce(m8[:], r8c[:], AX.X, ALU.max, negate=True))
            b.step("vector", lambda v: v.tensor_tensor(r8b[:], r8a[:], bc(m8[:]), ALU.add))
            b.step("vector", lambda v: v.tensor_scalar(
                r8b[:], r8b[:], 0.0, None, ALU.is_ge))
            b.step("vector", lambda v: v.tensor_tensor(cw_sb[:], r8b[:], cw_sb[:], ALU.mult))
            for t in range(0, T, 2):
                b.step("tensor", lambda te, t=t: te.transpose(
                    ps8t[:], cw_sb[:, t, :], ident_f[:]))
                b.step("tensor", lambda te, t=t: te.transpose(
                    ps8u[:], cw_sb[:, t + 1, :], ident_f[:]))
                def ev2(s, t=t):
                    s.activation(cwT_sb[:, 128 * t:128 * (t + 1)], ps8t[:], AF.Copy)
                    return s.activation(cwT_sb[:, 128 * (t + 1):128 * (t + 2)], ps8u[:], AF.Copy)
                b.step("scalar", ev2)
            if "dbg_cw" in dbg_outs:
                b.dma("sync", dbg_outs["dbg_cw"].rearrange("(t p) e -> p t e", p=128), cw_sb[:])
        es_ff.close()
        es_l1.close()
        if upto < 9:
            _close_lstacks(); b.emit(block); return nc

        pes = ExitStack()
        Aw = pes.enter_context(nc.sbuf_tensor("Aw", [128, DCH, 2048], BF16, **R))
        Bw = pes.enter_context(nc.sbuf_tensor("Bw", [128, DCH, 2048], BF16, **R))
        w2b = pes.enter_context(nc.sbuf_tensor("w2b", [128, DCH, 1024], BF16, **R))
        acc = pes.enter_context(nc.sbuf_tensor("acc", [128, 8, 1024], F32, **R))
        xn1h = pes.enter_context(nc.sbuf_tensor("xn1h", [128, DCH, 1024], BF16, **R))
        a_tb = pes.enter_context(nc.sbuf_tensor("a_tb", [128, DCH, 512], BF16, **R))
        cwrep = pes.enter_context(nc.sbuf_tensor("cwrep", [128, 1024], F32, **R))
        u_sc = [[pes.enter_context(nc.sbuf_tensor(f"u{p}_{j}", [128, 512], F32, **R))
                 for j in range(4)] for p in range(2)]
        gG = [pes.enter_context(nc.sbuf_tensor(f"gG{p}", [128, 512], F32, **R)) for p in range(2)]
        gL = [pes.enter_context(nc.sbuf_tensor(f"gL{p}", [128, 512], F32, **R)) for p in range(2)]
        b2sb = pes.enter_context(nc.sbuf_tensor("b2sb", [8, 1024], F32, **R))
        sb2r = pes.enter_context(nc.sbuf_tensor("sb2r", [1, 1024], F32, **R))
        g9 = pes.enter_context(nc.sbuf_tensor("g9", [128, 1024], F32, **R))
        x9 = pes.enter_context(nc.sbuf_tensor("x9", [128, 1024], F32, **R))
        hp = [[pes.enter_context(nc.psum_tensor(f"hp{p}_{j}", [128, 512], F32))
               for j in range(4)] for p in range(2)]
        ps9y = hp[0][0]
        ps9c = hp[1][0]
        with pes:
            p9_entry = b.last
            b.dma("sync", b2sb[:], b2_in)
            b.dma("sync", sb2r[:], sb2_in.unsqueeze(0))
            ch_dwh = b.ch("dwh")
            ch_dw2 = b.ch("dw2")
            ev_h_free = b.last
            ev_w2_free = b.last
            ev_tb_done = b.last
            for th in range(2):
                for c in range(DCH):
                    b.dma("sync", xn1h[:, c, :],
                          xn1_tm_d[1024 * th:1024 * (th + 1), 128 * c:128 * (c + 1)],
                          transpose=True)
                ev_xn1h = b.last
                for e in range(9):
                    shared = (e == 8)
                    if shared:
                        srcs = (sw1t_bf, sw3t_bf, sw2t_bf)
                        evs = (cast_ev["sw1"], cast_ev["sw3"], cast_ev["sw2"])
                    else:
                        srcs = (w1t_bf[e], w3t_bf[e], w2t_bf[e])
                        evs = (cast_ev[f"w1_{e}"], cast_ev[f"w3_{e}"], cast_ev[f"w2_{e}"])
                    b.dma("gpsimd", Aw[:], srcs[0].rearrange("(c p) f -> p c f", p=128),
                          ch=ch_dwh, waits=(evs[0], ev_h_free, p9_entry),
                          seq=False, set_last=False)
                    ev_ld_h = b.dma("gpsimd", Bw[:],
                                    srcs[1].rearrange("(c p) f -> p c f", p=128),
                                    ch=ch_dwh, waits=(evs[1],), seq=False, set_last=False)
                    ev_ld_w2 = b.dma("gpsimd", w2b[:],
                                     srcs[2].rearrange("(c p) f -> p c f", p=128),
                                     ch=ch_dw2, waits=(evs[2], ev_w2_free, p9_entry),
                                     seq=False, set_last=False)
                    for tb in range(2):
                        tsl = slice(512 * tb, 512 * (tb + 1))
                        ev_start = ev_tb_done
                        if not shared:
                            ev_c = b.step("tensor", lambda te, e=e, th=th, tb=tb: te.matmul(
                                ps9c[:], sel8[:, 128 * e:128 * (e + 1)],
                                cwT_sb[:, 1024 * th + 512 * tb:1024 * th + 512 * (tb + 1)],
                                start=True, stop=True),
                                waits=(ev_start,), seq=False, set_last=False)
                            ev_cw = b.step("scalar", lambda s, tb=tb: s.activation(
                                cwrep[:, 0:512], ps9c[:], AF.Copy),
                                waits=(ev_c,), seq=False, set_last=False)
                        ev_pe = {}
                        ev_evac = {}
                        ev_v1 = {}
                        ev_v2 = {}
                        for fc in range(DCH):
                            p = fc % 2
                            pw = [ev_xn1h, (ch_dwh, ch_dwh.n), ev_start]
                            if fc - 2 in ev_evac:
                                pw.append(ev_evac[fc - 2])
                            if fc == 1 and not shared:
                                pw.append(ev_cw)
                            def mm_h4(te, fc=fc, p=p, tsl=tsl):
                                ins = None
                                for (j, woff, wbuf) in ((0, 0, Aw), (1, 0, Bw),
                                                        (2, 1024, Aw), (3, 1024, Bw)):
                                    for c in range(DCH):
                                        ins = te.matmul(
                                            hp[p][j][:],
                                            wbuf[:, c, woff + 128 * fc:woff + 128 * (fc + 1)],
                                            xn1h[:, c, tsl], start=(c == 0),
                                            stop=(c == DCH - 1))
                                return ins
                            ev_pe[fc] = b.step("tensor", mm_h4, waits=tuple(pw),
                                               seq=False, set_last=False)
                            ei = 8 if shared else e
                            ew_ = [ev_pe[fc], ev_bias]
                            if fc - 2 in ev_v2:
                                ew_.append(ev_v2[fc - 2])
                            def ev4(s, fc=fc, p=p, ei=ei):
                                s.activation(u_sc[p][0][:], hp[p][0][:], AF.Identity,
                                             bias=b1cols[:, ei, fc:fc + 1])
                                s.activation(u_sc[p][1][:], hp[p][1][:], AF.Identity,
                                             bias=b3cols[:, ei, fc:fc + 1])
                                s.activation(u_sc[p][2][:], hp[p][2][:], AF.Identity,
                                             bias=b1cols[:, ei, 8 + fc:9 + fc])
                                return s.activation(u_sc[p][3][:], hp[p][3][:], AF.Identity,
                                                    bias=b3cols[:, ei, 8 + fc:9 + fc])
                            ev_evac[fc] = b.step("scalar", ev4, waits=tuple(ew_),
                                                 seq=False, set_last=False)
                            def v1(v, p=p):
                                v.tensor_tensor(gG[p][:], u_sc[p][0][:], u_sc[p][1][:], ALU.mult)
                                return v.tensor_tensor(gL[p][:], u_sc[p][2][:], u_sc[p][3][:], ALU.mult)
                            ev_v1[fc] = b.step("vector", v1, waits=(ev_evac[fc],),
                                               seq=False, set_last=False)
                            def p1(g, p=p):
                                g.tensor_scalar(gG[p][:], gG[p][:], 7.0, None, ALU.min)
                                return g.tensor_scalar(gL[p][:], gL[p][:], 7.0, -7.0,
                                                       ALU.min, ALU.max)
                            ev_p1 = b.step("gpsimd", p1, waits=(ev_v1[fc],),
                                           seq=False, set_last=False)
                            ev_a1 = b.step("scalar", lambda s, p=p: s.activation(
                                u_sc[p][1][:], gG[p][:], AF.Gelu_apprx_sigmoid),
                                waits=(ev_p1,), seq=False, set_last=False)
                            vw = [ev_a1]
                            if not shared:
                                vw.append(ev_cw)
                            def v2(v, p=p, fc=fc, tsl=tsl, shared=shared):
                                if shared:
                                    return v.scalar_tensor_tensor(
                                        a_tb[:, fc, :], gL[p][:], 1.0, u_sc[p][1][:],
                                        ALU.add, ALU.mult)
                                v.scalar_tensor_tensor(
                                    u_sc[p][0][:], gL[p][:], 1.0, u_sc[p][1][:],
                                    ALU.add, ALU.mult)
                                return v.tensor_tensor(a_tb[:, fc, :], u_sc[p][0][:],
                                                       cwrep[:, 0:512], ALU.mult)
                            ev_v2[fc] = b.step("vector", v2, waits=tuple(vw),
                                               seq=False, set_last=False)
                        ev_h_free = ev_pe[DCH - 1]
                        ev_ac_hist = [(ev_evac[DCH - 2],), (ev_evac[DCH - 1],)]
                        ev_y_base = (ev_v2[DCH - 1], (ch_dw2, ch_dw2.n))
                        k = 0
                        for tsub in range(4):
                            tt = 8 * th + 4 * tb + tsub
                            ta = 4 * tb + tsub
                            for dh in range(2):
                                yb = hp[k % 2][0]
                                def mm_y9(te, tsub=tsub, dh=dh, e=e, tt=tt, yb=yb):
                                    ins = None
                                    if e == 0:
                                        ins = te.matmul(yb[:], cwT_sb[:, 128 * tt:128 * (tt + 1)],
                                                        b2sb[:, 512 * dh:512 * (dh + 1)],
                                                        start=True, stop=False)
                                    if e == 8:
                                        ins = te.matmul(yb[:], ones_f[0:1, 0:128],
                                                        sb2r[0:1, 512 * dh:512 * (dh + 1)],
                                                        start=True, stop=False)
                                    for fc in range(DCH):
                                        ins = te.matmul(
                                            yb[:], a_tb[:, fc, 128 * tsub:128 * (tsub + 1)],
                                            w2b[:, fc, 512 * dh:512 * (dh + 1)],
                                            start=(fc == 0 and e not in (0, 8)),
                                            stop=(fc == DCH - 1))
                                    return ins
                                ev_y = b.step("tensor", mm_y9,
                                              waits=ev_y_base + ev_ac_hist[k % 2],
                                              seq=False, set_last=False)
                                if e == 0:
                                    ev_ac = b.step("scalar", lambda s, ta=ta, dh=dh, yb=yb: s.activation(
                                        acc[:, ta, 512 * dh:512 * (dh + 1)], yb[:], AF.Copy),
                                        waits=(ev_y,), seq=False, set_last=False)
                                else:
                                    ev_ac = b.step("vector", lambda v, ta=ta, dh=dh, yb=yb: v.tensor_tensor(
                                        acc[:, ta, 512 * dh:512 * (dh + 1)], yb[:],
                                        acc[:, ta, 512 * dh:512 * (dh + 1)], ALU.add),
                                        waits=(ev_y,), seq=False, set_last=False)
                                ev_ac_hist[k % 2] = (ev_ac,)
                                ev_y_base = ((ch_dw2, ch_dw2.n),)
                                k += 1
                        ev_w2_free = ev_y
                        ev_tb_done = ev_ac
                b.last = ev_tb_done
                for ta in range(8):
                    t = 8 * th + ta
                    b.step("scalar", lambda s, ta=ta: s.activation(g9[:], acc[:, ta, :], AF.Silu))
                    b.dma("sync", x9[:], x1_d[128 * t:128 * (t + 1), :])
                    b.step("vector", lambda v, t=t: v.tensor_tensor(
                        g9[:].rearrange("p (r v) -> p r v", v=64),
                        g9[:].rearrange("p (r v) -> p r v", v=64),
                        s1_sb[:, t, :].unsqueeze(1).broadcast_to([128, 16, 64]), ALU.mult))
                    b.step("vector", lambda v: v.tensor_tensor(x9[:], x9[:], g9[:], ALU.add))
                    b.dma("sync", out_d[128 * t:128 * (t + 1), :], x9[:])
                ev_tb_done = b.last
        fin_s, fin_v = b.last[0].sem, b.last[1]
        b.ops["gpsimd"].append(lambda E, s=fin_s, v=fin_v: E.wait_ge(s, v))
        _close_lstacks()
        b.emit(block)
    return nc


def prep_inputs(inputs):
    f = lambda v: np.ascontiguousarray(np.asarray(v, dtype=np.float32))
    i = {k: np.asarray(v) for k, v in inputs.items()}
    w1 = i["L1_ffn_w1"]; w3 = i["L1_ffn_w3"]
    w1sep = np.concatenate([w1[:, 0::2, :], w1[:, 1::2, :]], axis=1)
    w3sep = np.concatenate([w3[:, 0::2, :], w3[:, 1::2, :]], axis=1)
    sw1 = i["L1_ffn_sw1"]; sw3 = i["L1_ffn_sw3"]
    sw1sep = np.concatenate([sw1[0::2, :], sw1[1::2, :]], axis=0)
    sw3sep = np.concatenate([sw3[0::2, :], sw3[1::2, :]], axis=0)

    def cols(bvec):
        return f(bvec.reshape(8, 128, 2).transpose(1, 2, 0).reshape(128, 16))

    shared = {
        "wqt": f(i["L0_attn_Wq"].T), "wkt": f(i["L0_attn_Wk"].T),
        "wvt": f(i["L0_attn_Wv"].T), "wot": f(i["L0_attn_Wo"].T),
        "phiwt": f(i["L0_ffn_phi_w"].T), "phi_b": f(i["L0_ffn_phi_b"]),
        "l1t": f(np.concatenate([i["L1_attn_Wq"].T, i["L1_attn_Wk"].T,
                                 i["L1_attn_Wv"].T], axis=1)),
        "gwt": f(i["L1_ffn_gate_w"].T), "gate_b": f(i["L1_ffn_gate_b"]),
        "w1t": f(w1sep.transpose(0, 2, 1)), "w3t": f(w3sep.transpose(0, 2, 1)),
        "w2t": f(i["L1_ffn_w2"].transpose(0, 2, 1)),
        "sw1t": f(sw1sep.T), "sw3t": f(sw3sep.T), "sw2t": f(i["L1_ffn_sw2"].T),
        "b1c": np.stack([cols(i["L1_ffn_b1"][e]) for e in range(8)]),
        "b3c": np.stack([cols(i["L1_ffn_b3"][e]) for e in range(8)]),
        "sb1c": cols(i["L1_ffn_sb1"]), "sb3c": cols(i["L1_ffn_sb3"]),
        "b2": f(i["L1_ffn_b2"]), "sb2": f(i["L1_ffn_sb2"]),
    }
    per_core = []
    for c in range(8):
        m = dict(shared)
        m["x"] = f(i["L0_attn_in_features"][c])
        m["ew_sl"] = f(i["L0_ffn_expert_w"][8 * c:8 * (c + 1)])
        m["eb_sl"] = f(i["L0_ffn_expert_b"][8 * c:8 * (c + 1)])
        per_core.append(m)
    return per_core


def kernel(**inputs):
    nc = build_nc()
    in_maps = prep_inputs(inputs)
    res = run_bass_kernel_spmd(nc, in_maps, core_ids=list(range(8)))
    return np.stack([np.asarray(r["out"], dtype=np.float32) for r in res.results])


# version 04 (final)
# speedup vs baseline: 1.6066x; 1.0783x over previous
import numpy as np
from contextlib import ExitStack
import concourse.bass as bass
import concourse.mybir as mybir
from concourse.bass_utils import run_bass_kernel_spmd

F32 = mybir.dt.float32
BF16 = mybir.dt.bfloat16
AF = mybir.ActivationFunctionType
ALU = mybir.AluOpType
AX = mybir.AxisListType

ENG = ("tensor", "vector", "scalar", "gpsimd", "sync")
NEG = -1.0e30


class Chan:
    def __init__(self, nc, es, name):
        self.sem = es.enter_context(nc.semaphore(name))
        self.n = 0


class B:

    def __init__(self, nc, es):
        self.nc, self.es = nc, es
        self.ops = {e: [] for e in ENG}
        self.chans = {}
        self.last = None

    def ch(self, name):
        if name not in self.chans:
            self.chans[name] = Chan(self.nc, self.es, name)
        return self.chans[name]

    def step(self, eng, fn, ch=None, inc=1, waits=(), seq=True, set_last=True):
        ws = [(c, v) for (c, v) in waits if c is not None and v > 0]
        if seq and self.last is not None:
            ws.append(self.last)
        if ch is None:
            ch = self.ch("c_" + eng)
        ch.n += inc
        val = ch.n
        ws_f = tuple((c.sem, v) for (c, v) in ws)

        def closure(E, fn=fn, ws_f=ws_f, sem=ch.sem, inc=inc):
            for (s, v) in ws_f:
                E.wait_ge(s, v)
            ins = fn(E)
            ins.then_inc(sem, inc)

        self.ops[eng].append(closure)
        if set_last:
            self.last = (ch, val)
        return (ch, val)

    def dma(self, q, out, in_, waits=(), seq=True, set_last=True, ch=None, **kw):
        if ch is None:
            ch = self.ch("d_" + q)
        return self.step(q, lambda E: E.dma_start(out=out, in_=in_, **kw),
                         ch=ch, inc=16, waits=waits, seq=seq, set_last=set_last)

    def emit(self, block):
        for eng in ENG:
            fns = self.ops[eng]
            getattr(block, eng)(lambda E, fns=fns: [fn(E) for fn in fns])


def build_nc(upto=99, dbg=()):
    nc = bass.Bass("TRN2", target_bir_lowering=False)
    T = 16
    DCH = 8

    EI = lambda n, s, dt=F32: nc.dram_tensor(n, s, dt, kind="ExternalInput").ap()
    EO = lambda n, s, dt=F32: nc.dram_tensor(n, s, dt, kind="ExternalOutput").ap()
    IN = lambda n, s, dt: nc.dram_tensor(n, s, dt, kind="Internal").ap()

    x_in = EI("x", [2048, 1024])
    wqt_in, wkt_in, wvt_in, wot_in = (EI(n, [1024, 1024]) for n in ["wqt", "wkt", "wvt", "wot"])
    phiwt_in = EI("phiwt", [1024, 2048]); phib_in = EI("phi_b", [2048])
    ew_in = EI("ew_sl", [8, 1024, 1024]); eb_in = EI("eb_sl", [8, 1024])
    l1t_in = EI("l1t", [1024, 192])
    gwt_in = EI("gwt", [1024, 8]); gb_in = EI("gate_b", [8])
    w1t_in = EI("w1t", [8, 1024, 2048]); w3t_in = EI("w3t", [8, 1024, 2048])
    w2t_in = EI("w2t", [8, 1024, 1024])
    sw1t_in = EI("sw1t", [1024, 2048]); sw3t_in = EI("sw3t", [1024, 2048])
    sw2t_in = EI("sw2t", [1024, 1024])
    b1c_in = EI("b1c", [8, 128, 16]); b3c_in = EI("b3c", [8, 128, 16])
    sb1c_in = EI("sb1c", [128, 16]); sb3c_in = EI("sb3c", [128, 16])
    b2_in = EI("b2", [8, 1024]); sb2_in = EI("sb2", [1024])

    out_d = EO("out", [2048, 1024])

    wqt_bf, wkt_bf, wvt_bf, wot_bf = (IN(n, [1024, 1024], BF16)
                                      for n in ["wqt_bf", "wkt_bf", "wvt_bf", "wot_bf"])
    phiwt_bf = IN("phiwt_bf", [1024, 2048], BF16)
    l1t_bf = IN("l1t_bf", [1024, 192], BF16)
    gwt_bf = IN("gwt_bf", [1024, 8], BF16)
    w1t_bf = IN("w1t_bf", [8, 1024, 2048], BF16)
    w3t_bf = IN("w3t_bf", [8, 1024, 2048], BF16)
    w2t_bf = IN("w2t_bf", [8, 1024, 1024], BF16)
    sw1t_bf = IN("sw1t_bf", [1024, 2048], BF16)
    sw3t_bf = IN("sw3t_bf", [1024, 2048], BF16)
    sw2t_bf = IN("sw2t_bf", [1024, 1024], BF16)

    xn_tm_d = IN("xn_tm_d", [2048, 1024], BF16)
    ctm_d = IN("ctm_d", [2048, 2048], BF16)
    xt_part = IN("xt_part", [2048, 1024], F32)
    xt_red = IN("xt_red", [256, 1024], F32)
    s_d = IN("s_d", [2048, 1024], F32)
    xtt_d = IN("xtt_d", [256, 1024], BF16)
    ytl_d = IN("ytl_d", [256, 1024], BF16)
    yt_all = IN("yt_all", [2048, 1024], BF16)
    x1_d = IN("x1_d", [2048, 1024], F32)
    xn1_tm_d = IN("xn1_tm_d", [2048, 1024], BF16)
    xn1lo_d = IN("xn1lo_d", [2048, 1024], BF16)

    dbg_outs = {}
    for name, shape, dt in [
        ("dbg_xn", [2048, 1024], BF16), ("dbg_D", [2048, 2048], BF16),
        ("dbg_C", [2048, 2048], BF16), ("dbg_xtpart", [2048, 1024], F32),
        ("dbg_xtred", [256, 1024], F32), ("dbg_s", [2048, 1024], F32),
        ("dbg_ytall", [2048, 1024], BF16), ("dbg_x1", [2048, 1024], F32),
        ("dbg_xtt", [256, 1024], BF16), ("dbg_ytl", [256, 1024], BF16),
        ("dbg_xttfm", [128, 2048], BF16), ("dbg_ewb", [128, 16384], BF16),
        ("dbg_xn1", [2048, 1024], BF16), ("dbg_s1", [2048, 64], F32),
        ("dbg_pbt", [128, 2048], F32), ("dbg_v1", [2048, 64], F32),
        ("dbg_cw", [2048, 8], F32),
    ]:
        if name in dbg:
            dbg_outs[name] = EO(name, shape, dt)

    es = ExitStack()
    b = B(nc, es)
    sbL = lambda *a: es.enter_context(nc.sbuf_tensor(*a))
    R = {"side": "right"}

    with es, nc.Block() as block:
        ident_bf = sbL("ident_bf", [128, 128], BF16)
        ident_f = sbL("ident_f", [128, 128], F32)
        maskneg = sbL("maskneg", [128, 128], F32)
        ones_f = sbL("ones_f", [1, 512], F32)
        sel8 = sbL("sel8", [8, 8 * 128], F32)
        s1_sb = sbL("s1_sb", [128, T, 64], F32)
        cw_sb = sbL("cw_sb", [128, T, 8], F32)
        cwT_sb = sbL("cwT_sb", [8, 2048], F32)
        b1cols = sbL("b1cols", [128, 9, 16], F32)
        b3cols = sbL("b3cols", [128, 9, 16], F32)

        def _consts(g):
            g.memset(ident_bf[:], 1.0)
            g.affine_select(ident_bf[:], ident_bf[:], [[-1, 128]], ALU.is_equal,
                            0.0, base=0, channel_multiplier=1)
            g.memset(ident_f[:], 1.0)
            g.affine_select(ident_f[:], ident_f[:], [[-1, 128]], ALU.is_equal,
                            0.0, base=0, channel_multiplier=1)
            g.memset(maskneg[:], 0.0)
            g.affine_select(maskneg[:], maskneg[:], [[-1, 128]], ALU.is_ge,
                            NEG, base=0, channel_multiplier=1)
            g.memset(ones_f[:], 1.0)
            g.memset(sel8[:], 1.0)
            ins = None
            for e in range(8):
                ins = g.affine_select(sel8[:, e * 128:(e + 1) * 128],
                                      sel8[:, e * 128:(e + 1) * 128], [[0, 128]],
                                      ALU.is_equal, 0.0, base=-e, channel_multiplier=1)
            return ins
        b.step("gpsimd", _consts)

        wes = ExitStack()
        ps_warm = wes.enter_context(nc.psum_tensor("ps_warm", [128, 128], F32))
        with wes:
            b.step("tensor", lambda te: te.matmul(
                ps_warm[:], ident_bf[:], ident_bf[:], start=True, stop=True))

        for e in range(8):
            b.dma("scalar", b1cols[:, e, :], b1c_in[e], seq=False, set_last=False)
            b.dma("scalar", b3cols[:, e, :], b3c_in[e], seq=False, set_last=False)
        b.dma("scalar", b1cols[:, 8, :], sb1c_in, seq=False, set_last=False)
        ev_bias = b.dma("scalar", b3cols[:, 8, :], sb3c_in, seq=False, set_last=False)

        cast_ev = {}

        def cast(name, dst, src, grp):
            ch = b.ch(grp)
            b.dma("gpsimd", dst, src, ch=ch, seq=False, set_last=False)
            cast_ev[name] = (ch, None)

        cast("phiwt", phiwt_bf, phiwt_in, "dc0")
        cast("wqt", wqt_bf, wqt_in, "dc0"); cast("wkt", wkt_bf, wkt_in, "dc0")
        cast("wvt", wvt_bf, wvt_in, "dc0"); cast("wot", wot_bf, wot_in, "dc0")
        ch_g1 = b.ch("g1")
        b.ops["gpsimd"].append(lambda E, s=ch_g1.sem: E.wait_ge(s, 1))
        import os as _os
        _no_c2 = bool(_os.environ.get("MK_NO_CAST2"))
        for e in range(8):
            cast(f"w1_{e}", w1t_bf[e], w1t_in[e], "dc2")
            cast(f"w3_{e}", w3t_bf[e], w3t_in[e], "dc2")
            cast(f"w2_{e}", w2t_bf[e], w2t_in[e], "dc2")
        cast("sw1", sw1t_bf, sw1t_in, "dc1"); cast("sw3", sw3t_bf, sw3t_in, "dc1")
        cast("sw2", sw2t_bf, sw2t_in, "dc1")
        cast_ev = {k: (ch, ch.n) for k, (ch, _) in cast_ev.items()}

        def load_wT(buf, src_bf, ev, cols, q="sync"):
            return b.dma(q, buf[:, :, 0:cols],
                         src_bf.rearrange("(c p) f -> p c f", p=128),
                         waits=(ev,))

        _lstacks = []

        def _close_lstacks():
            for cname in ("cc1", "cc2"):
                if cname in b.chans and b.chans[cname].n > 0:
                    b.ops["gpsimd"].append(
                        lambda E, s=b.chans[cname].sem: E.wait_ge(s, 1))
            for st in reversed(_lstacks):
                st.close()

        es_l0b = ExitStack(); _lstacks.append(es_l0b)
        xn_fm = es_l0b.enter_context(nc.sbuf_tensor("xn_fm", [128, DCH, 2048], BF16))
        es_l0a = ExitStack(); _lstacks.append(es_l0a)

        pes = ExitStack()
        xall = pes.enter_context(nc.sbuf_tensor("xall", [128, T, 1024], F32, **R))
        xn_tm = pes.enter_context(nc.sbuf_tensor("xn_tm", [128, T, 1024], BF16, **R))
        p0sc = pes.enter_context(nc.sbuf_tensor("p0sc", [128, 1024], F32, **R))
        p0st = pes.enter_context(nc.sbuf_tensor("p0st", [128, T], F32, **R))
        p0st2 = pes.enter_context(nc.sbuf_tensor("p0st2", [128, T], F32, **R))
        with pes:
            b.dma("sync", xall[:], x_in.rearrange("(t p) d -> p t d", p=128))
            b.step("vector", lambda v: v.tensor_reduce(p0st[:], xall[:], AX.X, ALU.add))
            b.step("vector", lambda v: v.tensor_scalar(p0st2[:], p0st[:], 1.0 / 1024, None, ALU.mult))
            b.step("vector", lambda v: v.scalar_tensor_tensor(
                xall[:], p0st2[:].unsqueeze(2).broadcast_to([128, T, 1024]), -1.0,
                xall[:], ALU.mult, ALU.add))
            for t in range(T):
                b.step("scalar", lambda s, t=t: s.activation(
                    p0sc[:], xall[:, t, :], AF.Square, accum_out=p0st[:, t:t + 1]))
            b.step("vector", lambda v: v.tensor_scalar(
                p0st[:], p0st[:], 1.0 / 1024, 1e-5, ALU.mult, ALU.add))
            b.step("scalar", lambda s: s.activation(p0st2[:], p0st[:], AF.Sqrt))
            b.step("vector", lambda v: v.reciprocal(p0st[:], p0st2[:]))
            b.step("vector", lambda v: v.tensor_tensor(
                xn_tm[:], xall[:], p0st[:].unsqueeze(2).broadcast_to([128, T, 1024]),
                ALU.mult))
            b.dma("sync", xn_tm_d.rearrange("(t p) d -> p t d", p=128), xn_tm[:])
            for c in range(DCH):
                b.dma("sync", xn_fm[:, c, :], xn_tm_d[:, 128 * c:128 * (c + 1)],
                      transpose=True)
            if "dbg_xn" in dbg_outs:
                b.dma("sync", dbg_outs["dbg_xn"], xn_tm_d)
        if upto < 1:
            ch_g1.n += 1
            b.ops["sync"].append(lambda E, s=ch_g1.sem: E.nop().then_inc(s, 1))
            _close_lstacks(); b.emit(block); return nc

        D_all = es_l0a.enter_context(nc.sbuf_tensor("D_all", [128, T, 2048], BF16))
        pes = ExitStack()
        phiwT = pes.enter_context(nc.sbuf_tensor("phiwT", [128, DCH, 2048], BF16, **R))
        phib_r = pes.enter_context(nc.sbuf_tensor("phib_r", [1, 2048], F32, **R))
        E_sc = [pes.enter_context(nc.sbuf_tensor(f"E_sc{p}", [128, 2048], F32, **R)) for p in range(2)]
        C_sc = [pes.enter_context(nc.sbuf_tensor(f"C_sc{p}", [128, 2048], BF16, **R)) for p in range(2)]
        sd_sc = [pes.enter_context(nc.sbuf_tensor(f"sd_sc{p}", [128, 64], F32, **R)) for p in range(2)]
        sc_sc = [pes.enter_context(nc.sbuf_tensor(f"sc_sc{p}", [128, 32], F32, **R)) for p in range(2)]
        ps1 = [pes.enter_context(nc.psum_tensor(f"ps1_{j}", [128, 512], F32)) for j in range(4)]
        with pes:
            load_wT(phiwT, phiwt_bf, cast_ev["phiwt"], 2048)
            b.dma("sync", phib_r[:], phib_in.unsqueeze(0))
            p1_entry = b.last
            ev_exp = {}
            ev_dC = {}
            ev_dma = {}
            for t in range(T):
                par = t % 2
                for ebk in range(4):
                    def mm_logits(te, t=t, ebk=ebk):
                        ins = te.matmul(ps1[ebk][:], ones_f[0:1, 0:128],
                                        phib_r[0:1, 512 * ebk:512 * (ebk + 1)],
                                        start=True, stop=False)
                        for c in range(DCH):
                            ins = te.matmul(ps1[ebk][:], xn_fm[:, c, 128 * t:128 * (t + 1)],
                                            phiwT[:, c, 512 * ebk:512 * (ebk + 1)],
                                            start=False, stop=(c == DCH - 1))
                        return ins
                    mw = [ev_exp[(t - 1, ebk)]] if (t - 1, ebk) in ev_exp else [p1_entry]
                    ev_mm = b.step("tensor", mm_logits, waits=tuple(mw),
                                   seq=False, set_last=False)
                    xw = [ev_mm]
                    if (t - 2) in ev_dC:
                        xw.append(ev_dC[t - 2])
                    ev_exp[(t, ebk)] = b.step("scalar", lambda s, ebk=ebk, par=par: s.activation(
                        E_sc[par][:, 512 * ebk:512 * (ebk + 1)], ps1[ebk][:], AF.Exp),
                        waits=tuple(xw), seq=False, set_last=False)
                e3 = E_sc[par][:].rearrange("p (n q) -> p n q", q=32)
                def red2(v, e3=e3, par=par):
                    v.tensor_reduce(sd_sc[par][:], e3, AX.X, ALU.add)
                    return v.tensor_reduce(sc_sc[par][:], e3.transpose([0, 2, 1]),
                                           AX.X, ALU.add)
                ev_red = b.step("vector", red2, waits=(ev_exp[(t, 3)],),
                                seq=False, set_last=False)
                def rec2(v, par=par):
                    v.reciprocal(sd_sc[par][:], sd_sc[par][:])
                    return v.reciprocal(sc_sc[par][:], sc_sc[par][:])
                ev_rec = b.step("vector", rec2, waits=(ev_red,), seq=False, set_last=False)
                ev_dD = b.step("vector", lambda v, t=t, e3=e3, par=par: v.tensor_tensor(
                    D_all[:, t, :].rearrange("p (n q) -> p n q", q=32), e3,
                    sd_sc[par][:].unsqueeze(2).broadcast_to([128, 64, 32]), ALU.mult),
                    waits=(ev_rec,), seq=False, set_last=False)
                dw_ = [ev_rec] if (t - 2) not in ev_dma else [ev_rec, ev_dma[t - 2]]
                ev_dC[t] = b.step("vector", lambda v, e3=e3, par=par: v.tensor_tensor(
                    C_sc[par][:].rearrange("p (n q) -> p n q", q=32), e3,
                    sc_sc[par][:].unsqueeze(1).broadcast_to([128, 64, 32]), ALU.mult),
                    waits=tuple(dw_), seq=False, set_last=False)
                ev_dma[t] = b.dma("sync", ctm_d[128 * t:128 * (t + 1), :], C_sc[par][:],
                                  waits=(ev_dC[t],), seq=False, set_last=False)
            b.last = ev_dma[T - 1]
            ch_g1.n += 1
            b.ops["sync"].append(lambda E, s=ch_g1.sem: E.nop().then_inc(s, 1))
            if "dbg_D" in dbg_outs:
                b.dma("sync", dbg_outs["dbg_D"].rearrange("(t p) e -> p t e", p=128), D_all[:])
            if "dbg_C" in dbg_outs:
                b.dma("sync", dbg_outs["dbg_C"], ctm_d)
        if upto < 2:
            _close_lstacks(); b.emit(block); return nc

        pes = ExitStack()
        xtst = pes.enter_context(nc.sbuf_tensor("xtst", [128, 1024], F32, **R))
        xn_tm2 = pes.enter_context(nc.sbuf_tensor("xn_tm2", [128, T, 1024], BF16, **R))
        ps2 = pes.enter_context(nc.psum_tensor("ps2", [128, 512], F32))
        with pes:
            b.dma("sync", xn_tm2[:], xn_tm_d.rearrange("(t p) d -> p t d", p=128))
            for npc in range(T):
                for dh in range(2):
                    def mm_xt(te, npc=npc, dh=dh):
                        for t in range(T):
                            ins = te.matmul(ps2[:], D_all[:, t, 128 * npc:128 * (npc + 1)],
                                            xn_tm2[:, t, 512 * dh:512 * (dh + 1)],
                                            start=(t == 0), stop=(t == T - 1))
                        return ins
                    b.step("tensor", mm_xt)
                    b.step("scalar", lambda s, dh=dh: s.activation(
                        xtst[:, 512 * dh:512 * (dh + 1)], ps2[:], AF.Copy))
                b.dma("sync", xt_part[128 * npc:128 * (npc + 1), :], xtst[:])
            b.step("gpsimd", lambda g: g.collective_compute(
                "ReduceScatter", ALU.add, replica_groups=[list(range(8))],
                ins=[xt_part.opt()], outs=[xt_red.opt()]), ch=b.ch("cc1"),
                set_last=False)
            if "dbg_xtpart" in dbg_outs:
                b.dma("sync", dbg_outs["dbg_xtpart"], xt_part)
            if "dbg_xtred" in dbg_outs:
                b.dma("sync", dbg_outs["dbg_xtred"], xt_red)
        es_l0a.close()
        if upto < 3:
            _close_lstacks(); b.emit(block); return nc

        pes = ExitStack()
        woT = pes.enter_context(nc.sbuf_tensor("woT", [128, DCH, 1024], BF16, **R))
        Qfm = pes.enter_context(nc.sbuf_tensor("Qfm", [128, DCH, 2048], BF16, **R))
        Kfm = pes.enter_context(nc.sbuf_tensor("Kfm", [128, DCH, 2048], BF16, **R))
        Vtm = pes.enter_context(nc.sbuf_tensor("Vtm", [128, T, 1024], BF16, **R))
        S_sc = pes.enter_context(nc.sbuf_tensor("S_sc", [128, 2048], F32, **R))
        Pb_sc = pes.enter_context(nc.sbuf_tensor("Pb_sc", [128, 2048], BF16, **R))
        PbT = pes.enter_context(nc.sbuf_tensor("PbT", [128, T, 128], BF16, **R))
        avtm = pes.enter_context(nc.sbuf_tensor("avtm", [128, 1024], BF16, **R))
        avfm = pes.enter_context(nc.sbuf_tensor("avfm", [128, DCH, 128], BF16, **R))
        stile = pes.enter_context(nc.sbuf_tensor("stile", [128, 1024], F32, **R))
        m3 = pes.enter_context(nc.sbuf_tensor("m3", [128, 1], F32, **R))
        l3 = pes.enter_context(nc.sbuf_tensor("l3", [128, 1], F32, **R))
        rl3 = pes.enter_context(nc.sbuf_tensor("rl3", [128, 1], F32, **R))
        S4 = pes.enter_context(nc.psum_tensor("S4", [128, 2048], F32))
        pt4 = pes.enter_context(nc.psum_tensor("pt4", [128, 4, 128], BF16))
        ps3av = pes.enter_context(nc.psum_tensor("ps3av", [128, 1024], F32))
        ps3s = pes.enter_context(nc.psum_tensor("ps3s", [128, 512], F32))
        with pes:
            load_wT(woT, wot_bf, cast_ev["wot"], 1024)
            ev_qev = {}
            def qkv_group(idx, fn_mm, fn_ev):
                bank = idx % 4
                mw = [ev_qev[idx - 4]] if idx - 4 in ev_qev else [b.last]
                ev_mm = b.step("tensor", lambda te, fn_mm=fn_mm, bank=bank: fn_mm(te, S4[:, 512 * bank:512 * (bank + 1)]),
                               waits=tuple(mw), seq=False, set_last=False)
                ev_qev[idx] = b.step("scalar", lambda s, fn_ev=fn_ev, bank=bank: fn_ev(s, S4[:, 512 * bank:512 * (bank + 1)]),
                                     waits=(ev_mm,), seq=False, set_last=False)
            qes = ExitStack()
            wqk = qes.enter_context(nc.sbuf_tensor("wqk", [128, DCH, 1024], BF16, **R))
            with qes:
                idx = 0
                for (dst, wsrc, wev) in ((Qfm, wqt_bf, "wqt"), (Kfm, wkt_bf, "wkt")):
                    lw = [ev_qev[idx - 1]] if idx - 1 in ev_qev else []
                    b.dma("sync", wqk[:],
                          wsrc.rearrange("(c p) f -> p c f", p=128),
                          waits=tuple([cast_ev[wev]] + lw))
                    ld_ev = b.last
                    for fc in range(DCH):
                        for tb in range(4):
                            def fmm(te, ps, fc=fc, tb=tb):
                                for c in range(DCH):
                                    ins = te.matmul(
                                        ps, wqk[:, c, 128 * fc:128 * (fc + 1)],
                                        xn_fm[:, c, 512 * tb:512 * (tb + 1)],
                                        start=(c == 0), stop=(c == DCH - 1))
                                return ins
                            def fev(s, ps, dst=dst, fc=fc, tb=tb):
                                return s.activation(dst[:, fc, 512 * tb:512 * (tb + 1)],
                                                    ps, AF.Copy)
                            bank = idx % 4
                            mw = [ld_ev] + ([ev_qev[idx - 4]] if idx - 4 in ev_qev else [])
                            ev_mm = b.step("tensor", lambda te, fmm=fmm, bank=bank: fmm(te, S4[:, 512 * bank:512 * (bank + 1)]),
                                           waits=tuple(mw), seq=False, set_last=False)
                            ev_qev[idx] = b.step("scalar", lambda s, fev=fev, bank=bank: fev(s, S4[:, 512 * bank:512 * (bank + 1)]),
                                                 waits=(ev_mm,), seq=False, set_last=False)
                            idx += 1
            ves = ExitStack()
            wvT = ves.enter_context(nc.sbuf_tensor("wvT", [128, DCH, 1024], BF16, **R))
            with ves:
                lw = [ev_qev[idx - 1]]
                b.dma("sync", wvT[:], wvt_bf.rearrange("(c p) f -> p c f", p=128),
                      waits=tuple([cast_ev["wvt"]] + lw), seq=False, set_last=False)
                ld_ev = (b.ch("d_sync"), b.ch("d_sync").n)
                for t in range(T):
                    for fh in range(2):
                        def fmm(te, ps, t=t, fh=fh):
                            for c in range(DCH):
                                ins = te.matmul(ps, xn_fm[:, c, 128 * t:128 * (t + 1)],
                                                wvT[:, c, 512 * fh:512 * (fh + 1)],
                                                start=(c == 0), stop=(c == DCH - 1))
                            return ins
                        def fev(s, ps, t=t, fh=fh):
                            return s.activation(Vtm[:, t, 512 * fh:512 * (fh + 1)],
                                                ps, AF.Copy)
                        bank = idx % 4
                        mw = [ld_ev] + ([ev_qev[idx - 4]] if idx - 4 in ev_qev else [])
                        ev_mm = b.step("tensor", lambda te, fmm=fmm, bank=bank: fmm(te, S4[:, 512 * bank:512 * (bank + 1)]),
                                       waits=tuple(mw), seq=False, set_last=False)
                        ev_qev[idx] = b.step("scalar", lambda s, fev=fev, bank=bank: fev(s, S4[:, 512 * bank:512 * (bank + 1)]),
                                             waits=(ev_mm,), seq=False, set_last=False)
                        idx += 1
            ev_qkv_done = ev_qev[idx - 1]
            es_l0b.close()
            ev_sevac = ev_qkv_done
            ev_ptlast = None
            ev_avmm_prev = None
            ev_avevac_prev = None
            ev_avtlast = None
            ev_wolast = None
            ev_sdma_prev = None
            for qi in range(T):
                klen = 128 * (qi + 1)
                nkb = (klen + 511) // 512
                def mm_s(te, qi=qi, klen=klen, nkb=nkb):
                    ins = None
                    for kb in range(nkb):
                        kw = min(512, klen - 512 * kb)
                        for c in range(DCH):
                            ins = te.matmul(S4[:, 512 * kb:512 * kb + kw],
                                            Qfm[:, c, 128 * qi:128 * (qi + 1)],
                                            Kfm[:, c, 512 * kb:512 * kb + kw],
                                            start=(c == 0), stop=(c == DCH - 1))
                    return ins
                ev_smm = b.step("tensor", mm_s, waits=(ev_sevac,), seq=False, set_last=False)
                def sev(s, klen=klen, nkb=nkb):
                    ins = None
                    for kb in range(nkb):
                        kw = min(512, klen - 512 * kb)
                        ins = s.activation(S_sc[:, 512 * kb:512 * kb + kw],
                                           S4[:, 512 * kb:512 * kb + kw], AF.Copy)
                    return ins
                ev_sevac = b.step("scalar", sev, waits=(ev_smm,), seq=False, set_last=False)
                def vmax(v, klen=klen):
                    v.tensor_tensor(S_sc[:, klen - 128:klen], S_sc[:, klen - 128:klen],
                                    maskneg[:], ALU.add)
                    v.tensor_reduce(m3[:], S_sc[:, 0:klen], AX.X, ALU.max, negate=True)
                    return v.tensor_scalar(m3[:], m3[:], 1.0 / 32.0, None, ALU.mult)
                vw = [ev_sevac] + ([ev_avevac_prev] if ev_avevac_prev else [])
                ev_vmax = b.step("vector", vmax, waits=tuple(vw), seq=False, set_last=False)
                xw = [ev_vmax] + ([ev_ptlast] if ev_ptlast else [])
                ev_exp = b.step("scalar", lambda s, klen=klen: s.activation(
                    Pb_sc[:, 0:klen], S_sc[:, 0:klen], AF.Exp, bias=m3[:],
                    scale=1.0 / 32.0, accum_out=l3[:]),
                    waits=tuple(xw), seq=False, set_last=False)
                ev_rec = b.step("vector", lambda v: v.reciprocal(rl3[:], l3[:]),
                                waits=(ev_exp,), seq=False, set_last=False)
                ev_ptc = None
                for j0 in range(0, qi + 1, 4):
                    nj = min(4, qi + 1 - j0)
                    def ptb(te, j0=j0, nj=nj):
                        ins = None
                        for j in range(nj):
                            ins = te.transpose(pt4[:, j, :],
                                               Pb_sc[:, 128 * (j0 + j):128 * (j0 + j + 1)],
                                               ident_bf[:])
                        return ins
                    pw = [ev_exp] + ([ev_ptc] if ev_ptc else ([ev_avtlast] if ev_avtlast else []))
                    ev_ptx = b.step("tensor", ptb, waits=tuple(pw), seq=False, set_last=False)
                    ev_ptc = b.step("vector", lambda v, j0=j0, nj=nj: v.tensor_copy(
                        PbT[:, j0:j0 + nj, :], pt4[:, 0:nj, :]),
                        waits=(ev_ptx,), seq=False, set_last=False)
                ev_ptlast = ev_ptx
                def mm_av(te, qi=qi):
                    ins = None
                    for dh in range(2):
                        for kt in range(qi + 1):
                            ins = te.matmul(ps3av[:, 512 * dh:512 * (dh + 1)],
                                            PbT[:, kt, :],
                                            Vtm[:, kt, 512 * dh:512 * (dh + 1)],
                                            start=(kt == 0), stop=(kt == qi))
                    return ins
                aw = [ev_ptc] + ([ev_avevac_prev] if ev_avevac_prev else [])
                ev_avmm = b.step("tensor", mm_av, waits=tuple(aw), seq=False, set_last=False)
                def avev(s):
                    s.activation(avtm[:, 0:512], ps3av[:, 0:512], AF.Copy, scale=rl3[:])
                    return s.activation(avtm[:, 512:1024], ps3av[:, 512:1024], AF.Copy,
                                        scale=rl3[:])
                aew = [ev_avmm, ev_rec] + ([ev_avtlast] if ev_avtlast else [])
                ev_avevac_prev = b.step("scalar", avev, waits=tuple(aew),
                                        seq=False, set_last=False)
                ev_avtc = None
                for j0 in range(0, DCH, 4):
                    def avt(te, j0=j0):
                        ins = None
                        for j in range(4):
                            ins = te.transpose(pt4[:, j, :],
                                               avtm[:, 128 * (j0 + j):128 * (j0 + j + 1)],
                                               ident_bf[:])
                        return ins
                    tw = [ev_avevac_prev] + ([ev_avtc] if ev_avtc else [])
                    ev_avtx = b.step("tensor", avt, waits=tuple(tw), seq=False, set_last=False)
                    ev_avtc = b.step("vector", lambda v, j0=j0: v.tensor_copy(
                        avfm[:, j0:j0 + 4, :], pt4[:]),
                        waits=(ev_avtx,), seq=False, set_last=False)
                ev_avtlast = ev_avtc
                ev_wo_prev = None
                for fh in range(2):
                    def mm_so(te, fh=fh):
                        for c in range(DCH):
                            ins = te.matmul(ps3s[:], avfm[:, c, :],
                                            woT[:, c, 512 * fh:512 * (fh + 1)],
                                            start=(c == 0), stop=(c == DCH - 1))
                        return ins
                    ww = [ev_avtc] + ([ev_wo_prev] if ev_wo_prev else ([ev_wolast] if ev_wolast else []))
                    ev_womm = b.step("tensor", mm_so, waits=tuple(ww), seq=False, set_last=False)
                    sw2_ = [ev_womm] + ([ev_sdma_prev] if ev_sdma_prev else [])
                    ev_wo_prev = b.step("scalar", lambda s, fh=fh: s.activation(
                        stile[:, 512 * fh:512 * (fh + 1)], ps3s[:], AF.Copy),
                        waits=tuple(sw2_), seq=False, set_last=False)
                ev_wolast = ev_wo_prev
                ev_sdma_prev = b.dma("sync", s_d[128 * qi:128 * (qi + 1), :], stile[:],
                                     waits=(ev_wo_prev,), seq=False, set_last=False)
            b.last = ev_sdma_prev
            if "dbg_s" in dbg_outs:
                b.dma("sync", dbg_outs["dbg_s"], s_d)
        if upto < 4:
            _close_lstacks(); b.emit(block); return nc

        pes = ExitStack()
        xts = pes.enter_context(nc.sbuf_tensor("xts", [128, 2, 1024], F32, **R))
        p4sc = pes.enter_context(nc.sbuf_tensor("p4sc", [128, 1024], F32, **R))
        p4a = pes.enter_context(nc.sbuf_tensor("p4a", [128, 2], F32, **R))
        p4b = pes.enter_context(nc.sbuf_tensor("p4b", [128, 2], F32, **R))
        xnt = pes.enter_context(nc.sbuf_tensor("xnt", [128, 2, 1024], BF16, **R))
        xttfm = pes.enter_context(nc.sbuf_tensor("xttfm", [128, DCH, 256], BF16, **R))
        ewb = pes.enter_context(nc.sbuf_tensor("ewb", [128, DCH, 2048], BF16, **R))
        ebr = pes.enter_context(nc.sbuf_tensor("ebr", [1, 1024], F32, **R))
        ytst = pes.enter_context(nc.sbuf_tensor("ytst", [32, 512], BF16, **R))
        ps4 = pes.enter_context(nc.psum_tensor("ps4", [32, 512], F32))
        with pes:
            ev_ew = [None] * 8
            ev_mm4 = [None] * 8
            p4_start = b.last

            def load_ew(n):
                w = (ev_mm4[n - 2],) if n >= 2 else (p4_start,)
                ev_ew[n] = b.dma(
                    "gpsimd", ewb[:, :, 1024 * (n % 2):1024 * (n % 2) + 1024],
                    ew_in[n].rearrange("(c p) h -> p c h", p=128),
                    ch=b.ch(f"dew{n % 2}"), waits=w, seq=False, set_last=False)
            load_ew(0); load_ew(1)
            b.dma("sync", xts[:], xt_red.rearrange("(t p) d -> p t d", p=128),
                  waits=((b.ch("cc1"), 1),))
            b.step("vector", lambda v: v.tensor_scalar(
                xts[:], xts[:], 65000.0, -33000.0, ALU.min, ALU.max))
            b.step("vector", lambda v: v.tensor_reduce(p4a[:], xts[:], AX.X, ALU.add))
            b.step("vector", lambda v: v.tensor_scalar(p4b[:], p4a[:], 1.0 / 1024, None, ALU.mult))
            b.step("vector", lambda v: v.scalar_tensor_tensor(
                xts[:], p4b[:].unsqueeze(2).broadcast_to([128, 2, 1024]), -1.0,
                xts[:], ALU.mult, ALU.add))
            for t in range(2):
                b.step("scalar", lambda s, t=t: s.activation(
                    p4sc[:], xts[:, t, :], AF.Square, accum_out=p4a[:, t:t + 1]))
            b.step("vector", lambda v: v.tensor_scalar(
                p4a[:], p4a[:], 1.0 / 1024, 1e-5, ALU.mult, ALU.add))
            b.step("scalar", lambda s: s.activation(p4b[:], p4a[:], AF.Sqrt))
            b.step("vector", lambda v: v.reciprocal(p4a[:], p4b[:]))
            b.step("vector", lambda v: v.tensor_tensor(
                xts[:], xts[:], p4a[:].unsqueeze(2).broadcast_to([128, 2, 1024]),
                ALU.mult))
            b.step("scalar", lambda s: s.activation(xnt[:], xts[:], AF.Tanh))
            b.dma("sync", xtt_d.rearrange("(t p) d -> p t d", p=128), xnt[:])
            for c in range(DCH):
                b.dma("sync", xttfm[:, c, :], xtt_d[:, 128 * c:128 * (c + 1)],
                      transpose=True)
            for n in range(8):
                b.dma("sync", ebr[:], eb_in[n].unsqueeze(0))
                for hh in range(2):
                    def mm_yt(te, n=n, hh=hh):
                        ins = te.matmul(ps4[:], ones_f[0:1, 0:32],
                                        ebr[0:1, 512 * hh:512 * (hh + 1)],
                                        start=True, stop=False)
                        for c in range(DCH):
                            ins = te.matmul(
                                ps4[:], xttfm[:, c, 32 * n:32 * (n + 1)],
                                ewb[:, c, 1024 * (n % 2) + 512 * hh:1024 * (n % 2) + 512 * (hh + 1)],
                                start=False, stop=(c == DCH - 1))
                        return ins
                    ev_mm4[n] = b.step("tensor", mm_yt, waits=(ev_ew[n],))
                    if n == 1 and hh == 0 and "dbg_ewb" in dbg_outs:
                        b.dma("sync", dbg_outs["dbg_ewb"], ewb[:].rearrange("p c n -> p (c n)"))
                    b.step("scalar", lambda s: s.activation(ytst[:], ps4[:], AF.Copy))
                    b.dma("sync", ytl_d[32 * n:32 * (n + 1), 512 * hh:512 * (hh + 1)], ytst[:])
                if n + 2 < 8:
                    load_ew(n + 2)
            b.step("gpsimd", lambda g: g.collective_compute(
                "AllGather", ALU.bypass, replica_groups=[list(range(8))],
                ins=[ytl_d.opt()], outs=[yt_all.opt()]), ch=b.ch("cc2"))
            if "dbg_xttfm" in dbg_outs:
                b.dma("sync", dbg_outs["dbg_xttfm"], xttfm[:].rearrange("p c n -> p (c n)"))
            if "dbg_xtt" in dbg_outs:
                b.dma("sync", dbg_outs["dbg_xtt"], xtt_d)
            if "dbg_ytl" in dbg_outs:
                b.dma("sync", dbg_outs["dbg_ytl"], ytl_d)
            if "dbg_ytall" in dbg_outs:
                b.dma("sync", dbg_outs["dbg_ytall"], yt_all)
        if upto < 5:
            _close_lstacks(); b.emit(block); return nc

        pes = ExitStack()
        Cfm = pes.enter_context(nc.sbuf_tensor("Cfm", [128, T, 2048], BF16, **R))
        ytsb = pes.enter_context(nc.sbuf_tensor("ytsb", [128, T, 1024], BF16, **R))
        gt5 = [pes.enter_context(nc.sbuf_tensor(f"gt5{p}", [128, 1024], F32, **R)) for p in range(2)]
        xt5 = [pes.enter_context(nc.sbuf_tensor(f"xt5{p}", [128, 1024], F32, **R)) for p in range(2)]
        st5 = [pes.enter_context(nc.sbuf_tensor(f"st5{p}", [128, 1024], F32, **R)) for p in range(2)]
        ps5 = [pes.enter_context(nc.psum_tensor(f"ps5{p}", [128, 512], F32)) for p in range(2)]
        with pes:
            b.dma("sync", ytsb[:], yt_all.rearrange("(c p) h -> p c h", p=128),
                  waits=((b.ch("cc2"), 1),))
            for npc in range(T):
                b.dma("sync", Cfm[:, npc, :], ctm_d[:, 128 * npc:128 * (npc + 1)],
                      transpose=True)
            ev_cfm = b.last
            ev_silu = {}; ev_tt2 = {}; ev_out = {}; ev_ldx = {}; ev_lds = {}
            for t in range(T):
                par = t % 2
                wl = (ev_out[t - 2],) if t - 2 in ev_out else (ev_cfm,)
                ev_ldx[t] = b.dma("sync", xt5[par][:], x_in[128 * t:128 * (t + 1), :],
                                  waits=wl, seq=False, set_last=False)
                ev_lds[t] = b.dma("scalar", st5[par][:], s_d[128 * t:128 * (t + 1), :],
                                  waits=wl, seq=False, set_last=False)
                for hh in range(2):
                    mw = [ev_cfm]
                    if (t - 1, hh) in ev_silu:
                        mw.append(ev_silu[(t - 1, hh)])
                    def mm_y(te, t=t, hh=hh):
                        for npc in range(T):
                            ins = te.matmul(ps5[hh][:], Cfm[:, npc, 128 * t:128 * (t + 1)],
                                            ytsb[:, npc, 512 * hh:512 * (hh + 1)],
                                            start=(npc == 0), stop=(npc == T - 1))
                        return ins
                    ev_mm = b.step("tensor", mm_y, waits=tuple(mw), seq=False, set_last=False)
                    sw = [ev_mm]
                    if t - 2 in ev_tt2:
                        sw.append(ev_tt2[t - 2])
                    ev_silu[(t, hh)] = b.step("scalar", lambda s, hh=hh, par=par: s.activation(
                        gt5[par][:, 512 * hh:512 * (hh + 1)], ps5[hh][:], AF.Silu),
                        waits=tuple(sw), seq=False, set_last=False)
                b.step("vector", lambda v, par=par: v.tensor_tensor(
                    gt5[par][:], gt5[par][:], st5[par][:], ALU.mult),
                    waits=(ev_silu[(t, 1)], ev_lds[t]), seq=False, set_last=False)
                ev_tt2[t] = b.step("vector", lambda v, par=par: v.tensor_tensor(
                    xt5[par][:], xt5[par][:], gt5[par][:], ALU.add),
                    waits=(ev_ldx[t],), seq=False, set_last=False)
                ev_out[t] = b.dma("sync", x1_d[128 * t:128 * (t + 1), :], xt5[par][:],
                                  waits=(ev_tt2[t],), seq=False, set_last=False)
            b.last = ev_out[T - 1]
            if "dbg_x1" in dbg_outs:
                b.dma("sync", dbg_outs["dbg_x1"], x1_d)
        if upto < 6:
            _close_lstacks(); b.emit(block); return nc

        es_l1 = ExitStack(); _lstacks.append(es_l1)
        xn1_fm = es_l1.enter_context(nc.sbuf_tensor("xn1_fm", [128, DCH, 2048], BF16))
        pes = ExitStack()
        x1all = pes.enter_context(nc.sbuf_tensor("x1all", [128, T, 1024], F32, **R))
        p6sc = pes.enter_context(nc.sbuf_tensor("p6sc", [128, 1024], F32, **R))
        p6a = pes.enter_context(nc.sbuf_tensor("p6a", [128, T], F32, **R))
        p6b = pes.enter_context(nc.sbuf_tensor("p6b", [128, T], F32, **R))
        xn1tm = pes.enter_context(nc.sbuf_tensor("xn1tm", [128, T, 1024], BF16, **R))
        xn1lo = pes.enter_context(nc.sbuf_tensor("xn1lo", [128, T, 1024], BF16, **R))
        with pes:
            b.dma("sync", x1all[:], x1_d.rearrange("(t p) d -> p t d", p=128))
            b.step("vector", lambda v: v.tensor_reduce(p6a[:], x1all[:], AX.X, ALU.add))
            b.step("vector", lambda v: v.tensor_scalar(p6b[:], p6a[:], 1.0 / 1024, None, ALU.mult))
            b.step("vector", lambda v: v.scalar_tensor_tensor(
                x1all[:], p6b[:].unsqueeze(2).broadcast_to([128, T, 1024]), -1.0,
                x1all[:], ALU.mult, ALU.add))
            for t in range(T):
                b.step("scalar", lambda s, t=t: s.activation(
                    p6sc[:], x1all[:, t, :], AF.Square, accum_out=p6a[:, t:t + 1]))
            b.step("vector", lambda v: v.tensor_scalar(
                p6a[:], p6a[:], 1.0 / 1024, 1e-5, ALU.mult, ALU.add))
            b.step("scalar", lambda s: s.activation(p6b[:], p6a[:], AF.Sqrt))
            b.step("vector", lambda v: v.reciprocal(p6a[:], p6b[:]))
            b.step("vector", lambda v: v.tensor_tensor(
                x1all[:], x1all[:], p6a[:].unsqueeze(2).broadcast_to([128, T, 1024]),
                ALU.mult))
            b.step("vector", lambda v: v.tensor_copy(xn1tm[:], x1all[:]))
            b.step("vector", lambda v: v.tensor_tensor(
                xn1lo[:], x1all[:], xn1tm[:], ALU.subtract))
            b.dma("sync", xn1_tm_d.rearrange("(t p) d -> p t d", p=128), xn1tm[:])
            b.dma("sync", xn1lo_d.rearrange("(t p) d -> p t d", p=128), xn1lo[:])
            for c in range(DCH):
                b.dma("sync", xn1_fm[:, c, :], xn1_tm_d[:, 128 * c:128 * (c + 1)],
                      transpose=True)
            if "dbg_xn1" in dbg_outs:
                b.dma("sync", dbg_outs["dbg_xn1"], xn1_tm_d)
        if upto < 7:
            _close_lstacks(); b.emit(block); return nc

        es_ff = ExitStack(); _lstacks.append(es_ff)
        xn1_ff = es_ff.enter_context(nc.sbuf_tensor("xn1_ff", [128, DCH, 2048], F32, **R))
        pes = ExitStack()
        wt1 = pes.enter_context(nc.sbuf_tensor("wt1", [128, DCH, 192], F32, **R))
        qfm1 = pes.enter_context(nc.sbuf_tensor("qfm1", [64, 2048], F32, **R))
        kfm1 = pes.enter_context(nc.sbuf_tensor("kfm1", [64, 2048], F32, **R))
        vtm1 = pes.enter_context(nc.sbuf_tensor("vtm1", [128, T, 64], F32, **R))
        st1 = pes.enter_context(nc.sbuf_tensor("st1", [128, 2048], F32, **R))
        st1e = pes.enter_context(nc.sbuf_tensor("st1e", [128, 2048], F32, **R))
        pbt1 = pes.enter_context(nc.sbuf_tensor("pbt1", [128, 2048], F32, **R))
        m7 = pes.enter_context(nc.sbuf_tensor("m7", [128, 1], F32, **R))
        l7 = pes.enter_context(nc.sbuf_tensor("l7", [128, 1], F32, **R))
        ps7 = pes.enter_context(nc.psum_tensor("ps7", [128, 512], F32))
        ps7q = pes.enter_context(nc.psum_tensor("ps7q", [64, 512], F32))
        ps7av = pes.enter_context(nc.psum_tensor("ps7av", [128, 1024], F32))
        with pes:
            les = ExitStack()
            lo_fm = les.enter_context(nc.sbuf_tensor("lo_fm", [128, DCH, 2048], BF16, **R))
            with les:
                for c in range(DCH):
                    b.dma("sync", lo_fm[:, c, :], xn1lo_d[:, 128 * c:128 * (c + 1)],
                          transpose=True)
                b.step("vector", lambda v: v.tensor_tensor(
                    xn1_ff[:], xn1_fm[:], lo_fm[:], ALU.add))
            b.dma("sync", wt1[:], l1t_in.rearrange("(c p) f -> p c f", p=128))
            for tb in range(4):
                for (dst, woff) in ((qfm1, 0), (kfm1, 64)):
                    def mm_qk1(te, tb=tb, woff=woff):
                        for c in range(DCH):
                            ins = te.matmul(ps7q[:], wt1[:, c, woff:woff + 64],
                                            xn1_ff[:, c, 512 * tb:512 * (tb + 1)],
                                            start=(c == 0), stop=(c == DCH - 1))
                        return ins
                    b.step("tensor", mm_qk1)
                    b.step("scalar", lambda s, dst=dst, tb=tb: s.activation(
                        dst[0:64, 512 * tb:512 * (tb + 1)], ps7q[:], AF.Copy))
            for t in range(T):
                def mm_v1(te, t=t):
                    for c in range(DCH):
                        ins = te.matmul(ps7[:, 0:64],
                                        xn1_ff[:, c, 128 * t:128 * (t + 1)],
                                        wt1[:, c, 128:192],
                                        start=(c == 0), stop=(c == DCH - 1))
                    return ins
                b.step("tensor", mm_v1)
                b.step("scalar", lambda s, t=t: s.activation(vtm1[:, t, :], ps7[:, 0:64], AF.Copy))
            for kt in range(T):
                for ib in range(4):
                    b.step("tensor", lambda te, kt=kt, ib=ib: te.matmul(
                        ps7[:], kfm1[0:64, 128 * kt:128 * (kt + 1)],
                        qfm1[0:64, 512 * ib:512 * (ib + 1)], start=True, stop=True))
                    b.step("scalar", lambda s, ib=ib: s.activation(
                        st1[:, 512 * ib:512 * (ib + 1)], ps7[:], AF.Copy))
                b.step("vector", lambda v: v.tensor_reduce(
                    m7[:], st1[:], AX.X, ALU.max, negate=True))
                b.step("scalar", lambda s: s.activation(
                    st1e[:], st1[:], AF.Exp, bias=m7[:], accum_out=l7[:]))
                b.step("vector", lambda v: v.reciprocal(l7[:], l7[:]))
                b.step("vector", lambda v: v.tensor_scalar(
                    pbt1[:], st1e[:], l7[:], None, ALU.mult))
                if kt == 0 and "dbg_pbt" in dbg_outs:
                    b.dma("sync", dbg_outs["dbg_pbt"], pbt1[:])
                def mm_av1(te, kt=kt):
                    for ib in range(T):
                        ins = te.matmul(ps7av[:, 64 * ib:64 * (ib + 1)],
                                        pbt1[:, 128 * ib:128 * (ib + 1)], vtm1[:, kt, :],
                                        start=(kt == 0 and ib % 8 == 0),
                                        stop=(kt == T - 1))
                    return ins
                b.step("tensor", mm_av1)
            for ti2 in range(2):
                b.step("scalar", lambda s, ti2=ti2: s.activation(
                    s1_sb[:].rearrange("p t v -> p (t v)")[:, 512 * ti2:512 * (ti2 + 1)],
                    ps7av[:, 512 * ti2:512 * (ti2 + 1)], AF.Copy))
            if "dbg_v1" in dbg_outs:
                b.dma("sync", dbg_outs["dbg_v1"].rearrange("(t p) v -> p t v", p=128), vtm1[:])
            if "dbg_s1" in dbg_outs:
                b.dma("sync", dbg_outs["dbg_s1"].rearrange("(t p) v -> p t v", p=128), s1_sb[:])
        if upto < 8:
            _close_lstacks(); b.emit(block); return nc

        pes = ExitStack()
        gwT = pes.enter_context(nc.sbuf_tensor("gwT", [128, DCH, 8], F32, **R))
        gbrep = pes.enter_context(nc.sbuf_tensor("gbrep", [128, 8], F32, **R))
        gbrow = pes.enter_context(nc.sbuf_tensor("gbrow", [1, 8], F32, **R))
        r8a = pes.enter_context(nc.sbuf_tensor("r8a", [128, 16, 8], F32, **R))
        r8b = pes.enter_context(nc.sbuf_tensor("r8b", [128, 16, 8], F32, **R))
        r8c = pes.enter_context(nc.sbuf_tensor("r8c", [128, 16, 8], F32, **R))
        m8 = pes.enter_context(nc.sbuf_tensor("m8", [128, 16], F32, **R))
        m8b = pes.enter_context(nc.sbuf_tensor("m8b", [128, 16], F32, **R))
        ps8 = pes.enter_context(nc.psum_tensor("ps8", [128, 128], F32))
        ps8t = pes.enter_context(nc.psum_tensor("ps8t", [8, 128], F32))
        ps8u = pes.enter_context(nc.psum_tensor("ps8u", [8, 128], F32))
        with pes:
            b.dma("sync", gwT[:], gwt_in.rearrange("(c p) f -> p c f", p=128))
            b.dma("sync", gbrow[:], gb_in.unsqueeze(0))
            b.step("tensor", lambda te: te.matmul(
                ps8t[0:1, 0:8] if False else ps8[:, 0:8], ones_f[0:1, 0:128], gbrow[:],
                start=True, stop=True))
            b.step("scalar", lambda s: s.activation(gbrep[:], ps8[:, 0:8], AF.Copy))
            def mm_gall(te):
                ins = None
                for t in range(T):
                    for c in range(DCH):
                        ins = te.matmul(ps8[:, 8 * t:8 * (t + 1)],
                                        xn1_ff[:, c, 128 * t:128 * (t + 1)],
                                        gwT[:, c, :],
                                        start=(t == 0 and c == 0), stop=(c == DCH - 1))
                return ins
            b.step("tensor", mm_gall)
            b.step("scalar", lambda s: s.activation(
                r8a[:].rearrange("p t e -> p (t e)"), ps8[:], AF.Copy))
            bc = lambda x: x.unsqueeze(2).broadcast_to([128, T, 8])
            b.step("vector", lambda v: v.tensor_reduce(m8[:], r8a[:], AX.X, ALU.max, negate=True))
            b.step("vector", lambda v: v.tensor_tensor(r8c[:], r8a[:], bc(m8[:]), ALU.add))
            b.step("scalar", lambda s: s.activation(r8b[:], r8c[:], AF.Exp))
            b.step("vector", lambda v: v.tensor_reduce(m8[:], r8b[:], AX.X, ALU.add))
            b.step("vector", lambda v: v.reciprocal(m8[:], m8[:]))
            b.step("vector", lambda v: v.tensor_tensor(
                cw_sb[:], r8b[:], bc(m8[:]), ALU.mult))
            b.step("vector", lambda v: v.tensor_tensor(
                r8a[:], cw_sb[:], gbrep[:].unsqueeze(1).broadcast_to([128, T, 8]),
                ALU.add))
            b.step("vector", lambda v: v.tensor_reduce(m8[:], r8a[:], AX.X, ALU.max, negate=True))
            b.step("vector", lambda v: v.tensor_tensor(r8b[:], r8a[:], bc(m8[:]), ALU.add))
            b.step("vector", lambda v: v.tensor_scalar(
                r8b[:], r8b[:], 0.0, None, ALU.is_ge))
            b.step("vector", lambda v: v.scalar_tensor_tensor(
                r8c[:], r8b[:], NEG, r8a[:], ALU.mult, ALU.add))
            b.step("vector", lambda v: v.tensor_reduce(m8[:], r8c[:], AX.X, ALU.max, negate=True))
            b.step("vector", lambda v: v.tensor_tensor(r8b[:], r8a[:], bc(m8[:]), ALU.add))
            b.step("vector", lambda v: v.tensor_scalar(
                r8b[:], r8b[:], 0.0, None, ALU.is_ge))
            b.step("vector", lambda v: v.tensor_tensor(cw_sb[:], r8b[:], cw_sb[:], ALU.mult))
            for t in range(0, T, 2):
                b.step("tensor", lambda te, t=t: te.transpose(
                    ps8t[:], cw_sb[:, t, :], ident_f[:]))
                b.step("tensor", lambda te, t=t: te.transpose(
                    ps8u[:], cw_sb[:, t + 1, :], ident_f[:]))
                def ev2(s, t=t):
                    s.activation(cwT_sb[:, 128 * t:128 * (t + 1)], ps8t[:], AF.Copy)
                    return s.activation(cwT_sb[:, 128 * (t + 1):128 * (t + 2)], ps8u[:], AF.Copy)
                b.step("scalar", ev2)
            if "dbg_cw" in dbg_outs:
                b.dma("sync", dbg_outs["dbg_cw"].rearrange("(t p) e -> p t e", p=128), cw_sb[:])
        es_ff.close()
        es_l1.close()
        if upto < 9:
            _close_lstacks(); b.emit(block); return nc

        pes = ExitStack()
        Aw = pes.enter_context(nc.sbuf_tensor("Aw", [128, DCH, 2048], BF16, **R))
        Bw = pes.enter_context(nc.sbuf_tensor("Bw", [128, DCH, 2048], BF16, **R))
        w2b = pes.enter_context(nc.sbuf_tensor("w2b", [128, DCH, 1024], BF16, **R))
        acc = pes.enter_context(nc.sbuf_tensor("acc", [128, 8, 1024], F32, **R))
        xn1h = pes.enter_context(nc.sbuf_tensor("xn1h", [128, DCH, 1024], BF16, **R))
        a_tb = pes.enter_context(nc.sbuf_tensor("a_tb", [128, DCH, 512], BF16, **R))
        cwrep = pes.enter_context(nc.sbuf_tensor("cwrep", [128, 1024], F32, **R))
        u_sc = [[pes.enter_context(nc.sbuf_tensor(f"u{p}_{j}", [128, 512], F32, **R))
                 for j in range(4)] for p in range(2)]
        gG = [pes.enter_context(nc.sbuf_tensor(f"gG{p}", [128, 512], F32, **R)) for p in range(2)]
        gL = [pes.enter_context(nc.sbuf_tensor(f"gL{p}", [128, 512], F32, **R)) for p in range(2)]
        b2sb = pes.enter_context(nc.sbuf_tensor("b2sb", [8, 1024], F32, **R))
        sb2r = pes.enter_context(nc.sbuf_tensor("sb2r", [1, 1024], F32, **R))
        g9 = pes.enter_context(nc.sbuf_tensor("g9", [128, 1024], F32, **R))
        x9 = pes.enter_context(nc.sbuf_tensor("x9", [128, 1024], F32, **R))
        hp = [[pes.enter_context(nc.psum_tensor(f"hp{p}_{j}", [128, 512], F32))
               for j in range(4)] for p in range(2)]
        ps9y = hp[0][0]
        ps9c = hp[1][0]
        with pes:
            p9_entry = b.last
            b.dma("sync", b2sb[:], b2_in)
            b.dma("sync", sb2r[:], sb2_in.unsqueeze(0))
            ch_dwh = b.ch("dwh")
            ch_dw2 = b.ch("dw2")
            ev_h_free = b.last
            ev_w2_free = b.last
            ev_tb_done = b.last
            for th in range(2):
                for c in range(DCH):
                    b.dma("sync", xn1h[:, c, :],
                          xn1_tm_d[1024 * th:1024 * (th + 1), 128 * c:128 * (c + 1)],
                          transpose=True)
                ev_xn1h = b.last
                for e in range(9):
                    shared = (e == 8)
                    if shared:
                        srcs = (sw1t_bf, sw3t_bf, sw2t_bf)
                        evs = (cast_ev["sw1"], cast_ev["sw3"], cast_ev["sw2"])
                    else:
                        srcs = (w1t_bf[e], w3t_bf[e], w2t_bf[e])
                        evs = (cast_ev[f"w1_{e}"], cast_ev[f"w3_{e}"], cast_ev[f"w2_{e}"])
                    b.dma("gpsimd", Aw[:], srcs[0].rearrange("(c p) f -> p c f", p=128),
                          ch=ch_dwh, waits=(evs[0], ev_h_free, p9_entry),
                          seq=False, set_last=False)
                    ev_ld_h = b.dma("gpsimd", Bw[:],
                                    srcs[1].rearrange("(c p) f -> p c f", p=128),
                                    ch=ch_dwh, waits=(evs[1],), seq=False, set_last=False)
                    ev_ld_w2 = b.dma("gpsimd", w2b[:],
                                     srcs[2].rearrange("(c p) f -> p c f", p=128),
                                     ch=ch_dw2, waits=(evs[2], ev_w2_free, p9_entry),
                                     seq=False, set_last=False)
                    for tb in range(2):
                        tsl = slice(512 * tb, 512 * (tb + 1))
                        ev_start = ev_tb_done
                        if not shared:
                            ev_c = b.step("tensor", lambda te, e=e, th=th, tb=tb: te.matmul(
                                ps9c[:], sel8[:, 128 * e:128 * (e + 1)],
                                cwT_sb[:, 1024 * th + 512 * tb:1024 * th + 512 * (tb + 1)],
                                start=True, stop=True),
                                waits=(ev_start,), seq=False, set_last=False)
                            ev_cw = b.step("scalar", lambda s, tb=tb: s.activation(
                                cwrep[:, 0:512], ps9c[:], AF.Copy),
                                waits=(ev_c,), seq=False, set_last=False)
                        ev_pe = {}
                        ev_evac = {}
                        ev_v1 = {}
                        ev_v2 = {}
                        for fc in range(DCH):
                            p = fc % 2
                            pw = [ev_xn1h, (ch_dwh, ch_dwh.n), ev_start]
                            if fc - 2 in ev_evac:
                                pw.append(ev_evac[fc - 2])
                            if fc == 1 and not shared:
                                pw.append(ev_cw)
                            def mm_h4(te, fc=fc, p=p, tsl=tsl):
                                ins = None
                                for (j, woff, wbuf) in ((0, 0, Aw), (1, 0, Bw),
                                                        (2, 1024, Aw), (3, 1024, Bw)):
                                    for c in range(DCH):
                                        ins = te.matmul(
                                            hp[p][j][:],
                                            wbuf[:, c, woff + 128 * fc:woff + 128 * (fc + 1)],
                                            xn1h[:, c, tsl], start=(c == 0),
                                            stop=(c == DCH - 1))
                                return ins
                            ev_pe[fc] = b.step("tensor", mm_h4, waits=tuple(pw),
                                               seq=False, set_last=False)
                            ei = 8 if shared else e
                            ew_ = [ev_pe[fc], ev_bias]
                            if fc - 2 in ev_v2:
                                ew_.append(ev_v2[fc - 2])
                            def ev4(s, fc=fc, p=p, ei=ei):
                                s.activation(u_sc[p][0][:], hp[p][0][:], AF.Identity,
                                             bias=b1cols[:, ei, fc:fc + 1])
                                s.activation(u_sc[p][1][:], hp[p][1][:], AF.Identity,
                                             bias=b3cols[:, ei, fc:fc + 1])
                                s.activation(u_sc[p][2][:], hp[p][2][:], AF.Identity,
                                             bias=b1cols[:, ei, 8 + fc:9 + fc])
                                return s.activation(u_sc[p][3][:], hp[p][3][:], AF.Identity,
                                                    bias=b3cols[:, ei, 8 + fc:9 + fc])
                            ev_evac[fc] = b.step("scalar", ev4, waits=tuple(ew_),
                                                 seq=False, set_last=False)
                            def v1(v, p=p):
                                v.tensor_tensor(gG[p][:], u_sc[p][0][:], u_sc[p][1][:], ALU.mult)
                                return v.tensor_tensor(gL[p][:], u_sc[p][2][:], u_sc[p][3][:], ALU.mult)
                            ev_v1[fc] = b.step("vector", v1, waits=(ev_evac[fc],),
                                               seq=False, set_last=False)
                            def p1(g, p=p):
                                g.tensor_scalar(gG[p][:], gG[p][:], 7.0, None, ALU.min)
                                return g.tensor_scalar(gL[p][:], gL[p][:], 7.0, -7.0,
                                                       ALU.min, ALU.max)
                            ev_p1 = b.step("gpsimd", p1, waits=(ev_v1[fc],),
                                           seq=False, set_last=False)
                            ev_a1 = b.step("scalar", lambda s, p=p: s.activation(
                                u_sc[p][1][:], gG[p][:], AF.Gelu_apprx_sigmoid),
                                waits=(ev_p1,), seq=False, set_last=False)
                            vw = [ev_a1]
                            if not shared:
                                vw.append(ev_cw)
                            def v2(v, p=p, fc=fc, tsl=tsl, shared=shared):
                                if shared:
                                    return v.scalar_tensor_tensor(
                                        a_tb[:, fc, :], gL[p][:], 1.0, u_sc[p][1][:],
                                        ALU.add, ALU.mult)
                                v.scalar_tensor_tensor(
                                    u_sc[p][0][:], gL[p][:], 1.0, u_sc[p][1][:],
                                    ALU.add, ALU.mult)
                                return v.tensor_tensor(a_tb[:, fc, :], u_sc[p][0][:],
                                                       cwrep[:, 0:512], ALU.mult)
                            ev_v2[fc] = b.step("vector", v2, waits=tuple(vw),
                                               seq=False, set_last=False)
                        ev_h_free = ev_pe[DCH - 1]
                        ev_ac_hist = [(ev_evac[DCH - 2],), (ev_evac[DCH - 1],)]
                        ev_y_base = (ev_v2[DCH - 1], (ch_dw2, ch_dw2.n))
                        k = 0
                        for tsub in range(4):
                            tt = 8 * th + 4 * tb + tsub
                            ta = 4 * tb + tsub
                            for dh in range(2):
                                yb = hp[k % 2][0]
                                def mm_y9(te, tsub=tsub, dh=dh, e=e, tt=tt, yb=yb):
                                    ins = None
                                    if e == 0:
                                        ins = te.matmul(yb[:], cwT_sb[:, 128 * tt:128 * (tt + 1)],
                                                        b2sb[:, 512 * dh:512 * (dh + 1)],
                                                        start=True, stop=False)
                                    if e == 8:
                                        ins = te.matmul(yb[:], ones_f[0:1, 0:128],
                                                        sb2r[0:1, 512 * dh:512 * (dh + 1)],
                                                        start=True, stop=False)
                                    for fc in range(DCH):
                                        ins = te.matmul(
                                            yb[:], a_tb[:, fc, 128 * tsub:128 * (tsub + 1)],
                                            w2b[:, fc, 512 * dh:512 * (dh + 1)],
                                            start=(fc == 0 and e not in (0, 8)),
                                            stop=(fc == DCH - 1))
                                    return ins
                                ev_y = b.step("tensor", mm_y9,
                                              waits=ev_y_base + ev_ac_hist[k % 2],
                                              seq=False, set_last=False)
                                if e == 0:
                                    ev_ac = b.step("scalar", lambda s, ta=ta, dh=dh, yb=yb: s.activation(
                                        acc[:, ta, 512 * dh:512 * (dh + 1)], yb[:], AF.Copy),
                                        waits=(ev_y,), seq=False, set_last=False)
                                else:
                                    ev_ac = b.step("vector", lambda v, ta=ta, dh=dh, yb=yb: v.tensor_tensor(
                                        acc[:, ta, 512 * dh:512 * (dh + 1)], yb[:],
                                        acc[:, ta, 512 * dh:512 * (dh + 1)], ALU.add),
                                        waits=(ev_y,), seq=False, set_last=False)
                                ev_ac_hist[k % 2] = (ev_ac,)
                                ev_y_base = ((ch_dw2, ch_dw2.n),)
                                k += 1
                        ev_w2_free = ev_y
                        ev_tb_done = ev_ac
                b.last = ev_tb_done
                for ta in range(8):
                    t = 8 * th + ta
                    b.step("scalar", lambda s, ta=ta: s.activation(g9[:], acc[:, ta, :], AF.Silu))
                    b.dma("sync", x9[:], x1_d[128 * t:128 * (t + 1), :])
                    b.step("vector", lambda v, t=t: v.tensor_tensor(
                        g9[:].rearrange("p (r v) -> p r v", v=64),
                        g9[:].rearrange("p (r v) -> p r v", v=64),
                        s1_sb[:, t, :].unsqueeze(1).broadcast_to([128, 16, 64]), ALU.mult))
                    b.step("vector", lambda v: v.tensor_tensor(x9[:], x9[:], g9[:], ALU.add))
                    b.dma("sync", out_d[128 * t:128 * (t + 1), :], x9[:])
                ev_tb_done = b.last
        fin_s, fin_v = b.last[0].sem, b.last[1]
        b.ops["gpsimd"].append(lambda E, s=fin_s, v=fin_v: E.wait_ge(s, v))
        _close_lstacks()
        b.emit(block)
    return nc


def prep_inputs(inputs):
    f = lambda v: np.ascontiguousarray(np.asarray(v, dtype=np.float32))
    i = {k: np.asarray(v) for k, v in inputs.items()}
    w1 = i["L1_ffn_w1"]; w3 = i["L1_ffn_w3"]
    w1sep = np.concatenate([w1[:, 0::2, :], w1[:, 1::2, :]], axis=1)
    w3sep = np.concatenate([w3[:, 0::2, :], w3[:, 1::2, :]], axis=1)
    sw1 = i["L1_ffn_sw1"]; sw3 = i["L1_ffn_sw3"]
    sw1sep = np.concatenate([sw1[0::2, :], sw1[1::2, :]], axis=0)
    sw3sep = np.concatenate([sw3[0::2, :], sw3[1::2, :]], axis=0)

    def cols(bvec):
        return f(bvec.reshape(8, 128, 2).transpose(1, 2, 0).reshape(128, 16))

    shared = {
        "wqt": f(i["L0_attn_Wq"].T), "wkt": f(i["L0_attn_Wk"].T),
        "wvt": f(i["L0_attn_Wv"].T), "wot": f(i["L0_attn_Wo"].T),
        "phiwt": f(i["L0_ffn_phi_w"].T), "phi_b": f(i["L0_ffn_phi_b"]),
        "l1t": f(np.concatenate([i["L1_attn_Wq"].T, i["L1_attn_Wk"].T,
                                 i["L1_attn_Wv"].T], axis=1)),
        "gwt": f(i["L1_ffn_gate_w"].T), "gate_b": f(i["L1_ffn_gate_b"]),
        "w1t": f(w1sep.transpose(0, 2, 1)), "w3t": f(w3sep.transpose(0, 2, 1)),
        "w2t": f(i["L1_ffn_w2"].transpose(0, 2, 1)),
        "sw1t": f(sw1sep.T), "sw3t": f(sw3sep.T), "sw2t": f(i["L1_ffn_sw2"].T),
        "b1c": np.stack([cols(i["L1_ffn_b1"][e]) for e in range(8)]),
        "b3c": np.stack([cols(i["L1_ffn_b3"][e]) for e in range(8)]),
        "sb1c": cols(i["L1_ffn_sb1"]), "sb3c": cols(i["L1_ffn_sb3"]),
        "b2": f(i["L1_ffn_b2"]), "sb2": f(i["L1_ffn_sb2"]),
    }
    per_core = []
    for c in range(8):
        m = dict(shared)
        m["x"] = f(i["L0_attn_in_features"][c])
        m["ew_sl"] = f(i["L0_ffn_expert_w"][8 * c:8 * (c + 1)])
        m["eb_sl"] = f(i["L0_ffn_expert_b"][8 * c:8 * (c + 1)])
        per_core.append(m)
    return per_core


def kernel(**inputs):
    nc = build_nc()
    in_maps = prep_inputs(inputs)
    res = run_bass_kernel_spmd(nc, in_maps, core_ids=list(range(8)))
    return np.stack([np.asarray(r["out"], dtype=np.float32) for r in res.results])
